# version 07
# baseline (speedup 1.0000x reference)
import numpy as np
from contextlib import ExitStack

import concourse.bass as bass
import concourse.mybir as mybir
from concourse.bass_utils import run_bass_kernel_spmd

F32 = mybir.dt.float32
BF16 = mybir.dt.bfloat16
I32 = mybir.dt.int32
AX = mybir.AxisListType.X
ALU = mybir.AluOpType
ACTF = mybir.ActivationFunctionType


def _sp(v):
    hi = float(np.float32(v))
    lo = float(np.float32(v - np.float64(hi)))
    return hi, lo


PI2H, PI2L = _sp(np.pi / 2)
TPH, TPL = _sp(2 * np.pi)
I2PH, I2PL = _sp(1.0 / (2 * np.pi))

D = 1024
DBG_STAGE = None
SENTINEL = False
FORCE_WAITS = True
DC = 8
LC = 2
HC = 24
QC = 4
KT = 16

ENG_SEM = {"tensor": "T", "vector": "V", "scalar": "S", "gpsimd": "G",
           "sync": "Y"}


class Sched:
    def __init__(self):
        self.q = {e: [] for e in ENG_SEM}
        self.new_sem = None
        self.cnt = {s: 0 for s in ("T", "V", "S", "G", "Y")}
        self.eclk = {e: {} for e in ENG_SEM}
        self.tokclk = {}
        self.res = {}
        self.semh = None
        self.nwait = 0

    def r_(self, name):
        if name not in self.res:
            self.res[name] = {"w": [], "r": []}
        return self.res[name]

    def emit(self, eng, fn, reads=(), writes=(), after=(), dma=False,
             tick=True, force=False):
        waits = []
        for r in reads:
            waits += self.r_(r)["w"]
        for w in writes:
            waits += self.r_(w)["w"] + self.r_(w)["r"]
        waits += list(after)
        my = ENG_SEM[eng]
        if dma:
            assert isinstance(dma, str), "dma must be a stream name"
            sem = "d_" + dma
            if sem not in self.cnt:
                self.cnt[sem] = 0
                self.semh[sem] = self.new_sem(sem)
            inc = 16
        else:
            sem = my
            inc = 1
        clk = self.eclk[eng]
        clk[my] = self.cnt[my]
        final = []
        for s, v in sorted(set(waits)):
            if s == my and not dma:
                continue
            if clk.get(s, 0) >= v and not (force or FORCE_WAITS):
                continue
            final.append((s, v))
            tc = self.tokclk.get((s, v))
            if tc:
                for s2, v2 in tc.items():
                    if clk.get(s2, 0) < v2:
                        clk[s2] = v2
            else:
                clk[s] = max(clk.get(s, 0), v)
        if not FORCE_WAITS:
            final = [(s, v) for (s, v) in final
                     if not (s != my and self.eclk[eng].get(s, -1) > v)]
        else:
            best = {}
            for s, v in final:
                best[s] = max(best.get(s, 0), v)
            final = sorted(best.items())
        self.nwait += len(final)
        tok = None
        if tick:
            self.cnt[sem] += inc
            tok = (sem, self.cnt[sem])
            clk2 = dict(clk)
            clk2[sem] = self.cnt[sem]
            self.tokclk[tok] = clk2
        self.eclk[eng] = clk

        def run(e, _w=tuple(final), _fn=fn, _tok=tok, _sem=sem, _inc=inc):
            for s, v in _w:
                e.wait_ge(self.semh[s], v)
            ins = _fn(e)
            if _tok is not None:
                ins.then_inc(self.semh[_sem], _inc)
        run._waits = tuple(final)
        run._tok = tok
        run._writes = tuple(writes)
        run._reads = tuple(reads)
        self.q[eng].append(run)

        if tok is not None:
            for w in writes:
                self.res[w] = {"w": [tok], "r": []}
            for r in reads:
                self.r_(r)["r"].append(tok)
        return tok

    def bank_deps(self, bank):
        return tuple(self.r_(bank)["w"] + self.r_(bank)["r"])


def blocks(base, col, n, gran=512):
    return ["%s%d" % (base, i) for i in range(col // gran,
                                              (col + n + gran - 1) // gran)]


def build_nc():
    nc = bass.Bass("TRN2", target_bir_lowering=False,
                   detect_race_conditions=False)
    dt = nc.dram_tensor
    x_d = dt("x", [2, D, 2048], F32, kind="ExternalInput").ap()
    wq_d = dt("wq", [D, D], F32, kind="ExternalInput").ap()
    wdkv_d = dt("wdkv", [D, 256], F32, kind="ExternalInput").ap()
    wuk_d = dt("wuk", [256, D], F32, kind="ExternalInput").ap()
    wuv_d = dt("wuv", [256, D], F32, kind="ExternalInput").ap()
    wo_d = dt("wo", [D, D], F32, kind="ExternalInput").ap()
    gam_d = dt("gam", [2, 128], F32, kind="ExternalInput").ap()
    bet_d = dt("bet", [2, 128], F32, kind="ExternalInput").ap()
    wgt_d = dt("wgt", [D, 3072], I32, kind="ExternalInput").ap()
    wut_d = dt("wut", [D, 3072], I32, kind="ExternalInput").ap()
    wdt_d = dt("wdt", [3072, D], I32, kind="ExternalInput").ap()
    fhl_d = dt("fhl", [32, 2], F32, kind="ExternalInput").ap()
    idf_d = dt("idf", [128, 128], F32, kind="ExternalInput").ap()
    p64_d = dt("p64", [128, 128], F32, kind="ExternalInput").ap()
    cst_d = dt("cst", [128, 2118], F32, kind="ExternalInput").ap()
    mkp_d = dt("mkp", [128, 128], F32, kind="ExternalInput").ap()
    pos_d = dt("post", [128, 2048], F32, kind="ExternalInput").ap()
    out_d = dt("out", [2, D, 2048], F32, kind="ExternalOutput").ap()

    sc = Sched()
    E = sc.emit

    with ExitStack() as stk:
        en = stk.enter_context
        sems = {}
        for s in ("T", "V", "S", "G", "Y"):
            sems[s] = en(nc.semaphore("sem_" + s))
        sc.semh = sems
        sc.new_sem = lambda name: en(nc.semaphore("sem_" + name))

        ident = en(nc.sbuf_tensor("ident", [128, 128], F32))
        p64b = en(nc.sbuf_tensor("p64b", [128, 128], BF16))
        maskP = en(nc.sbuf_tensor("maskP", [128, 128], BF16))
        onesb = en(nc.sbuf_tensor("onesb", [128, 1], BF16))
        onescol = en(nc.sbuf_tensor("onescol", [65, 128], BF16))
        sgn = en(nc.sbuf_tensor("sgn", [128, 1], F32))
        sfh = en(nc.sbuf_tensor("sfh", [128, 1], F32))
        sfl = en(nc.sbuf_tensor("sfl", [128, 1], F32))
        s127 = en(nc.sbuf_tensor("s127", [128, 1], F32))
        seps = en(nc.sbuf_tensor("seps", [128, 1], F32))
        r127 = en(nc.sbuf_tensor("r127", [128, 1], F32))
        gamb = en(nc.sbuf_tensor("gamb", [128, 2], F32))
        betb = en(nc.sbuf_tensor("betb", [128, 2], F32))
        rowpool = en(nc.sbuf_tensor("rowpool", [1, 1536], BF16))
        statrow = en(nc.sbuf_tensor("statrow", [1, 1024], BF16))
        reciprow = en(nc.sbuf_tensor("reciprow", [65, 1024], BF16))
        bstg = en(nc.sbuf_tensor("bstg", [64, 2048], BF16))
        smtt = en(nc.sbuf_tensor("smtt", [16, 256], BF16))
        mbuf = en(nc.sbuf_tensor("mbuf", [128, 32], F32))
        dbgc = en(nc.sbuf_tensor("dbgc", [128, 4], F32))
        mtmp = en(nc.sbuf_tensor("mtmp", [128, 2], F32))
        mrow = en(nc.sbuf_tensor("mrow", [33, 2048], BF16))
        x2scr = en(nc.sbuf_tensor("x2scr", [128, 1024], BF16))
        q_sb = en(nc.sbuf_tensor("q_sb", [128, 1024], BF16))
        t2sb = en(nc.sbuf_tensor("t2sb", [128, 512], F32))
        t1sb = en(nc.sbuf_tensor("t1sb", [128, 1024], BF16))
        av_sb = en(nc.sbuf_tensor("av_sb", [64, 1024], BF16))
        Ctab = en(nc.sbuf_tensor("Ctab", [128, 2048], BF16))
        Stab = en(nc.sbuf_tensor("Stab", [128, 2048], BF16))
        wdkv_s = en(nc.sbuf_tensor("wdkv_s", [128, 2048], BF16))
        wuk_s = en(nc.sbuf_tensor("wuk_s", [128, 2048], BF16))
        wuv_s = en(nc.sbuf_tensor("wuv_s", [128, 2048], BF16))
        wqs = en(nc.sbuf_tensor("wqs", [128, 2048], BF16))
        wos = en(nc.sbuf_tensor("wos", [128, 2048], BF16))
        wgs = en(nc.sbuf_tensor("wgs", [128, 2048], BF16))
        wus = en(nc.sbuf_tensor("wus", [128, 2048], BF16))
        wds = en(nc.sbuf_tensor("wds", [128, 3072], BF16))
        latn = en(nc.sbuf_tensor("latn", [128, 4096], BF16))
        qt = en(nc.sbuf_tensor("qt", [128, 4096], BF16))
        ktb = en(nc.sbuf_tensor("ktb", [128, 4096], BF16))
        vbuf = en(nc.sbuf_tensor("vbuf", [128, 2112], BF16))
        pbuf = en(nc.sbuf_tensor("pbuf", [128, 2048], BF16))
        tb1 = en(nc.sbuf_tensor("tb1", [128, 512], F32))
        tbh = en(nc.sbuf_tensor("tbh", [128, 512], F32))
        tbT = en(nc.sbuf_tensor("tbT", [128, 512], F32))
        tbK = en(nc.sbuf_tensor("tbK", [128, 512], I32))
        R = en(nc.sbuf_tensor("R", [128, 32768], BF16))
        xnA = en(nc.sbuf_tensor("xnA", [128, 16384], BF16))

        pb = [en(nc.psum_tensor("pb%d" % i, [128, 512], F32))
              for i in range(8)]
        PB = ["pb%d" % i for i in range(8)]

        blk = en(nc.Block())

        TBRES = blocks("R", 0, 4096)

        def mm(bank, out_ap, lhsT, rhs, reads, start, stop, tp=None,
               first=True, last=True):
            return E("tensor",
                     lambda e: e.matmul(out_ap, lhsT, rhs, start=start,
                                        stop=stop, tile_position=tp,
                                        skip_group_check=True),
                     reads=reads, writes=([bank] if last else []),
                     after=sc.bank_deps(bank) if first else ())

        def mm_group(bank, items):
            n = len(items)
            for i, (o, l, r, rd, st, sp, tp) in enumerate(items):
                mm(bank, o, l, r, rd, st, sp, tp, first=(i == 0),
                   last=(i == n - 1))

        E("sync", lambda e: e.dma_start(out=ident[:], in_=idf_d), dma="ident",
          writes=["ident"])
        E("gpsimd", lambda e: e.dma_start(out=p64b[:], in_=p64_d), dma="p64b",
          writes=["p64b"])
        for c in range(2):
            E("sync", lambda e, c=c: e.dma_start(out=gamb[:, c:c + 1],
              in_=gam_d[c].unsqueeze(1)), dma="gamb", writes=["gamb"])
            E("sync", lambda e, c=c: e.dma_start(out=betb[:, c:c + 1],
              in_=bet_d[c].unsqueeze(1)), dma="betb", writes=["betb"])
        for b in range(4):
            E("sync", lambda e, b=b: e.dma_start(
                out=sfh[b * 32:(b + 1) * 32, 0:1], in_=fhl_d[:, 0:1],
                allow_slow_non_contiguous=True), dma="sfh", writes=["sfh"])
            E("sync", lambda e, b=b: e.dma_start(
                out=sfl[b * 32:(b + 1) * 32, 0:1], in_=fhl_d[:, 1:2],
                allow_slow_non_contiguous=True), dma="sfl", writes=["sfl"])
        E("gpsimd", lambda e: e.dma_start(
            out=wdkv_s[:].rearrange("p (c o) -> p c o", c=8),
            in_=wdkv_d.rearrange("(c p) o -> p c o", p=128)), dma="wdkv",
          writes=["wdkv_s"])
        E("gpsimd", lambda e: e.dma_start(
            out=wuk_s[:].rearrange("p (c o) -> p c o", c=2),
            in_=wuk_d.rearrange("(c p) o -> p c o", p=128)), dma="wuk",
          writes=["wuk_s"])
        E("gpsimd", lambda e: e.dma_start(
            out=wuv_s[:].rearrange("p (c o) -> p c o", c=2),
            in_=wuv_d.rearrange("(c p) o -> p c o", p=128)), dma="wuv",
          writes=["wuv_s"])

        E("gpsimd", lambda e: e.dma_start(out=onesb[:], in_=cst_d[:, 0:1], allow_slow_non_contiguous=True),
          dma="onesb", writes=["onesb"])
        E("gpsimd", lambda e: e.dma_start(out=onescol[:],
          in_=cst_d[0:65, 0:128], allow_slow_non_contiguous=True), dma="onescol", writes=["onescol"])
        E("sync", lambda e: e.dma_start(out=s127[:],
          in_=cst_d[:, 2113:2114], allow_slow_non_contiguous=True), dma="s127", writes=["s127"])
        E("sync", lambda e: e.dma_start(out=seps[:],
          in_=cst_d[:, 2114:2115], allow_slow_non_contiguous=True), dma="seps", writes=["seps"])
        E("sync", lambda e: e.dma_start(out=sgn[:],
          in_=cst_d[:, 2112:2113], allow_slow_non_contiguous=True), dma="sgn", writes=["sgn"])
        E("vector", lambda e: e.reciprocal(r127[:], s127[:]),
          reads=["s127"], writes=["r127"])
        E("gpsimd", lambda e: e.dma_start(out=vbuf[:],
          in_=cst_d[:, 0:2112], allow_slow_non_contiguous=True), dma="vbufi",
          writes=blocks("vb", 0, 2112, 132))
        E("gpsimd", lambda e: e.dma_start(out=maskP[:], in_=mkp_d),
          dma="maskP", writes=["maskP"])


        PIH = float(np.float32(np.pi))
        PIL = float(np.float32(np.pi - np.float64(PIH)))

        def tv(fn):
            E("vector", fn, reads=TBRES, writes=TBRES)

        def table_chunk(dst, dstres, c0, add_pi2, scale_ap):
            E("sync", lambda e: e.dma_start(out=tb1[:],
              in_=pos_d[:, c0:c0 + 512]), dma="post", writes=TBRES)
            E("vector", lambda e: e.tensor_scalar(tbh[:], tb1[:], sfh[:],
              None, ALU.mult), reads=TBRES + ["sfh"], writes=TBRES)
            E("vector", lambda e: e.tensor_scalar(tbT[:], tb1[:], sfl[:],
              None, ALU.mult), reads=TBRES + ["sfl"], writes=TBRES)
            tv(lambda e: e.tensor_add(tbh[:], tbh[:], tbT[:]))
            if add_pi2:
                tv(lambda e: e.tensor_single_scalar(tbh[:], tbh[:], PI2H,
                   ALU.add))
                tv(lambda e: e.tensor_single_scalar(tbh[:], tbh[:], PI2L,
                   ALU.add))
            tv(lambda e: e.tensor_single_scalar(tbT[:], tbh[:], I2PH,
               ALU.mult))
            tv(lambda e: e.tensor_single_scalar(tb1[:], tbh[:], I2PL,
               ALU.mult))
            tv(lambda e: e.tensor_add(tbT[:], tbT[:], tb1[:]))
            tv(lambda e: e.tensor_copy(tbK[:], tbT[:]))
            tv(lambda e: e.tensor_copy(tb1[:], tbK[:]))
            tv(lambda e: e.tensor_tensor(tbT[:], tb1[:], tbT[:], ALU.is_gt))
            tv(lambda e: e.tensor_sub(tb1[:], tb1[:], tbT[:]))
            tv(lambda e: e.tensor_single_scalar(tbT[:], tb1[:], TPH,
               ALU.mult))
            tv(lambda e: e.tensor_sub(tbh[:], tbh[:], tbT[:]))
            tv(lambda e: e.tensor_single_scalar(tbT[:], tb1[:], TPL,
               ALU.mult))
            tv(lambda e: e.tensor_sub(tbh[:], tbh[:], tbT[:]))
            tv(lambda e: e.tensor_single_scalar(tbh[:], tbh[:], -PIH,
               ALU.add))
            tv(lambda e: e.tensor_single_scalar(tbh[:], tbh[:], -PIL,
               ALU.add))
            E("scalar", lambda e: e.activation(dst[:, c0:c0 + 512], tbh[:],
              ACTF.Sin, scale=scale_ap), reads=TBRES + ["sgn"],
              writes=[dstres])

        for q4 in range(4):
            table_chunk(Stab, "Stab", q4 * 512, False, sgn[:])
            table_chunk(Ctab, "Ctab", q4 * 512, True, -1.0)

        def layer_norm(qci, nch, src_fn, dst_fn, nrm):
            items = []
            for dc in range(nch):
                a, rn = src_fn(dc)
                items.append((pb[6][0:1, :], onesb[:, 0:1], a,
                              rn + ["onesb"], dc == 0, dc == nch - 1, None))
            mm_group("pb6", items)
            for dc in range(nch):
                a, rn = src_fn(dc)
                xa = x2scr[:, (dc % 2) * 512:(dc % 2) * 512 + 512]
                x2n = "x2_%d" % (dc % 2)
                E("vector", lambda e, a=a, xa=xa: e.tensor_mul(xa, a, a),
                  reads=rn, writes=[x2n])
                mm("pb7", pb[7][0:1, :], onesb[:, 0:1], xa,
                   [x2n, "onesb"], dc == 0, dc == nch - 1, None,
                   first=(dc == 0), last=(dc == nch - 1))
            rpA = rowpool[0:1, 0:512]
            rpB = rowpool[0:1, 512:1024]
            rpT = rowpool[0:1, 1024:1536]
            sl = 0
            srs = statrow[0:1, 0:512]
            srm = statrow[0:1, 512:1024]
            rs_n, rm_n = "srs%d" % sl, "srm%d" % sl
            inv_n = 1.0 / nrm
            E("scalar", lambda e: e.activation(rpA, pb[6][0:1, :], ACTF.Copy,
              scale=inv_n), reads=["pb6"], writes=["rpA"])
            E("scalar", lambda e: e.activation(rpB, pb[7][0:1, :], ACTF.Copy,
              scale=inv_n), reads=["pb7"], writes=["rpB"])
            E("vector", lambda e: e.tensor_mul(rpT, rpA, rpA),
              reads=["rpA"], writes=["rpT"])
            E("vector", lambda e: e.tensor_sub(rpT, rpB, rpT),
              reads=["rpB", "rpT"], writes=["rpT"])
            E("scalar", lambda e: e.activation(rpB, rpT, ACTF.Ln,
              bias=seps[0:1, 0:1]), reads=["rpT", "seps"], writes=["rpB"])
            E("scalar", lambda e: e.activation(srs, rpB, ACTF.Exp,
              scale=-0.5), reads=["rpB"], writes=[rs_n])
            E("vector", lambda e: e.scalar_tensor_tensor(srm, rpA, -1.0, srs,
              ALU.mult, ALU.mult), reads=["rpA", rs_n], writes=[rm_n])
            mm("pb6", pb[6][:], onescol[0:1, 0:128], srs,
               [rs_n, "onescol"], True, True, (0, 0))
            mm("pb7", pb[7][:], onescol[0:1, 0:128], srm,
               [rm_n, "onescol"], True, True, (0, 0))
            for dc in range(nch):
                a, rn = src_fn(dc)
                d, wn, post = dst_fn(dc)
                E("vector", lambda e, a=a, d=d: e.scalar_tensor_tensor(
                    d, a, 1.0, pb[6][:], ALU.mult, ALU.mult),
                  reads=rn + ["pb6"], writes=wn)
                E("vector", lambda e, d=d: e.tensor_add(d, d, pb[7][:]),
                  reads=["pb7"] + wn, writes=wn)
                if post is not None:
                    oc = post
                    E("vector", lambda e, d=d, oc=oc: e.tensor_scalar(
                        d, d, gamb[:, oc:oc + 1], betb[:, oc:oc + 1],
                        ALU.mult, ALU.add),
                      reads=wn + ["gamb", "betb"], writes=wn)

        def dbg_store(s, ap3, rd=()):
            return E("gpsimd", lambda e, s=s, ap3=ap3: e.dma_start(
                out=out_d[s].rearrange("(c p) t -> p c t", p=128)[
                    :, 0:ap3.shape[1], :],
                in_=ap3), dma="dbg",
                reads=list(rd), after=sc.barrier(), force=True)

        sc.barrier = lambda: [(k, v) for k, v in sc.cnt.items() if v > 0]

        store_toks = []
        for s in range(2):
            if DBG_STAGE == "s0only" and s == 1:
                continue
            Rc = s * 16384
            Ro = (1 - s) * 16384

            E("gpsimd", lambda e, s=s, Rc=Rc: e.dma_start(
                out=R[:, Rc:Rc + 16384].rearrange("p (c t) -> p c t", c=8),
                in_=x_d[s].rearrange("(c p) t -> p c t", p=128)),
              dma="xl", writes=blocks("R", Rc, 16384))

            for qc in range(QC):
                tc = qc * 512

                def src(dc, Rc=Rc, tc=tc):
                    c = Rc + dc * 2048 + tc
                    return (R[:, c:c + 512], blocks("R", c, 512))

                def dst(dc, tc=tc):
                    c = dc * 2048 + tc
                    return (xnA[:, c:c + 512], blocks("xA", c, 512), None)
                layer_norm(qc, DC, src, dst, D)

            if DBG_STAGE == "xn":
                store_toks.append(dbg_store(
                    s, xnA[:].rearrange("p (c t) -> p c t", c=8),
                    rd=blocks("xA", 0, 16384)))
                continue

            for qc in range(QC):
                tc = qc * 512
                for oc in range(LC):
                    items = []
                    for dc in range(DC):
                        c = dc * 2048 + tc
                        items.append((pb[0][:],
                                      wdkv_s[:, dc * 256 + oc * 128:
                                             dc * 256 + oc * 128 + 128],
                                      xnA[:, c:c + 512],
                                      ["wdkv_s"] + blocks("xA", c, 512),
                                      dc == 0, dc == DC - 1, None))
                    mm_group("pb0", items)
                    lc_ = oc * 2048 + tc
                    E("scalar", lambda e, lc_=lc_: e.activation(
                        latn[:, lc_:lc_ + 512], pb[0][:], ACTF.Copy),
                      reads=["pb0"], writes=blocks("ln", lc_, 512))

                def srcl(ocx, tc=tc):
                    c = ocx * 2048 + tc
                    return (latn[:, c:c + 512], blocks("ln", c, 512))

                def dstl(ocx, tc=tc):
                    c = ocx * 2048 + tc
                    return (latn[:, c:c + 512], blocks("ln", c, 512), ocx)
                layer_norm(qc, LC, srcl, dstl, 256)

            if DBG_STAGE == "latn":
                store_toks.append(dbg_store(
                    s, latn[:].rearrange("p (c t) -> p c t", c=2),
                    rd=blocks("ln", 0, 4096)))
                continue

            for hp in range(8):
                hs = hp % 2
                qof = hs * 2048
                wqn = "wqs%d" % hs
                E("gpsimd", lambda e, hp=hp, hs=hs: e.dma_start(
                    out=wqs[:, hs * 1024:hs * 1024 + 1024].rearrange(
                        "p (c o) -> p c o", c=8),
                    in_=wq_d.rearrange("(c p) o -> p c o", p=128)[
                        :, :, hp * 128:hp * 128 + 128]),
                  dma="wq%d" % hs, writes=[wqn])

                def rope(dst_ap, dst_res, tc, qp):
                    qsl = q_sb[:, qp * 512:qp * 512 + 512]
                    t2 = t2sb[:, 0:512]
                    t1 = t1sb[:, qp * 512:qp * 512 + 512]
                    qn, t1n, t2n = "qsb%d" % qp, "t1_%d" % qp, "t2_0"
                    E("scalar", lambda e: e.activation(qsl, pb[0][:],
                      ACTF.Copy), reads=["pb0"], writes=[qn])
                    mm("pb1", pb[1][:], p64b[:], qsl, ["p64b", qn],
                       True, True, None)
                    E("vector", lambda e: e.tensor_mul(t2, pb[1][:],
                      Stab[:, tc:tc + 512]), reads=["pb1", "Stab"],
                      writes=[t2n])
                    E("vector", lambda e: e.tensor_mul(t1, qsl,
                      Ctab[:, tc:tc + 512]), reads=[qn, "Ctab"],
                      writes=[t1n])
                    E("vector", lambda e: e.tensor_add(dst_ap, t1, t2),
                      reads=[t1n, t2n], writes=dst_res)

                for qc in range(QC):
                    tc = qc * 512
                    items = []
                    for dc in range(DC):
                        c = dc * 2048 + tc
                        items.append((pb[0][:],
                                      wqs[:, hs * 1024 + dc * 128:
                                          hs * 1024 + dc * 128 + 128],
                                      xnA[:, c:c + 512],
                                      [wqn] + blocks("xA", c, 512),
                                      dc == 0, dc == DC - 1, None))
                    mm_group("pb0", items)
                    rope(qt[:, qof + tc:qof + tc + 512],
                         blocks("qt", qof + tc, 512), tc, qc % 2)
                for qc in range(QC):
                    tc = qc * 512
                    items = []
                    for lc in range(LC):
                        c = lc * 2048 + tc
                        items.append((pb[0][:],
                                      wuk_s[:, lc * 1024 + hp * 128:
                                            lc * 1024 + hp * 128 + 128],
                                      latn[:, c:c + 512],
                                      ["wuk_s"] + blocks("ln", c, 512),
                                      lc == 0, lc == LC - 1, None))
                    mm_group("pb0", items)
                    rope(ktb[:, qof + tc:qof + tc + 512],
                         blocks("kk", qof + tc, 512), tc, qc % 2)
                for kti in range(KT):
                    items = []
                    for lc in range(LC):
                        c = lc * 2048 + kti * 128
                        items.append((pb[2][:, 0:128],
                                      latn[:, c:c + 128],
                                      wuv_s[:, lc * 1024 + hp * 128:
                                            lc * 1024 + hp * 128 + 128],
                                      ["wuv_s"] + blocks("ln", c, 128),
                                      lc == 0, lc == LC - 1, None))
                    mm_group("pb2", items)
                    E("vector", lambda e, kti=kti: e.tensor_copy(
                        vbuf[:, kti * 132:kti * 132 + 132].rearrange(
                            "p (h x) -> p h x", h=2)[:, :, 0:64],
                        pb[2][:, 0:128].rearrange("p (h x) -> p h x", h=2)),
                      reads=["pb2"], writes=["vb%d" % kti])

                if DBG_STAGE == "qkv" and hp == 0:
                    store_toks.append(dbg_store(
                        s, ktb[:, 0:2048].rearrange("p (c t) -> p c t", c=1),
                        rd=blocks("kk", 0, 2048)))
                    store_toks.append(E("gpsimd", lambda e, s=s: e.dma_start(
                        out=out_d[s].rearrange("(c p) t -> p c t", p=128)[
                            :, 1:2, :],
                        in_=qt[:, 0:2048].rearrange(
                            "p (c t) -> p c t", c=1)), dma="dbg",
                        force=True,
                        reads=blocks("qt", 0, 2048), after=sc.barrier()))
                    store_toks.append(E("gpsimd", lambda e, s=s: e.dma_start(
                        out=out_d[s].rearrange("(c p) t -> p c t", p=128)[
                            :, 2:3, 0:2048],
                        in_=vbuf[:, 0:2048].rearrange(
                            "p (c t) -> p c t", c=1)), dma="dbg",
                        force=True,
                        reads=blocks("vb", 0, 2048, 132),
                        after=sc.barrier()))
                    break

                for p1rep in range(2):
                  for head in range(2):
                      r0 = head * 64
                      for qtile in range(KT):
                          qcol = qof + qtile * 128
                          qres = blocks("qt", qcol, 128)
                          nkc = (qtile * 128 + 128 + 511) // 512
                          for kc in range(nkc):
                              N = min(512, qtile * 128 + 128 - kc * 512)
                              b = head + 2 * (kc % 2)
                              mm(PB[b], pb[b][:, 0:N],
                                 qt[r0:r0 + 64, qcol:qcol + 128],
                                 ktb[r0:r0 + 64, qof + kc * 512:
                                     qof + kc * 512 + N],
                                 qres + blocks("kk", qof + kc * 512, N),
                                 True, True, (r0, 0))
                          for kc in range(nkc):
                              N = min(512, qtile * 128 + 128 - kc * 512)
                              b = head + 2 * (kc % 2)
                              if kc == 0:
                                  E("vector", lambda e, b=b, N=N, head=head,
                                    qtile=qtile: e.reduce_max(
                                      mbuf[:, head * 16 + qtile:
                                           head * 16 + qtile + 1],
                                      pb[b][:, 0:N], axis=AX),
                                    reads=[PB[b]], writes=["mbuf"])
                              else:
                                  E("vector", lambda e, b=b, N=N, head=head:
                                    e.reduce_max(mtmp[:, head:head + 1],
                                                 pb[b][:, 0:N], axis=AX),
                                    reads=[PB[b]], writes=["mtmp"])
                                  E("vector", lambda e, head=head, qtile=qtile:
                                    e.tensor_max(
                                      mbuf[:, head * 16 + qtile:
                                           head * 16 + qtile + 1],
                                      mbuf[:, head * 16 + qtile:
                                           head * 16 + qtile + 1],
                                      mtmp[:, head:head + 1]),
                                    reads=["mtmp", "mbuf"], writes=["mbuf"])
                                  if (DBG_STAGE == "mdump" and hp == 0
                                          and head == 0 and qtile == 4
                                          and s == 0):
                                      E("vector", lambda e: e.reduce_max(
                                          dbgc[:, 0:1], pb[0][:, 0:512],
                                          axis=AX), reads=["pb0"],
                                        writes=["dbgc"])
                                      E("vector", lambda e: e.reduce_max(
                                          dbgc[:, 1:2], pb[2][:, 0:128],
                                          axis=AX), reads=["pb2"],
                                        writes=["dbgc"])
                      E("vector", lambda e, head=head: e.tensor_scalar_mul(
                          mbuf[:, head * 16:head * 16 + 16],
                          mbuf[:, head * 16:head * 16 + 16], -1.0),
                        reads=["mbuf"], writes=["mbuf"])
                      E("vector", lambda e, head=head: e.tensor_scalar_max(
                          mbuf[:, head * 16:head * 16 + 16],
                          mbuf[:, head * 16:head * 16 + 16], -64.0),
                        reads=["mbuf"], writes=["mbuf"])
                      E("vector", lambda e, head=head: e.tensor_scalar_min(
                          mbuf[:, head * 16:head * 16 + 16],
                          mbuf[:, head * 16:head * 16 + 16], 64.0),
                        reads=["mbuf"], writes=["mbuf"])
                      E("tensor", lambda e, head=head: e.transpose(
                          pb[2][0:16, 0:128], mbuf[:, head * 16:head * 16 + 16],
                          ident[:]), reads=["mbuf", "ident"], writes=["pb2"],
                        after=sc.bank_deps("pb2"))
                      E("vector", lambda e, head=head: e.tensor_copy(
                          smtt[:, head * 128:head * 128 + 128],
                          pb[2][0:16, 0:128]), reads=["pb2"],
                        writes=["smtt%d" % head])
                      mr = head * 32
                      E("sync", lambda e, head=head, mr=mr, hs=hs: e.dma_start(
                          out=mrow[mr:mr + 1, 0:2048].rearrange(
                              "p (t q) -> p t q", t=16),
                          in_=smtt[:, head * 128:head * 128 + 128]),
                        dma="mr%d%d" % (head, hs),
                        reads=["smtt%d" % head],
                        writes=["mrow%d_%d" % (head, hs)])

                  if DBG_STAGE == "mdump" and hp == 0:
                      store_toks.append(E("gpsimd", lambda e, s=s: e.dma_start(
                          out=out_d[s].rearrange("(c p) t -> p c t", p=128)[
                              :, 0:1, 32:34],
                          in_=mtmp[:].rearrange("p (c t) -> p c t", c=1)),
                          dma="dbg", force=True, reads=["mtmp"],
                          after=sc.barrier()))
                      store_toks.append(E("gpsimd", lambda e, s=s: e.dma_start(
                          out=out_d[s].rearrange("(c p) t -> p c t", p=128)[
                              :, 0:1, 40:44],
                          in_=dbgc[:].rearrange("p (c t) -> p c t", c=1)),
                          dma="dbg", force=True, reads=["dbgc"],
                          after=sc.barrier()))
                      store_toks.append(E("gpsimd", lambda e, s=s: e.dma_start(
                          out=out_d[s].rearrange("(c p) t -> p c t", p=128)[
                              :, 2:3, 0:2048],
                          in_=qt[:, 0:2048].rearrange(
                              "p (c t) -> p c t", c=1)),
                          dma="dbg", force=True, reads=blocks("qt", 0, 2048),
                          after=sc.barrier()))
                      store_toks.append(E("gpsimd", lambda e, s=s: e.dma_start(
                          out=out_d[s].rearrange("(c p) t -> p c t", p=128)[
                              :, 3:4, 0:2048],
                          in_=ktb[:, 0:2048].rearrange(
                              "p (c t) -> p c t", c=1)),
                          dma="dbg", force=True, reads=blocks("kk", 0, 2048),
                          after=sc.barrier()))
                      store_toks.append(E("gpsimd", lambda e, s=s: e.dma_start(
                          out=out_d[s].rearrange("(c p) t -> p c t", p=128)[
                              :, 0:1, 0:32],
                          in_=mbuf[:].rearrange("p (c t) -> p c t", c=1)),
                          dma="dbg", force=True, reads=["mbuf"],
                          after=sc.barrier()))
                      store_toks.append(E("gpsimd", lambda e, s=s: e.dma_start(
                          out=out_d[s].rearrange("(c p) t -> p c t", p=128)[
                              0:33, 1:2, 0:2048],
                          in_=mrow[:, 0:2048].rearrange(
                              "p (c t) -> p c t", c=1)),
                          dma="dbg", force=True,
                          reads=["mrow0_0", "mrow1_0"], after=sc.barrier()))
                      break

                for qc in range(QC):
                    tc = qc * 512
                    lastkt = min(KT - 1, 4 * qc + 3)

                    def st_pair(kti, tc=tc, qof=qof, hs=hs):
                        par = kti % 2
                        off = max(0, kti * 128 - tc)
                        for head in range(2):
                            b = par * 2 + head
                            r0 = head * 64
                            kcol = qof + kti * 128
                            mm(PB[b], pb[b][:, off:512],
                               ktb[r0:r0 + 64, kcol:kcol + 128],
                               qt[r0:r0 + 64, qof + tc + off:qof + tc + 512],
                               blocks("kk", kcol, 128) +
                               blocks("qt", qof + tc + off, 512 - off),
                               True, False, (r0, 0), first=True, last=False)
                            mr = head * 32
                            mm(PB[b], pb[b][:, off:512],
                               onescol[mr:mr + 1, 0:128],
                               mrow[mr:mr + 1, tc + off:tc + 512],
                               ["onescol", "mrow%d_%d" % (head, hs)],
                               False, True, (mr, 0), first=False, last=True)

                    def exp_mask(kti, tc=tc):
                        par = kti % 2
                        off = max(0, kti * 128 - tc)
                        for head in range(2):
                            b = par * 2 + head
                            sl = head * 2 + par
                            ps = pbuf[:, sl * 512:sl * 512 + 512]
                            pres = "pbuf%d" % sl
                            E("scalar", lambda e, b=b, off=off, ps=ps:
                              e.activation(ps[:, off:512], pb[b][:, off:512],
                                           ACTF.Exp, scale=0.125),
                              reads=[PB[b]], writes=[pres])
                            if kti * 128 >= tc:
                                E("vector", lambda e, ps=ps, off=off:
                                  e.tensor_mul(ps[:, off:off + 128],
                                               ps[:, off:off + 128],
                                               maskP[:]),
                                  reads=[pres, "maskP"], writes=[pres])

                    def av(kti, tc=tc, lastkt=lastkt):
                        par = kti % 2
                        off = max(0, kti * 128 - tc)
                        for head in range(2):
                            bav = 4 + head
                            sl = head * 2 + par
                            ps = pbuf[:, sl * 512:sl * 512 + 512]
                            pres = "pbuf%d" % sl
                            mm(PB[bav], pb[bav][0:65, off:512],
                               vbuf[:, kti * 132 + head * 66:
                                    kti * 132 + head * 66 + 65],
                               ps[:, off:512], ["vb%d" % kti, pres],
                               kti == 0, kti == lastkt, None,
                               first=(kti == 0), last=(kti == lastkt))

                    st_pair(0)
                    for kti in range(lastkt + 1):
                        if kti + 1 <= lastkt:
                            st_pair(kti + 1)
                        exp_mask(kti)
                        av(kti)

                    for head in range(2):
                        bav = 4 + head
                        bb = head
                        rr = reciprow[64:65, head * 512:head * 512 + 512]
                        rres = "recip%d" % head
                        E("scalar", lambda e, bav=bav: e.activation(
                            pb[bav][64:65, :], pb[bav][64:65, :], ACTF.Ln),
                          reads=[PB[bav]], writes=[PB[bav]])
                        E("scalar", lambda e, bav=bav, rr=rr: e.activation(
                            rr, pb[bav][64:65, :], ACTF.Exp, scale=-1.0),
                          reads=[PB[bav]], writes=[rres])
                        mm(PB[bb], pb[bb][0:64, :], onescol[64:65, 0:64], rr,
                           [rres, "onescol"], True, True, (64, 0))
                        avs = av_sb[:, head * 512:head * 512 + 512]
                        avn = "avsb%d" % head
                        E("scalar", lambda e, bav=bav, avs=avs: e.activation(
                            avs, pb[bav][0:64, :], ACTF.Copy),
                          reads=[PB[bav]], writes=[avn])
                        ac = Ro + hp * 2048 + tc
                        if head == 0:
                            E("vector", lambda e, avs=avs, bb=bb, ac=ac:
                              e.scalar_tensor_tensor(
                                R[0:64, ac:ac + 512], avs, 1.0,
                                pb[bb][0:64, :], ALU.mult, ALU.mult),
                              reads=[avn, PB[bb]], writes=blocks("R", ac, 512))
                        else:
                            E("vector", lambda e, avs=avs, bb=bb, tc=tc:
                              e.scalar_tensor_tensor(
                                bstg[:, tc:tc + 512], avs, 1.0,
                                pb[bb][0:64, :], ALU.mult, ALU.mult),
                              reads=[avn, PB[bb]], writes=["bstg"])
                if True:
                    ac0 = Ro + hp * 2048
                    E("sync", lambda e, ac0=ac0: e.dma_start(
                        out=R[64:128, ac0:ac0 + 2048], in_=bstg[:]),
                      dma="bs", reads=["bstg"],
                      writes=blocks("R", ac0, 2048))

            if DBG_STAGE == "qkv":
                continue
            if DBG_STAGE == "att":
                store_toks.append(dbg_store(
                    s, R[:, Ro:Ro + 16384].rearrange("p (c t) -> p c t",
                                                     c=8),
                    rd=blocks("R", Ro, 16384)))
                continue

            for oc in range(8):
                osl = oc % 2
                won = "wos%d" % osl
                E("gpsimd", lambda e, oc=oc, osl=osl: e.dma_start(
                    out=wos[:, osl * 1024:osl * 1024 + 1024].rearrange(
                        "p (c o) -> p c o", c=8),
                    in_=wo_d.rearrange("(c p) o -> p c o", p=128)[
                        :, :, oc * 128:oc * 128 + 128]),
                  dma="wo%d" % osl, writes=[won])
                for qc in range(QC):
                    tc = qc * 512
                    b = qc % 2
                    items = []
                    for dc in range(DC):
                        ac = Ro + dc * 2048 + tc
                        items.append((pb[b][:],
                                      wos[:, osl * 1024 + dc * 128:
                                          osl * 1024 + dc * 128 + 128],
                                      R[:, ac:ac + 512],
                                      [won] + blocks("R", ac, 512),
                                      dc == 0, dc == DC - 1, None))
                    mm_group(PB[b], items)
                    rc = Rc + oc * 2048 + tc
                    E("vector", lambda e, b=b, rc=rc: e.tensor_add(
                        R[:, rc:rc + 512], R[:, rc:rc + 512], pb[b][:]),
                      reads=[PB[b]] + blocks("R", rc, 512),
                      writes=blocks("R", rc, 512))

            if DBG_STAGE == "wo":
                store_toks.append(dbg_store(
                    s, R[:, Rc:Rc + 16384].rearrange("p (c t) -> p c t",
                                                     c=8),
                    rd=blocks("R", Rc, 16384)))
                continue

            def hloc(hc, t0):
                if hc < 16:
                    c = Ro + hc * 1024 + t0
                    return (R[:, c:c + 512], blocks("R", c, 512))
                c = 8192 + (hc - 16) * 1024 + t0
                return (xnA[:, c:c + 512], blocks("xA", c, 512))

            for st_i in range(2):
                stc = st_i * 1024
                for qc2 in range(2):
                    loc = qc2 * 512

                    def src2(dc, Rc=Rc, stc=stc, loc=loc):
                        c = Rc + dc * 2048 + stc + loc
                        return (R[:, c:c + 512], blocks("R", c, 512))

                    def dst2(dc, loc=loc):
                        c = dc * 1024 + loc
                        return (xnA[:, c:c + 512], blocks("xA", c, 512),
                                None)
                    layer_norm(qc2, DC, src2, dst2, D)

                if DBG_STAGE == "ffnln" and st_i == 0:
                    store_toks.append(dbg_store(
                        s, xnA[:, 0:8192].rearrange("p (c t) -> p c t", c=4),
                        rd=blocks("xA", 0, 8192)))
                    break
                for hc in range(HC):
                    hsl = hc % 2
                    wgn, wun = "wgs%d" % hsl, "wus%d" % hsl
                    E("gpsimd", lambda e, hc=hc, hsl=hsl: e.dma_start(
                        out=wgs[:, hsl * 1024:hsl * 1024 + 1024].rearrange(
                            "p (c o) -> p c o", c=8),
                        in_=wgt_d.rearrange("(c p) h -> p c h", p=128)[
                            :, :, hc * 128:hc * 128 + 128]),
                      dma="wg%d" % hsl, writes=[wgn])
                    E("gpsimd", lambda e, hc=hc, hsl=hsl: e.dma_start(
                        out=wus[:, hsl * 1024:hsl * 1024 + 1024].rearrange(
                            "p (c o) -> p c o", c=8),
                        in_=wut_d.rearrange("(c p) h -> p c h", p=128)[
                            :, :, hc * 128:hc * 128 + 128]),
                      dma="wu%d" % hsl, writes=[wun])
                    for tt in range(2):
                        t0 = tt * 512
                        bg, bu = tt * 2, tt * 2 + 1
                        for wname, bx in ((wgn, bg), (wun, bu)):
                            wt = wgs if wname.startswith("wg") else wus
                            items = []
                            for dc in range(DC):
                                c = dc * 1024 + t0
                                items.append((pb[bx][:],
                                              wt[:, hsl * 1024 + dc * 128:
                                                 hsl * 1024 + dc * 128 + 128],
                                              xnA[:, c:c + 512],
                                              [wname] + blocks("xA", c, 512),
                                              dc == 0, dc == DC - 1, None))
                            mm_group(PB[bx], items)
                        gs = q_sb[:, tt * 512:tt * 512 + 512]
                        gn = "qsb%d" % tt
                        E("scalar", lambda e, bg=bg, gs=gs: e.activation(
                            gs, pb[bg][:], ACTF.Silu, scale=r127[:]),
                          reads=[PB[bg], "r127"], writes=[gn])
                        ha, hres = hloc(hc, t0)
                        E("vector", lambda e, bu=bu, gs=gs, ha=ha:
                          e.scalar_tensor_tensor(ha, pb[bu][:], r127[:], gs,
                                                 ALU.mult, ALU.mult),
                          reads=[PB[bu], "r127", gn], writes=hres)
                if DBG_STAGE == "ffnh" and st_i == 0:
                    store_toks.append(dbg_store(
                        s, R[:, Ro:Ro + 16384].rearrange(
                            "p (c t) -> p c t", c=8),
                        rd=blocks("R", Ro, 16384)))
                    break
                if DBG_STAGE == "ffnh2" and st_i == 0:
                    store_toks.append(dbg_store(
                        s, xnA[:].rearrange("p (c t) -> p c t", c=8),
                        rd=blocks("xA", 0, 16384)))
                    break
                for dc in range(DC):
                    dsl = 0
                    wdn = "wds0"
                    E("gpsimd", lambda e, dc=dc, dsl=dsl: e.dma_start(
                        out=wds[:, 0:3072].rearrange(
                            "p (c o) -> p c o", c=24),
                        in_=wdt_d.rearrange("(c p) o -> p c o", p=128)[
                            :, :, dc * 128:dc * 128 + 128]),
                      dma="wd%d" % dsl, writes=[wdn])
                    for tt in range(2):
                        t0 = tt * 512
                        b = 4 + tt
                        items = []
                        for hc in range(HC):
                            ha, hres = hloc(hc, t0)
                            items.append((pb[b][:],
                                          wds[:, dsl * 3072 + hc * 128:
                                              dsl * 3072 + hc * 128 + 128],
                                          ha, [wdn] + hres,
                                          hc == 0, hc == HC - 1, None))
                        mm_group(PB[b], items)
                        rc = Rc + dc * 2048 + stc + t0
                        E("vector", lambda e, b=b, rc=rc:
                          e.scalar_tensor_tensor(
                            R[:, rc:rc + 512], pb[b][:], r127[:],
                            R[:, rc:rc + 512], ALU.mult, ALU.add),
                          reads=[PB[b], "r127"] + blocks("R", rc, 512),
                          writes=blocks("R", rc, 512))

            if DBG_STAGE not in ("ffnln", "ffnh", "ffnh2", "mdump"):
                tok = E("gpsimd", lambda e, s=s, Rc=Rc: e.dma_start(
                    out=out_d[s].rearrange("(c p) t -> p c t", p=128),
                    in_=R[:, Rc:Rc + 16384].rearrange("p (c t) -> p c t",
                                                      c=8)),
                  dma="st", reads=blocks("R", Rc, 16384))
                store_toks.append(tok)

        E("sync", lambda e: e.nop(), after=store_toks, tick=False)

        blk.tensor(lambda e: [f(e) for f in sc.q["tensor"]])
        blk.vector(lambda e: [f(e) for f in sc.q["vector"]])
        blk.scalar(lambda e: [f(e) for f in sc.q["scalar"]])
        blk.gpsimd(lambda e: [f(e) for f in sc.q["gpsimd"]])
        blk.sync(lambda e: [f(e) for f in sc.q["sync"]])

    nc._sched = sc
    import sys
    print("kernel build: instrs",
          {k: len(v) for k, v in sc.q.items()}, "waits", sc.nwait,
          file=sys.stderr)
    return nc


_NC = None


def _get_nc():
    global _NC
    if _NC is None:
        _NC = build_nc()
    return _NC


def kernel(L0_attn_x, L0_attn_W_q, L0_attn_W_dkv, L0_attn_W_uk, L0_attn_W_uv,
           L0_attn_W_o, L0_attn_ln_gamma, L0_attn_ln_beta, L0_ffn_w_gate,
           L0_ffn_w_up, L0_ffn_w_down):
    x = np.asarray(L0_attn_x, dtype=np.float32)
    xT = np.ascontiguousarray(np.transpose(x, (0, 2, 1)))
    wq = np.ascontiguousarray(np.asarray(L0_attn_W_q, np.float32))
    wdkv = np.ascontiguousarray(np.asarray(L0_attn_W_dkv, np.float32))
    wuk = np.ascontiguousarray(np.asarray(L0_attn_W_uk, np.float32))
    wuv = np.ascontiguousarray(np.asarray(L0_attn_W_uv, np.float32))
    wo = np.ascontiguousarray(np.asarray(L0_attn_W_o, np.float32))
    gam = np.ascontiguousarray(
        np.asarray(L0_attn_ln_gamma, np.float32).reshape(2, 128))
    bet = np.ascontiguousarray(
        np.asarray(L0_attn_ln_beta, np.float32).reshape(2, 128))
    wgt = np.ascontiguousarray(np.asarray(L0_ffn_w_gate, np.int32).T)
    wut = np.ascontiguousarray(np.asarray(L0_ffn_w_up, np.int32).T)
    wdt = np.ascontiguousarray(np.asarray(L0_ffn_w_down, np.int32).T)
    inv = 1.0 / (10000.0 ** (np.arange(32, dtype=np.float64) / 32.0))
    fh = inv.astype(np.float32)
    fl = (inv - fh.astype(np.float64)).astype(np.float32)
    fhl = np.ascontiguousarray(np.stack([fh, fl], axis=1))
    idf = np.eye(128, dtype=np.float32)
    p64 = np.zeros((128, 128), dtype=np.float32)
    p64[np.arange(128), np.arange(128) ^ 32] = 1.0
    cst = np.ones((128, 2118), dtype=np.float32)
    sgnv = np.ones(128, np.float32)
    sgnv[0:32] = 1.0; sgnv[32:64] = -1.0; sgnv[64:96] = 1.0; sgnv[96:128] = -1.0
    cst[:, 2112] = sgnv
    cst[:, 2113] = 127.0
    cst[:, 2114] = 1e-5
    mkp = (np.arange(128)[None, :] >= np.arange(128)[:, None]).astype(np.float32)
    post = np.broadcast_to(np.arange(2048, dtype=np.float32)[None, :],
                           (128, 2048)).copy()

    nc = _get_nc()
    in_maps = []
    for c in range(8):
        in_maps.append(dict(
            x=np.ascontiguousarray(xT[2 * c:2 * c + 2]),
            wq=wq, wdkv=wdkv, wuk=wuk, wuv=wuv, wo=wo, gam=gam, bet=bet,
            wgt=wgt, wut=wut, wdt=wdt, fhl=fhl, idf=idf, p64=p64,
            cst=cst, mkp=mkp, post=post))
    res = run_bass_kernel_spmd(nc, in_maps, core_ids=list(range(8)))
    outs = []
    for c in range(8):
        o = np.asarray(res.results[c]["out"])
        outs.append(np.transpose(o, (0, 2, 1)))
    return np.ascontiguousarray(np.concatenate(outs, axis=0))


if __name__ == "__main__":
    build_nc()
    print("build ok")


# version 08
# speedup vs baseline: 1.1903x; 1.1903x over previous
import numpy as np
from contextlib import ExitStack

import concourse.bass as bass
import concourse.mybir as mybir
from concourse.bass_utils import run_bass_kernel_spmd

F32 = mybir.dt.float32
BF16 = mybir.dt.bfloat16
I32 = mybir.dt.int32
AX = mybir.AxisListType.X
ALU = mybir.AluOpType
ACTF = mybir.ActivationFunctionType


def _sp(v):
    hi = float(np.float32(v))
    lo = float(np.float32(v - np.float64(hi)))
    return hi, lo


PI2H, PI2L = _sp(np.pi / 2)
TPH, TPL = _sp(2 * np.pi)
I2PH, I2PL = _sp(1.0 / (2 * np.pi))

D = 1024
DBG_STAGE = None
SENTINEL = False
FORCE_WAITS = True
DC = 8
LC = 2
HC = 24
QC = 4
KT = 16

ENG_SEM = {"tensor": "T", "vector": "V", "scalar": "S", "gpsimd": "G",
           "sync": "Y"}


class Sched:
    def __init__(self):
        self.q = {e: [] for e in ENG_SEM}
        self.new_sem = None
        self.cnt = {s: 0 for s in ("T", "V", "S", "G", "Y")}
        self.eclk = {e: {} for e in ENG_SEM}
        self.tokclk = {}
        self.res = {}
        self.semh = None
        self.nwait = 0

    def r_(self, name):
        if name not in self.res:
            self.res[name] = {"w": [], "r": []}
        return self.res[name]

    def emit(self, eng, fn, reads=(), writes=(), after=(), dma=False,
             tick=True, force=False):
        waits = []
        for r in reads:
            waits += self.r_(r)["w"]
        for w in writes:
            waits += self.r_(w)["w"] + self.r_(w)["r"]
        waits += list(after)
        my = ENG_SEM[eng]
        if dma:
            assert isinstance(dma, str), "dma must be a stream name"
            sem = "d_" + dma
            if sem not in self.cnt:
                self.cnt[sem] = 0
                self.semh[sem] = self.new_sem(sem)
            inc = 16
        else:
            sem = my
            inc = 1
        clk = self.eclk[eng]
        clk[my] = self.cnt[my]
        final = []
        for s, v in sorted(set(waits)):
            if s == my and not dma:
                continue
            if clk.get(s, 0) >= v and not (force or FORCE_WAITS):
                continue
            final.append((s, v))
            tc = self.tokclk.get((s, v))
            if tc:
                for s2, v2 in tc.items():
                    if clk.get(s2, 0) < v2:
                        clk[s2] = v2
            else:
                clk[s] = max(clk.get(s, 0), v)
        if not FORCE_WAITS:
            final = [(s, v) for (s, v) in final
                     if not (s != my and self.eclk[eng].get(s, -1) > v)]
        else:
            best = {}
            for s, v in final:
                best[s] = max(best.get(s, 0), v)
            final = sorted(best.items())
        self.nwait += len(final)
        tok = None
        if tick:
            self.cnt[sem] += inc
            tok = (sem, self.cnt[sem])
            clk2 = dict(clk)
            clk2[sem] = self.cnt[sem]
            self.tokclk[tok] = clk2
        self.eclk[eng] = clk

        def run(e, _w=tuple(final), _fn=fn, _tok=tok, _sem=sem, _inc=inc):
            for s, v in _w:
                e.wait_ge(self.semh[s], v)
            ins = _fn(e)
            if _tok is not None:
                ins.then_inc(self.semh[_sem], _inc)
        run._waits = tuple(final)
        run._tok = tok
        run._writes = tuple(writes)
        run._reads = tuple(reads)
        self.q[eng].append(run)

        if tok is not None:
            for w in writes:
                self.res[w] = {"w": [tok], "r": []}
            for r in reads:
                self.r_(r)["r"].append(tok)
        return tok

    def bank_deps(self, bank):
        return tuple(self.r_(bank)["w"] + self.r_(bank)["r"])


def blocks(base, col, n, gran=512):
    return ["%s%d" % (base, i) for i in range(col // gran,
                                              (col + n + gran - 1) // gran)]


def build_nc():
    nc = bass.Bass("TRN2", target_bir_lowering=False,
                   detect_race_conditions=False)
    dt = nc.dram_tensor
    x_d = dt("x", [2, D, 2048], F32, kind="ExternalInput").ap()
    wq_d = dt("wq", [D, D], F32, kind="ExternalInput").ap()
    wdkv_d = dt("wdkv", [D, 256], F32, kind="ExternalInput").ap()
    wuk_d = dt("wuk", [256, D], F32, kind="ExternalInput").ap()
    wuv_d = dt("wuv", [256, D], F32, kind="ExternalInput").ap()
    wo_d = dt("wo", [D, D], F32, kind="ExternalInput").ap()
    gam_d = dt("gam", [2, 128], F32, kind="ExternalInput").ap()
    bet_d = dt("bet", [2, 128], F32, kind="ExternalInput").ap()
    wgt_d = dt("wgt", [D, 3072], I32, kind="ExternalInput").ap()
    wut_d = dt("wut", [D, 3072], I32, kind="ExternalInput").ap()
    wdt_d = dt("wdt", [3072, D], I32, kind="ExternalInput").ap()
    fhl_d = dt("fhl", [32, 2], F32, kind="ExternalInput").ap()
    idf_d = dt("idf", [128, 128], F32, kind="ExternalInput").ap()
    p64_d = dt("p64", [128, 128], F32, kind="ExternalInput").ap()
    cst_d = dt("cst", [128, 2118], F32, kind="ExternalInput").ap()
    mkp_d = dt("mkp", [128, 128], F32, kind="ExternalInput").ap()
    pos_d = dt("post", [128, 2048], F32, kind="ExternalInput").ap()
    out_d = dt("out", [2, D, 2048], F32, kind="ExternalOutput").ap()

    sc = Sched()
    E = sc.emit

    with ExitStack() as stk:
        en = stk.enter_context
        sems = {}
        for s in ("T", "V", "S", "G", "Y"):
            sems[s] = en(nc.semaphore("sem_" + s))
        sc.semh = sems
        sc.new_sem = lambda name: en(nc.semaphore("sem_" + name))

        ident = en(nc.sbuf_tensor("ident", [128, 128], F32))
        p64b = en(nc.sbuf_tensor("p64b", [128, 128], BF16))
        maskP = en(nc.sbuf_tensor("maskP", [128, 128], BF16))
        onesb = en(nc.sbuf_tensor("onesb", [128, 1], BF16))
        onescol = en(nc.sbuf_tensor("onescol", [65, 128], BF16))
        sgn = en(nc.sbuf_tensor("sgn", [128, 1], F32))
        sfh = en(nc.sbuf_tensor("sfh", [128, 1], F32))
        sfl = en(nc.sbuf_tensor("sfl", [128, 1], F32))
        s127 = en(nc.sbuf_tensor("s127", [128, 1], F32))
        seps = en(nc.sbuf_tensor("seps", [128, 1], F32))
        r127 = en(nc.sbuf_tensor("r127", [128, 1], F32))
        gamb = en(nc.sbuf_tensor("gamb", [128, 2], F32))
        betb = en(nc.sbuf_tensor("betb", [128, 2], F32))
        rowpool = en(nc.sbuf_tensor("rowpool", [1, 1536], BF16))
        statrow = en(nc.sbuf_tensor("statrow", [1, 1024], BF16))
        reciprow = en(nc.sbuf_tensor("reciprow", [65, 1024], BF16))
        bstg = en(nc.sbuf_tensor("bstg", [64, 2048], BF16))
        smtt = en(nc.sbuf_tensor("smtt", [16, 256], BF16))
        mbuf = en(nc.sbuf_tensor("mbuf", [128, 32], F32))
        dbgc = en(nc.sbuf_tensor("dbgc", [128, 4], F32))
        mtmp = en(nc.sbuf_tensor("mtmp", [128, 2], F32))
        mrow = en(nc.sbuf_tensor("mrow", [33, 2048], BF16))
        x2scr = en(nc.sbuf_tensor("x2scr", [128, 1024], BF16))
        q_sb = en(nc.sbuf_tensor("q_sb", [128, 1024], BF16))
        t2sb = en(nc.sbuf_tensor("t2sb", [128, 512], F32))
        t1sb = en(nc.sbuf_tensor("t1sb", [128, 1024], BF16))
        av_sb = en(nc.sbuf_tensor("av_sb", [64, 1024], BF16))
        Ctab = en(nc.sbuf_tensor("Ctab", [128, 2048], BF16))
        Stab = en(nc.sbuf_tensor("Stab", [128, 2048], BF16))
        wdkv_s = en(nc.sbuf_tensor("wdkv_s", [128, 2048], BF16))
        wuk_s = en(nc.sbuf_tensor("wuk_s", [128, 2048], BF16))
        wuv_s = en(nc.sbuf_tensor("wuv_s", [128, 2048], BF16))
        wqs = en(nc.sbuf_tensor("wqs", [128, 2048], BF16))
        wos = en(nc.sbuf_tensor("wos", [128, 2048], BF16))
        wgs = en(nc.sbuf_tensor("wgs", [128, 2048], BF16))
        wus = en(nc.sbuf_tensor("wus", [128, 2048], BF16))
        wds = en(nc.sbuf_tensor("wds", [128, 3072], BF16))
        latn = en(nc.sbuf_tensor("latn", [128, 4096], BF16))
        qt = en(nc.sbuf_tensor("qt", [128, 4096], BF16))
        ktb = en(nc.sbuf_tensor("ktb", [128, 4096], BF16))
        vbuf = en(nc.sbuf_tensor("vbuf", [128, 2112], BF16))
        pbuf = en(nc.sbuf_tensor("pbuf", [128, 2048], BF16))
        tb1 = en(nc.sbuf_tensor("tb1", [128, 512], F32))
        tbh = en(nc.sbuf_tensor("tbh", [128, 512], F32))
        tbT = en(nc.sbuf_tensor("tbT", [128, 512], F32))
        tbK = en(nc.sbuf_tensor("tbK", [128, 512], I32))
        R = en(nc.sbuf_tensor("R", [128, 32768], BF16))
        xnA = en(nc.sbuf_tensor("xnA", [128, 16384], BF16))

        pb = [en(nc.psum_tensor("pb%d" % i, [128, 512], F32))
              for i in range(8)]
        PB = ["pb%d" % i for i in range(8)]

        blk = en(nc.Block())

        TBRES = blocks("R", 0, 4096)

        def mm(bank, out_ap, lhsT, rhs, reads, start, stop, tp=None,
               first=True, last=True):
            return E("tensor",
                     lambda e: e.matmul(out_ap, lhsT, rhs, start=start,
                                        stop=stop, tile_position=tp,
                                        skip_group_check=True),
                     reads=reads, writes=([bank] if last else []),
                     after=sc.bank_deps(bank) if first else ())

        def mm_group(bank, items):
            n = len(items)
            for i, (o, l, r, rd, st, sp, tp) in enumerate(items):
                mm(bank, o, l, r, rd, st, sp, tp, first=(i == 0),
                   last=(i == n - 1))

        E("sync", lambda e: e.dma_start(out=ident[:], in_=idf_d), dma="ident",
          writes=["ident"])
        E("gpsimd", lambda e: e.dma_start(out=p64b[:], in_=p64_d), dma="p64b",
          writes=["p64b"])
        for c in range(2):
            E("sync", lambda e, c=c: e.dma_start(out=gamb[:, c:c + 1],
              in_=gam_d[c].unsqueeze(1)), dma="gamb", writes=["gamb"])
            E("sync", lambda e, c=c: e.dma_start(out=betb[:, c:c + 1],
              in_=bet_d[c].unsqueeze(1)), dma="betb", writes=["betb"])
        for b in range(4):
            E("sync", lambda e, b=b: e.dma_start(
                out=sfh[b * 32:(b + 1) * 32, 0:1], in_=fhl_d[:, 0:1],
                allow_slow_non_contiguous=True), dma="sfh", writes=["sfh"])
            E("sync", lambda e, b=b: e.dma_start(
                out=sfl[b * 32:(b + 1) * 32, 0:1], in_=fhl_d[:, 1:2],
                allow_slow_non_contiguous=True), dma="sfl", writes=["sfl"])
        E("gpsimd", lambda e: e.dma_start(
            out=wdkv_s[:].rearrange("p (c o) -> p c o", c=8),
            in_=wdkv_d.rearrange("(c p) o -> p c o", p=128)), dma="wdkv",
          writes=["wdkv_s"])
        E("gpsimd", lambda e: e.dma_start(
            out=wuk_s[:].rearrange("p (c o) -> p c o", c=2),
            in_=wuk_d.rearrange("(c p) o -> p c o", p=128)), dma="wuk",
          writes=["wuk_s"])
        E("gpsimd", lambda e: e.dma_start(
            out=wuv_s[:].rearrange("p (c o) -> p c o", c=2),
            in_=wuv_d.rearrange("(c p) o -> p c o", p=128)), dma="wuv",
          writes=["wuv_s"])

        E("gpsimd", lambda e: e.dma_start(out=onesb[:], in_=cst_d[:, 0:1], allow_slow_non_contiguous=True),
          dma="onesb", writes=["onesb"])
        E("gpsimd", lambda e: e.dma_start(out=onescol[:],
          in_=cst_d[0:65, 0:128], allow_slow_non_contiguous=True), dma="onescol", writes=["onescol"])
        E("sync", lambda e: e.dma_start(out=s127[:],
          in_=cst_d[:, 2113:2114], allow_slow_non_contiguous=True), dma="s127", writes=["s127"])
        E("sync", lambda e: e.dma_start(out=seps[:],
          in_=cst_d[:, 2114:2115], allow_slow_non_contiguous=True), dma="seps", writes=["seps"])
        E("sync", lambda e: e.dma_start(out=sgn[:],
          in_=cst_d[:, 2112:2113], allow_slow_non_contiguous=True), dma="sgn", writes=["sgn"])
        E("vector", lambda e: e.reciprocal(r127[:], s127[:]),
          reads=["s127"], writes=["r127"])
        E("gpsimd", lambda e: e.dma_start(out=vbuf[:],
          in_=cst_d[:, 0:2112], allow_slow_non_contiguous=True), dma="vbufi",
          writes=blocks("vb", 0, 2112, 132))
        E("gpsimd", lambda e: e.dma_start(out=maskP[:], in_=mkp_d),
          dma="maskP", writes=["maskP"])


        PIH = float(np.float32(np.pi))
        PIL = float(np.float32(np.pi - np.float64(PIH)))

        def tv(fn):
            E("vector", fn, reads=TBRES, writes=TBRES)

        def table_chunk(dst, dstres, c0, add_pi2, scale_ap):
            E("sync", lambda e: e.dma_start(out=tb1[:],
              in_=pos_d[:, c0:c0 + 512]), dma="post", writes=TBRES)
            E("vector", lambda e: e.tensor_scalar(tbh[:], tb1[:], sfh[:],
              None, ALU.mult), reads=TBRES + ["sfh"], writes=TBRES)
            E("vector", lambda e: e.tensor_scalar(tbT[:], tb1[:], sfl[:],
              None, ALU.mult), reads=TBRES + ["sfl"], writes=TBRES)
            tv(lambda e: e.tensor_add(tbh[:], tbh[:], tbT[:]))
            if add_pi2:
                tv(lambda e: e.tensor_single_scalar(tbh[:], tbh[:], PI2H,
                   ALU.add))
                tv(lambda e: e.tensor_single_scalar(tbh[:], tbh[:], PI2L,
                   ALU.add))
            tv(lambda e: e.tensor_single_scalar(tbT[:], tbh[:], I2PH,
               ALU.mult))
            tv(lambda e: e.tensor_single_scalar(tb1[:], tbh[:], I2PL,
               ALU.mult))
            tv(lambda e: e.tensor_add(tbT[:], tbT[:], tb1[:]))
            tv(lambda e: e.tensor_copy(tbK[:], tbT[:]))
            tv(lambda e: e.tensor_copy(tb1[:], tbK[:]))
            tv(lambda e: e.tensor_tensor(tbT[:], tb1[:], tbT[:], ALU.is_gt))
            tv(lambda e: e.tensor_sub(tb1[:], tb1[:], tbT[:]))
            tv(lambda e: e.tensor_single_scalar(tbT[:], tb1[:], TPH,
               ALU.mult))
            tv(lambda e: e.tensor_sub(tbh[:], tbh[:], tbT[:]))
            tv(lambda e: e.tensor_single_scalar(tbT[:], tb1[:], TPL,
               ALU.mult))
            tv(lambda e: e.tensor_sub(tbh[:], tbh[:], tbT[:]))
            tv(lambda e: e.tensor_single_scalar(tbh[:], tbh[:], -PIH,
               ALU.add))
            tv(lambda e: e.tensor_single_scalar(tbh[:], tbh[:], -PIL,
               ALU.add))
            tv(lambda e: e.tensor_scalar_max(tbh[:], tbh[:], -3.1415925))
            tv(lambda e: e.tensor_scalar_min(tbh[:], tbh[:], 3.1415925))
            E("scalar", lambda e: e.activation(dst[:, c0:c0 + 512], tbh[:],
              ACTF.Sin, scale=scale_ap), reads=TBRES + ["sgn"],
              writes=[dstres])

        for q4 in range(4):
            table_chunk(Stab, "Stab", q4 * 512, False, sgn[:])
            table_chunk(Ctab, "Ctab", q4 * 512, True, -1.0)

        def layer_norm(qci, nch, src_fn, dst_fn, nrm):
            items = []
            for dc in range(nch):
                a, rn = src_fn(dc)
                items.append((pb[6][0:1, :], onesb[:, 0:1], a,
                              rn + ["onesb"], dc == 0, dc == nch - 1, None))
            mm_group("pb6", items)
            for dc in range(nch):
                a, rn = src_fn(dc)
                xa = x2scr[:, (dc % 2) * 512:(dc % 2) * 512 + 512]
                x2n = "x2_%d" % (dc % 2)
                E("vector", lambda e, a=a, xa=xa: e.tensor_mul(xa, a, a),
                  reads=rn, writes=[x2n])
                mm("pb7", pb[7][0:1, :], onesb[:, 0:1], xa,
                   [x2n, "onesb"], dc == 0, dc == nch - 1, None,
                   first=(dc == 0), last=(dc == nch - 1))
            rpA = rowpool[0:1, 0:512]
            rpB = rowpool[0:1, 512:1024]
            rpT = rowpool[0:1, 1024:1536]
            sl = 0
            srs = statrow[0:1, 0:512]
            srm = statrow[0:1, 512:1024]
            rs_n, rm_n = "srs%d" % sl, "srm%d" % sl
            inv_n = 1.0 / nrm
            E("scalar", lambda e: e.activation(rpA, pb[6][0:1, :], ACTF.Copy,
              scale=inv_n), reads=["pb6"], writes=["rpA"])
            E("scalar", lambda e: e.activation(rpB, pb[7][0:1, :], ACTF.Copy,
              scale=inv_n), reads=["pb7"], writes=["rpB"])
            E("vector", lambda e: e.tensor_mul(rpT, rpA, rpA),
              reads=["rpA"], writes=["rpT"])
            E("vector", lambda e: e.tensor_sub(rpT, rpB, rpT),
              reads=["rpB", "rpT"], writes=["rpT"])
            E("scalar", lambda e: e.activation(rpB, rpT, ACTF.Ln,
              bias=seps[0:1, 0:1]), reads=["rpT", "seps"], writes=["rpB"])
            E("scalar", lambda e: e.activation(srs, rpB, ACTF.Exp,
              scale=-0.5), reads=["rpB"], writes=[rs_n])
            E("vector", lambda e: e.scalar_tensor_tensor(srm, rpA, -1.0, srs,
              ALU.mult, ALU.mult), reads=["rpA", rs_n], writes=[rm_n])
            mm("pb6", pb[6][:], onescol[0:1, 0:128], srs,
               [rs_n, "onescol"], True, True, (0, 0))
            mm("pb7", pb[7][:], onescol[0:1, 0:128], srm,
               [rm_n, "onescol"], True, True, (0, 0))
            for dc in range(nch):
                a, rn = src_fn(dc)
                d, wn, post = dst_fn(dc)
                E("vector", lambda e, a=a, d=d: e.scalar_tensor_tensor(
                    d, a, 1.0, pb[6][:], ALU.mult, ALU.mult),
                  reads=rn + ["pb6"], writes=wn)
                E("vector", lambda e, d=d: e.tensor_add(d, d, pb[7][:]),
                  reads=["pb7"] + wn, writes=wn)
                if post is not None:
                    oc = post
                    E("vector", lambda e, d=d, oc=oc: e.tensor_scalar(
                        d, d, gamb[:, oc:oc + 1], betb[:, oc:oc + 1],
                        ALU.mult, ALU.add),
                      reads=wn + ["gamb", "betb"], writes=wn)

        def dbg_store(s, ap3, rd=()):
            return E("gpsimd", lambda e, s=s, ap3=ap3: e.dma_start(
                out=out_d[s].rearrange("(c p) t -> p c t", p=128)[
                    :, 0:ap3.shape[1], :],
                in_=ap3), dma="dbg",
                reads=list(rd), after=sc.barrier(), force=True)

        sc.barrier = lambda: [(k, v) for k, v in sc.cnt.items() if v > 0]

        store_toks = []
        for s in range(2):
            if DBG_STAGE == "s0only" and s == 1:
                continue
            Rc = s * 16384
            Ro = (1 - s) * 16384

            E("gpsimd", lambda e, s=s, Rc=Rc: e.dma_start(
                out=R[:, Rc:Rc + 16384].rearrange("p (c t) -> p c t", c=8),
                in_=x_d[s].rearrange("(c p) t -> p c t", p=128)),
              dma="xl", writes=blocks("R", Rc, 16384))

            for qc in range(QC):
                tc = qc * 512

                def src(dc, Rc=Rc, tc=tc):
                    c = Rc + dc * 2048 + tc
                    return (R[:, c:c + 512], blocks("R", c, 512))

                def dst(dc, tc=tc):
                    c = dc * 2048 + tc
                    return (xnA[:, c:c + 512], blocks("xA", c, 512), None)
                layer_norm(qc, DC, src, dst, D)

            if DBG_STAGE == "xn":
                store_toks.append(dbg_store(
                    s, xnA[:].rearrange("p (c t) -> p c t", c=8),
                    rd=blocks("xA", 0, 16384)))
                continue

            for qc in range(QC):
                tc = qc * 512
                for oc in range(LC):
                    items = []
                    for dc in range(DC):
                        c = dc * 2048 + tc
                        items.append((pb[0][:],
                                      wdkv_s[:, dc * 256 + oc * 128:
                                             dc * 256 + oc * 128 + 128],
                                      xnA[:, c:c + 512],
                                      ["wdkv_s"] + blocks("xA", c, 512),
                                      dc == 0, dc == DC - 1, None))
                    mm_group("pb0", items)
                    lc_ = oc * 2048 + tc
                    E("scalar", lambda e, lc_=lc_: e.activation(
                        latn[:, lc_:lc_ + 512], pb[0][:], ACTF.Copy),
                      reads=["pb0"], writes=blocks("ln", lc_, 512))

                def srcl(ocx, tc=tc):
                    c = ocx * 2048 + tc
                    return (latn[:, c:c + 512], blocks("ln", c, 512))

                def dstl(ocx, tc=tc):
                    c = ocx * 2048 + tc
                    return (latn[:, c:c + 512], blocks("ln", c, 512), ocx)
                layer_norm(qc, LC, srcl, dstl, 256)

            if DBG_STAGE == "latn":
                store_toks.append(dbg_store(
                    s, latn[:].rearrange("p (c t) -> p c t", c=2),
                    rd=blocks("ln", 0, 4096)))
                continue

            for hp in range(8):
                hs = hp % 2
                qof = hs * 2048
                wqn = "wqs%d" % hs
                E("gpsimd", lambda e, hp=hp, hs=hs: e.dma_start(
                    out=wqs[:, hs * 1024:hs * 1024 + 1024].rearrange(
                        "p (c o) -> p c o", c=8),
                    in_=wq_d.rearrange("(c p) o -> p c o", p=128)[
                        :, :, hp * 128:hp * 128 + 128]),
                  dma="wq%d" % hs, writes=[wqn])

                def rope(dst_ap, dst_res, tc, qp):
                    qsl = q_sb[:, qp * 512:qp * 512 + 512]
                    t2 = t2sb[:, 0:512]
                    t1 = t1sb[:, qp * 512:qp * 512 + 512]
                    qn, t1n, t2n = "qsb%d" % qp, "t1_%d" % qp, "t2_0"
                    E("scalar", lambda e: e.activation(qsl, pb[0][:],
                      ACTF.Copy), reads=["pb0"], writes=[qn])
                    mm("pb1", pb[1][:], p64b[:], qsl, ["p64b", qn],
                       True, True, None)
                    E("vector", lambda e: e.tensor_mul(t2, pb[1][:],
                      Stab[:, tc:tc + 512]), reads=["pb1", "Stab"],
                      writes=[t2n])
                    E("vector", lambda e: e.tensor_mul(t1, qsl,
                      Ctab[:, tc:tc + 512]), reads=[qn, "Ctab"],
                      writes=[t1n])
                    E("vector", lambda e: e.tensor_add(dst_ap, t1, t2),
                      reads=[t1n, t2n], writes=dst_res)

                for qc in range(QC):
                    tc = qc * 512
                    items = []
                    for dc in range(DC):
                        c = dc * 2048 + tc
                        items.append((pb[0][:],
                                      wqs[:, hs * 1024 + dc * 128:
                                          hs * 1024 + dc * 128 + 128],
                                      xnA[:, c:c + 512],
                                      [wqn] + blocks("xA", c, 512),
                                      dc == 0, dc == DC - 1, None))
                    mm_group("pb0", items)
                    rope(qt[:, qof + tc:qof + tc + 512],
                         blocks("qt", qof + tc, 512), tc, qc % 2)
                for qc in range(QC):
                    tc = qc * 512
                    items = []
                    for lc in range(LC):
                        c = lc * 2048 + tc
                        items.append((pb[0][:],
                                      wuk_s[:, lc * 1024 + hp * 128:
                                            lc * 1024 + hp * 128 + 128],
                                      latn[:, c:c + 512],
                                      ["wuk_s"] + blocks("ln", c, 512),
                                      lc == 0, lc == LC - 1, None))
                    mm_group("pb0", items)
                    rope(ktb[:, qof + tc:qof + tc + 512],
                         blocks("kk", qof + tc, 512), tc, qc % 2)
                for kti in range(KT):
                    items = []
                    for lc in range(LC):
                        c = lc * 2048 + kti * 128
                        items.append((pb[2][:, 0:128],
                                      latn[:, c:c + 128],
                                      wuv_s[:, lc * 1024 + hp * 128:
                                            lc * 1024 + hp * 128 + 128],
                                      ["wuv_s"] + blocks("ln", c, 128),
                                      lc == 0, lc == LC - 1, None))
                    mm_group("pb2", items)
                    E("vector", lambda e, kti=kti: e.tensor_copy(
                        vbuf[:, kti * 132:kti * 132 + 132].rearrange(
                            "p (h x) -> p h x", h=2)[:, :, 0:64],
                        pb[2][:, 0:128].rearrange("p (h x) -> p h x", h=2)),
                      reads=["pb2"], writes=["vb%d" % kti])

                if DBG_STAGE == "qkv" and hp == 0:
                    store_toks.append(dbg_store(
                        s, ktb[:, 0:2048].rearrange("p (c t) -> p c t", c=1),
                        rd=blocks("kk", 0, 2048)))
                    store_toks.append(E("gpsimd", lambda e, s=s: e.dma_start(
                        out=out_d[s].rearrange("(c p) t -> p c t", p=128)[
                            :, 1:2, :],
                        in_=qt[:, 0:2048].rearrange(
                            "p (c t) -> p c t", c=1)), dma="dbg",
                        force=True,
                        reads=blocks("qt", 0, 2048), after=sc.barrier()))
                    store_toks.append(E("gpsimd", lambda e, s=s: e.dma_start(
                        out=out_d[s].rearrange("(c p) t -> p c t", p=128)[
                            :, 2:3, 0:2048],
                        in_=vbuf[:, 0:2048].rearrange(
                            "p (c t) -> p c t", c=1)), dma="dbg",
                        force=True,
                        reads=blocks("vb", 0, 2048, 132),
                        after=sc.barrier()))
                    break

                for p1rep in range(2):
                  for head in range(2):
                      r0 = head * 64
                      for qtile in range(KT):
                          qcol = qof + qtile * 128
                          qres = blocks("qt", qcol, 128)
                          nkc = (qtile * 128 + 128 + 511) // 512
                          for kc in range(nkc):
                              N = min(512, qtile * 128 + 128 - kc * 512)
                              b = head + 2 * (kc % 2)
                              mm(PB[b], pb[b][:, 0:N],
                                 qt[r0:r0 + 64, qcol:qcol + 128],
                                 ktb[r0:r0 + 64, qof + kc * 512:
                                     qof + kc * 512 + N],
                                 qres + blocks("kk", qof + kc * 512, N),
                                 True, True, (r0, 0))
                          for kc in range(nkc):
                              N = min(512, qtile * 128 + 128 - kc * 512)
                              b = head + 2 * (kc % 2)
                              if kc == 0:
                                  E("vector", lambda e, b=b, N=N, head=head,
                                    qtile=qtile: e.reduce_max(
                                      mbuf[:, head * 16 + qtile:
                                           head * 16 + qtile + 1],
                                      pb[b][:, 0:N], axis=AX),
                                    reads=[PB[b]], writes=["mbuf"])
                              else:
                                  E("vector", lambda e, b=b, N=N, head=head:
                                    e.reduce_max(mtmp[:, head:head + 1],
                                                 pb[b][:, 0:N], axis=AX),
                                    reads=[PB[b]], writes=["mtmp"])
                                  E("vector", lambda e, head=head, qtile=qtile:
                                    e.tensor_max(
                                      mbuf[:, head * 16 + qtile:
                                           head * 16 + qtile + 1],
                                      mbuf[:, head * 16 + qtile:
                                           head * 16 + qtile + 1],
                                      mtmp[:, head:head + 1]),
                                    reads=["mtmp", "mbuf"], writes=["mbuf"])
                                  if (DBG_STAGE == "mdump" and hp == 0
                                          and head == 0 and qtile == 4
                                          and s == 0):
                                      E("vector", lambda e: e.reduce_max(
                                          dbgc[:, 0:1], pb[0][:, 0:512],
                                          axis=AX), reads=["pb0"],
                                        writes=["dbgc"])
                                      E("vector", lambda e: e.reduce_max(
                                          dbgc[:, 1:2], pb[2][:, 0:128],
                                          axis=AX), reads=["pb2"],
                                        writes=["dbgc"])
                      E("vector", lambda e, head=head: e.tensor_scalar_mul(
                          mbuf[:, head * 16:head * 16 + 16],
                          mbuf[:, head * 16:head * 16 + 16], -1.0),
                        reads=["mbuf"], writes=["mbuf"])
                      E("vector", lambda e, head=head: e.tensor_scalar_max(
                          mbuf[:, head * 16:head * 16 + 16],
                          mbuf[:, head * 16:head * 16 + 16], -64.0),
                        reads=["mbuf"], writes=["mbuf"])
                      E("vector", lambda e, head=head: e.tensor_scalar_min(
                          mbuf[:, head * 16:head * 16 + 16],
                          mbuf[:, head * 16:head * 16 + 16], 64.0),
                        reads=["mbuf"], writes=["mbuf"])
                      E("tensor", lambda e, head=head: e.transpose(
                          pb[2][0:16, 0:128], mbuf[:, head * 16:head * 16 + 16],
                          ident[:]), reads=["mbuf", "ident"], writes=["pb2"],
                        after=sc.bank_deps("pb2"))
                      E("vector", lambda e, head=head: e.tensor_copy(
                          smtt[:, head * 128:head * 128 + 128],
                          pb[2][0:16, 0:128]), reads=["pb2"],
                        writes=["smtt%d" % head])
                      mr = head * 32
                      E("sync", lambda e, head=head, mr=mr, hs=hs: e.dma_start(
                          out=mrow[mr:mr + 1, 0:2048].rearrange(
                              "p (t q) -> p t q", t=16),
                          in_=smtt[:, head * 128:head * 128 + 128]),
                        dma="mr%d%d" % (head, hs),
                        reads=["smtt%d" % head],
                        writes=["mrow%d_%d" % (head, hs)])

                  if DBG_STAGE == "mdump" and hp == 0:
                      store_toks.append(E("gpsimd", lambda e, s=s: e.dma_start(
                          out=out_d[s].rearrange("(c p) t -> p c t", p=128)[
                              :, 0:1, 32:34],
                          in_=mtmp[:].rearrange("p (c t) -> p c t", c=1)),
                          dma="dbg", force=True, reads=["mtmp"],
                          after=sc.barrier()))
                      store_toks.append(E("gpsimd", lambda e, s=s: e.dma_start(
                          out=out_d[s].rearrange("(c p) t -> p c t", p=128)[
                              :, 0:1, 40:44],
                          in_=dbgc[:].rearrange("p (c t) -> p c t", c=1)),
                          dma="dbg", force=True, reads=["dbgc"],
                          after=sc.barrier()))
                      store_toks.append(E("gpsimd", lambda e, s=s: e.dma_start(
                          out=out_d[s].rearrange("(c p) t -> p c t", p=128)[
                              :, 2:3, 0:2048],
                          in_=qt[:, 0:2048].rearrange(
                              "p (c t) -> p c t", c=1)),
                          dma="dbg", force=True, reads=blocks("qt", 0, 2048),
                          after=sc.barrier()))
                      store_toks.append(E("gpsimd", lambda e, s=s: e.dma_start(
                          out=out_d[s].rearrange("(c p) t -> p c t", p=128)[
                              :, 3:4, 0:2048],
                          in_=ktb[:, 0:2048].rearrange(
                              "p (c t) -> p c t", c=1)),
                          dma="dbg", force=True, reads=blocks("kk", 0, 2048),
                          after=sc.barrier()))
                      store_toks.append(E("gpsimd", lambda e, s=s: e.dma_start(
                          out=out_d[s].rearrange("(c p) t -> p c t", p=128)[
                              :, 0:1, 0:32],
                          in_=mbuf[:].rearrange("p (c t) -> p c t", c=1)),
                          dma="dbg", force=True, reads=["mbuf"],
                          after=sc.barrier()))
                      store_toks.append(E("gpsimd", lambda e, s=s: e.dma_start(
                          out=out_d[s].rearrange("(c p) t -> p c t", p=128)[
                              0:33, 1:2, 0:2048],
                          in_=mrow[:, 0:2048].rearrange(
                              "p (c t) -> p c t", c=1)),
                          dma="dbg", force=True,
                          reads=["mrow0_0", "mrow1_0"], after=sc.barrier()))
                      break

                for qc in range(QC):
                    tc = qc * 512
                    lastkt = min(KT - 1, 4 * qc + 3)

                    def st_pair(kti, tc=tc, qof=qof, hs=hs):
                        par = kti % 2
                        off = max(0, kti * 128 - tc)
                        for head in range(2):
                            b = par * 2 + head
                            r0 = head * 64
                            kcol = qof + kti * 128
                            mm(PB[b], pb[b][:, off:512],
                               ktb[r0:r0 + 64, kcol:kcol + 128],
                               qt[r0:r0 + 64, qof + tc + off:qof + tc + 512],
                               blocks("kk", kcol, 128) +
                               blocks("qt", qof + tc + off, 512 - off),
                               True, False, (r0, 0), first=True, last=False)
                            mr = head * 32
                            mm(PB[b], pb[b][:, off:512],
                               onescol[mr:mr + 1, 0:128],
                               mrow[mr:mr + 1, tc + off:tc + 512],
                               ["onescol", "mrow%d_%d" % (head, hs)],
                               False, True, (mr, 0), first=False, last=True)

                    def exp_mask(kti, tc=tc):
                        par = kti % 2
                        off = max(0, kti * 128 - tc)
                        for head in range(2):
                            b = par * 2 + head
                            sl = head * 2 + par
                            ps = pbuf[:, sl * 512:sl * 512 + 512]
                            pres = "pbuf%d" % sl
                            E("scalar", lambda e, b=b, off=off, ps=ps:
                              e.activation(ps[:, off:512], pb[b][:, off:512],
                                           ACTF.Exp, scale=0.125),
                              reads=[PB[b]], writes=[pres])
                            if kti * 128 >= tc:
                                E("vector", lambda e, ps=ps, off=off:
                                  e.tensor_mul(ps[:, off:off + 128],
                                               ps[:, off:off + 128],
                                               maskP[:]),
                                  reads=[pres, "maskP"], writes=[pres])

                    def av(kti, tc=tc, lastkt=lastkt):
                        par = kti % 2
                        off = max(0, kti * 128 - tc)
                        for head in range(2):
                            bav = 4 + head
                            sl = head * 2 + par
                            ps = pbuf[:, sl * 512:sl * 512 + 512]
                            pres = "pbuf%d" % sl
                            mm(PB[bav], pb[bav][0:65, off:512],
                               vbuf[:, kti * 132 + head * 66:
                                    kti * 132 + head * 66 + 65],
                               ps[:, off:512], ["vb%d" % kti, pres],
                               kti == 0, kti == lastkt, None,
                               first=(kti == 0), last=(kti == lastkt))

                    st_pair(0)
                    for kti in range(lastkt + 1):
                        if kti + 1 <= lastkt:
                            st_pair(kti + 1)
                        exp_mask(kti)
                        av(kti)

                    for head in range(2):
                        bav = 4 + head
                        bb = head
                        rr = reciprow[64:65, head * 512:head * 512 + 512]
                        rres = "recip%d" % head
                        E("scalar", lambda e, bav=bav: e.activation(
                            pb[bav][64:65, :], pb[bav][64:65, :], ACTF.Ln),
                          reads=[PB[bav]], writes=[PB[bav]])
                        E("scalar", lambda e, bav=bav, rr=rr: e.activation(
                            rr, pb[bav][64:65, :], ACTF.Exp, scale=-1.0),
                          reads=[PB[bav]], writes=[rres])
                        mm(PB[bb], pb[bb][0:64, :], onescol[64:65, 0:64], rr,
                           [rres, "onescol"], True, True, (64, 0))
                        avs = av_sb[:, head * 512:head * 512 + 512]
                        avn = "avsb%d" % head
                        E("scalar", lambda e, bav=bav, avs=avs: e.activation(
                            avs, pb[bav][0:64, :], ACTF.Copy),
                          reads=[PB[bav]], writes=[avn])
                        ac = Ro + hp * 2048 + tc
                        if head == 0:
                            E("vector", lambda e, avs=avs, bb=bb, ac=ac:
                              e.scalar_tensor_tensor(
                                R[0:64, ac:ac + 512], avs, 1.0,
                                pb[bb][0:64, :], ALU.mult, ALU.mult),
                              reads=[avn, PB[bb]], writes=blocks("R", ac, 512))
                        else:
                            E("vector", lambda e, avs=avs, bb=bb, tc=tc:
                              e.scalar_tensor_tensor(
                                bstg[:, tc:tc + 512], avs, 1.0,
                                pb[bb][0:64, :], ALU.mult, ALU.mult),
                              reads=[avn, PB[bb]], writes=["bstg"])
                if True:
                    ac0 = Ro + hp * 2048
                    E("sync", lambda e, ac0=ac0: e.dma_start(
                        out=R[64:128, ac0:ac0 + 2048], in_=bstg[:]),
                      dma="bs", reads=["bstg"],
                      writes=blocks("R", ac0, 2048))

            if DBG_STAGE == "qkv":
                continue
            if DBG_STAGE == "att":
                store_toks.append(dbg_store(
                    s, R[:, Ro:Ro + 16384].rearrange("p (c t) -> p c t",
                                                     c=8),
                    rd=blocks("R", Ro, 16384)))
                continue

            for oc in range(8):
                osl = oc % 2
                won = "wos%d" % osl
                E("gpsimd", lambda e, oc=oc, osl=osl: e.dma_start(
                    out=wos[:, osl * 1024:osl * 1024 + 1024].rearrange(
                        "p (c o) -> p c o", c=8),
                    in_=wo_d.rearrange("(c p) o -> p c o", p=128)[
                        :, :, oc * 128:oc * 128 + 128]),
                  dma="wo%d" % osl, writes=[won])
                for qc in range(QC):
                    tc = qc * 512
                    b = qc % 2
                    items = []
                    for dc in range(DC):
                        ac = Ro + dc * 2048 + tc
                        items.append((pb[b][:],
                                      wos[:, osl * 1024 + dc * 128:
                                          osl * 1024 + dc * 128 + 128],
                                      R[:, ac:ac + 512],
                                      [won] + blocks("R", ac, 512),
                                      dc == 0, dc == DC - 1, None))
                    mm_group(PB[b], items)
                    rc = Rc + oc * 2048 + tc
                    E("vector", lambda e, b=b, rc=rc: e.tensor_add(
                        R[:, rc:rc + 512], R[:, rc:rc + 512], pb[b][:]),
                      reads=[PB[b]] + blocks("R", rc, 512),
                      writes=blocks("R", rc, 512))

            if DBG_STAGE == "wo":
                store_toks.append(dbg_store(
                    s, R[:, Rc:Rc + 16384].rearrange("p (c t) -> p c t",
                                                     c=8),
                    rd=blocks("R", Rc, 16384)))
                continue

            def hloc(hc, t0):
                if hc < 16:
                    c = Ro + hc * 1024 + t0
                    return (R[:, c:c + 512], blocks("R", c, 512))
                c = 8192 + (hc - 16) * 1024 + t0
                return (xnA[:, c:c + 512], blocks("xA", c, 512))

            for st_i in range(2):
                stc = st_i * 1024
                for qc2 in range(2):
                    loc = qc2 * 512

                    def src2(dc, Rc=Rc, stc=stc, loc=loc):
                        c = Rc + dc * 2048 + stc + loc
                        return (R[:, c:c + 512], blocks("R", c, 512))

                    def dst2(dc, loc=loc):
                        c = dc * 1024 + loc
                        return (xnA[:, c:c + 512], blocks("xA", c, 512),
                                None)
                    layer_norm(qc2, DC, src2, dst2, D)

                if DBG_STAGE == "ffnln" and st_i == 0:
                    store_toks.append(dbg_store(
                        s, xnA[:, 0:8192].rearrange("p (c t) -> p c t", c=4),
                        rd=blocks("xA", 0, 8192)))
                    break
                for hc in range(HC):
                    hsl = hc % 2
                    wgn, wun = "wgs%d" % hsl, "wus%d" % hsl
                    E("gpsimd", lambda e, hc=hc, hsl=hsl: e.dma_start(
                        out=wgs[:, hsl * 1024:hsl * 1024 + 1024].rearrange(
                            "p (c o) -> p c o", c=8),
                        in_=wgt_d.rearrange("(c p) h -> p c h", p=128)[
                            :, :, hc * 128:hc * 128 + 128]),
                      dma="wg%d" % hsl, writes=[wgn])
                    E("gpsimd", lambda e, hc=hc, hsl=hsl: e.dma_start(
                        out=wus[:, hsl * 1024:hsl * 1024 + 1024].rearrange(
                            "p (c o) -> p c o", c=8),
                        in_=wut_d.rearrange("(c p) h -> p c h", p=128)[
                            :, :, hc * 128:hc * 128 + 128]),
                      dma="wu%d" % hsl, writes=[wun])
                    for tt in range(2):
                        t0 = tt * 512
                        bg, bu = tt * 2, tt * 2 + 1
                        for wname, bx in ((wgn, bg), (wun, bu)):
                            wt = wgs if wname.startswith("wg") else wus
                            items = []
                            for dc in range(DC):
                                c = dc * 1024 + t0
                                items.append((pb[bx][:],
                                              wt[:, hsl * 1024 + dc * 128:
                                                 hsl * 1024 + dc * 128 + 128],
                                              xnA[:, c:c + 512],
                                              [wname] + blocks("xA", c, 512),
                                              dc == 0, dc == DC - 1, None))
                            mm_group(PB[bx], items)
                        gs = q_sb[:, tt * 512:tt * 512 + 512]
                        gn = "qsb%d" % tt
                        E("scalar", lambda e, bg=bg, gs=gs: e.activation(
                            gs, pb[bg][:], ACTF.Silu, scale=r127[:]),
                          reads=[PB[bg], "r127"], writes=[gn])
                        ha, hres = hloc(hc, t0)
                        E("vector", lambda e, bu=bu, gs=gs, ha=ha:
                          e.scalar_tensor_tensor(ha, pb[bu][:], r127[:], gs,
                                                 ALU.mult, ALU.mult),
                          reads=[PB[bu], "r127", gn], writes=hres)
                if DBG_STAGE == "ffnh" and st_i == 0:
                    store_toks.append(dbg_store(
                        s, R[:, Ro:Ro + 16384].rearrange(
                            "p (c t) -> p c t", c=8),
                        rd=blocks("R", Ro, 16384)))
                    break
                if DBG_STAGE == "ffnh2" and st_i == 0:
                    store_toks.append(dbg_store(
                        s, xnA[:].rearrange("p (c t) -> p c t", c=8),
                        rd=blocks("xA", 0, 16384)))
                    break
                for dc in range(DC):
                    dsl = 0
                    wdn = "wds0"
                    E("gpsimd", lambda e, dc=dc, dsl=dsl: e.dma_start(
                        out=wds[:, 0:3072].rearrange(
                            "p (c o) -> p c o", c=24),
                        in_=wdt_d.rearrange("(c p) o -> p c o", p=128)[
                            :, :, dc * 128:dc * 128 + 128]),
                      dma="wd%d" % dsl, writes=[wdn])
                    for tt in range(2):
                        t0 = tt * 512
                        b = 4 + tt
                        items = []
                        for hc in range(HC):
                            ha, hres = hloc(hc, t0)
                            items.append((pb[b][:],
                                          wds[:, dsl * 3072 + hc * 128:
                                              dsl * 3072 + hc * 128 + 128],
                                          ha, [wdn] + hres,
                                          hc == 0, hc == HC - 1, None))
                        mm_group(PB[b], items)
                        rc = Rc + dc * 2048 + stc + t0
                        E("vector", lambda e, b=b, rc=rc:
                          e.scalar_tensor_tensor(
                            R[:, rc:rc + 512], pb[b][:], r127[:],
                            R[:, rc:rc + 512], ALU.mult, ALU.add),
                          reads=[PB[b], "r127"] + blocks("R", rc, 512),
                          writes=blocks("R", rc, 512))

            if DBG_STAGE not in ("ffnln", "ffnh", "ffnh2", "mdump"):
                tok = E("gpsimd", lambda e, s=s, Rc=Rc: e.dma_start(
                    out=out_d[s].rearrange("(c p) t -> p c t", p=128),
                    in_=R[:, Rc:Rc + 16384].rearrange("p (c t) -> p c t",
                                                      c=8)),
                  dma="st", reads=blocks("R", Rc, 16384))
                store_toks.append(tok)

        E("sync", lambda e: e.nop(), after=store_toks, tick=False)

        blk.tensor(lambda e: [f(e) for f in sc.q["tensor"]])
        blk.vector(lambda e: [f(e) for f in sc.q["vector"]])
        blk.scalar(lambda e: [f(e) for f in sc.q["scalar"]])
        blk.gpsimd(lambda e: [f(e) for f in sc.q["gpsimd"]])
        blk.sync(lambda e: [f(e) for f in sc.q["sync"]])

    nc._sched = sc
    import sys
    print("kernel build: instrs",
          {k: len(v) for k, v in sc.q.items()}, "waits", sc.nwait,
          file=sys.stderr)
    return nc


_NC = None


def _get_nc():
    global _NC
    if _NC is None:
        _NC = build_nc()
    return _NC


def kernel(L0_attn_x, L0_attn_W_q, L0_attn_W_dkv, L0_attn_W_uk, L0_attn_W_uv,
           L0_attn_W_o, L0_attn_ln_gamma, L0_attn_ln_beta, L0_ffn_w_gate,
           L0_ffn_w_up, L0_ffn_w_down):
    x = np.asarray(L0_attn_x, dtype=np.float32)
    xT = np.ascontiguousarray(np.transpose(x, (0, 2, 1)))
    wq = np.ascontiguousarray(np.asarray(L0_attn_W_q, np.float32))
    wdkv = np.ascontiguousarray(np.asarray(L0_attn_W_dkv, np.float32))
    wuk = np.ascontiguousarray(np.asarray(L0_attn_W_uk, np.float32))
    wuv = np.ascontiguousarray(np.asarray(L0_attn_W_uv, np.float32))
    wo = np.ascontiguousarray(np.asarray(L0_attn_W_o, np.float32))
    gam = np.ascontiguousarray(
        np.asarray(L0_attn_ln_gamma, np.float32).reshape(2, 128))
    bet = np.ascontiguousarray(
        np.asarray(L0_attn_ln_beta, np.float32).reshape(2, 128))
    wgt = np.ascontiguousarray(np.asarray(L0_ffn_w_gate, np.int32).T)
    wut = np.ascontiguousarray(np.asarray(L0_ffn_w_up, np.int32).T)
    wdt = np.ascontiguousarray(np.asarray(L0_ffn_w_down, np.int32).T)
    inv = 1.0 / (10000.0 ** (np.arange(32, dtype=np.float64) / 32.0))
    fh = inv.astype(np.float32)
    fl = (inv - fh.astype(np.float64)).astype(np.float32)
    fhl = np.ascontiguousarray(np.stack([fh, fl], axis=1))
    idf = np.eye(128, dtype=np.float32)
    p64 = np.zeros((128, 128), dtype=np.float32)
    p64[np.arange(128), np.arange(128) ^ 32] = 1.0
    cst = np.ones((128, 2118), dtype=np.float32)
    sgnv = np.ones(128, np.float32)
    sgnv[0:32] = 1.0; sgnv[32:64] = -1.0; sgnv[64:96] = 1.0; sgnv[96:128] = -1.0
    cst[:, 2112] = sgnv
    cst[:, 2113] = 127.0
    cst[:, 2114] = 1e-5
    mkp = (np.arange(128)[None, :] >= np.arange(128)[:, None]).astype(np.float32)
    post = np.broadcast_to(np.arange(2048, dtype=np.float32)[None, :],
                           (128, 2048)).copy()

    nc = _get_nc()
    in_maps = []
    for c in range(8):
        in_maps.append(dict(
            x=np.ascontiguousarray(xT[2 * c:2 * c + 2]),
            wq=wq, wdkv=wdkv, wuk=wuk, wuv=wuv, wo=wo, gam=gam, bet=bet,
            wgt=wgt, wut=wut, wdt=wdt, fhl=fhl, idf=idf, p64=p64,
            cst=cst, mkp=mkp, post=post))
    res = run_bass_kernel_spmd(nc, in_maps, core_ids=list(range(8)))
    outs = []
    for c in range(8):
        o = np.asarray(res.results[c]["out"])
        outs.append(np.transpose(o, (0, 2, 1)))
    return np.ascontiguousarray(np.concatenate(outs, axis=0))


if __name__ == "__main__":
    build_nc()
    print("build ok")


# version 09
# speedup vs baseline: 1.1907x; 1.0003x over previous
import numpy as np
from contextlib import ExitStack

import concourse.bass as bass
import concourse.mybir as mybir
from concourse.bass_utils import run_bass_kernel_spmd

F32 = mybir.dt.float32
BF16 = mybir.dt.bfloat16
I32 = mybir.dt.int32
AX = mybir.AxisListType.X
ALU = mybir.AluOpType
ACTF = mybir.ActivationFunctionType


def _sp(v):
    hi = float(np.float32(v))
    lo = float(np.float32(v - np.float64(hi)))
    return hi, lo


PI2H, PI2L = _sp(np.pi / 2)
TPH, TPL = _sp(2 * np.pi)
I2PH, I2PL = _sp(1.0 / (2 * np.pi))

D = 1024
DBG_STAGE = None
SENTINEL = False
FORCE_WAITS = True
DC = 8
LC = 2
HC = 24
QC = 4
KT = 16

ENG_SEM = {"tensor": "T", "vector": "V", "scalar": "S", "gpsimd": "G",
           "sync": "Y"}


class Sched:
    def __init__(self):
        self.q = {e: [] for e in ENG_SEM}
        self.new_sem = None
        self.cnt = {s: 0 for s in ("T", "V", "S", "G", "Y")}
        self.eclk = {e: {} for e in ENG_SEM}
        self.tokclk = {}
        self.res = {}
        self.semh = None
        self.nwait = 0

    def r_(self, name):
        if name not in self.res:
            self.res[name] = {"w": [], "r": []}
        return self.res[name]

    def emit(self, eng, fn, reads=(), writes=(), after=(), dma=False,
             tick=True, force=False):
        waits = []
        for r in reads:
            waits += self.r_(r)["w"]
        for w in writes:
            waits += self.r_(w)["w"] + self.r_(w)["r"]
        waits += list(after)
        my = ENG_SEM[eng]
        if dma:
            assert isinstance(dma, str), "dma must be a stream name"
            sem = "d_" + dma
            if sem not in self.cnt:
                self.cnt[sem] = 0
                self.semh[sem] = self.new_sem(sem)
            inc = 16
        else:
            sem = my
            inc = 1
        clk = self.eclk[eng]
        clk[my] = self.cnt[my]
        final = []
        for s, v in sorted(set(waits)):
            if s == my and not dma:
                continue
            if clk.get(s, 0) >= v and not (force or FORCE_WAITS):
                continue
            final.append((s, v))
            tc = self.tokclk.get((s, v))
            if tc:
                for s2, v2 in tc.items():
                    if clk.get(s2, 0) < v2:
                        clk[s2] = v2
            else:
                clk[s] = max(clk.get(s, 0), v)
        if not FORCE_WAITS:
            final = [(s, v) for (s, v) in final
                     if not (s != my and self.eclk[eng].get(s, -1) > v)]
        else:
            best = {}
            for s, v in final:
                best[s] = max(best.get(s, 0), v)
            final = sorted(best.items())
        self.nwait += len(final)
        tok = None
        if tick:
            self.cnt[sem] += inc
            tok = (sem, self.cnt[sem])
            clk2 = dict(clk)
            clk2[sem] = self.cnt[sem]
            self.tokclk[tok] = clk2
        self.eclk[eng] = clk

        def run(e, _w=tuple(final), _fn=fn, _tok=tok, _sem=sem, _inc=inc):
            for s, v in _w:
                e.wait_ge(self.semh[s], v)
            ins = _fn(e)
            if _tok is not None:
                ins.then_inc(self.semh[_sem], _inc)
        run._waits = tuple(final)
        run._tok = tok
        run._writes = tuple(writes)
        run._reads = tuple(reads)
        self.q[eng].append(run)

        if tok is not None:
            for w in writes:
                self.res[w] = {"w": [tok], "r": []}
            for r in reads:
                self.r_(r)["r"].append(tok)
        return tok

    def bank_deps(self, bank):
        return tuple(self.r_(bank)["w"] + self.r_(bank)["r"])


def blocks(base, col, n, gran=512):
    return ["%s%d" % (base, i) for i in range(col // gran,
                                              (col + n + gran - 1) // gran)]


def build_nc():
    nc = bass.Bass("TRN2", target_bir_lowering=False,
                   detect_race_conditions=False)
    dt = nc.dram_tensor
    x_d = dt("x", [2, D, 2048], F32, kind="ExternalInput").ap()
    wq_d = dt("wq", [D, D], F32, kind="ExternalInput").ap()
    wdkv_d = dt("wdkv", [D, 256], F32, kind="ExternalInput").ap()
    wuk_d = dt("wuk", [256, D], F32, kind="ExternalInput").ap()
    wuv_d = dt("wuv", [256, D], F32, kind="ExternalInput").ap()
    wo_d = dt("wo", [D, D], F32, kind="ExternalInput").ap()
    gam_d = dt("gam", [2, 128], F32, kind="ExternalInput").ap()
    bet_d = dt("bet", [2, 128], F32, kind="ExternalInput").ap()
    wgt_d = dt("wgt", [D, 3072], I32, kind="ExternalInput").ap()
    wut_d = dt("wut", [D, 3072], I32, kind="ExternalInput").ap()
    wdt_d = dt("wdt", [3072, D], I32, kind="ExternalInput").ap()
    fhl_d = dt("fhl", [32, 2], F32, kind="ExternalInput").ap()
    idf_d = dt("idf", [128, 128], F32, kind="ExternalInput").ap()
    p64_d = dt("p64", [128, 128], F32, kind="ExternalInput").ap()
    cst_d = dt("cst", [128, 2118], F32, kind="ExternalInput").ap()
    mkp_d = dt("mkp", [128, 128], F32, kind="ExternalInput").ap()
    pos_d = dt("post", [128, 2048], F32, kind="ExternalInput").ap()
    out_d = dt("out", [2, D, 2048], F32, kind="ExternalOutput").ap()

    sc = Sched()
    E = sc.emit

    with ExitStack() as stk:
        en = stk.enter_context
        sems = {}
        for s in ("T", "V", "S", "G", "Y"):
            sems[s] = en(nc.semaphore("sem_" + s))
        sc.semh = sems
        sc.new_sem = lambda name: en(nc.semaphore("sem_" + name))

        ident = en(nc.sbuf_tensor("ident", [128, 128], F32))
        p64b = en(nc.sbuf_tensor("p64b", [128, 128], BF16))
        maskP = en(nc.sbuf_tensor("maskP", [128, 128], BF16))
        onesb = en(nc.sbuf_tensor("onesb", [128, 1], BF16))
        onescol = en(nc.sbuf_tensor("onescol", [65, 128], BF16))
        sgn = en(nc.sbuf_tensor("sgn", [128, 1], F32))
        sfh = en(nc.sbuf_tensor("sfh", [128, 1], F32))
        sfl = en(nc.sbuf_tensor("sfl", [128, 1], F32))
        s127 = en(nc.sbuf_tensor("s127", [128, 1], F32))
        seps = en(nc.sbuf_tensor("seps", [128, 1], F32))
        r127 = en(nc.sbuf_tensor("r127", [128, 1], F32))
        gamb = en(nc.sbuf_tensor("gamb", [128, 2], F32))
        betb = en(nc.sbuf_tensor("betb", [128, 2], F32))
        rowpool = en(nc.sbuf_tensor("rowpool", [1, 1536], BF16))
        statrow = en(nc.sbuf_tensor("statrow", [1, 1024], BF16))
        reciprow = en(nc.sbuf_tensor("reciprow", [65, 1024], BF16))
        bstg = en(nc.sbuf_tensor("bstg", [64, 2048], BF16))
        smtt = en(nc.sbuf_tensor("smtt", [16, 256], BF16))
        mbuf = en(nc.sbuf_tensor("mbuf", [128, 32], F32))
        dbgc = en(nc.sbuf_tensor("dbgc", [128, 4], F32))
        mtmp = en(nc.sbuf_tensor("mtmp", [128, 2], F32))
        mrow = en(nc.sbuf_tensor("mrow", [33, 2048], BF16))
        x2scr = en(nc.sbuf_tensor("x2scr", [128, 1024], BF16))
        q_sb = en(nc.sbuf_tensor("q_sb", [128, 1024], BF16))
        t2sb = en(nc.sbuf_tensor("t2sb", [128, 512], F32))
        t1sb = en(nc.sbuf_tensor("t1sb", [128, 1024], BF16))
        av_sb = en(nc.sbuf_tensor("av_sb", [64, 1024], BF16))
        Ctab = en(nc.sbuf_tensor("Ctab", [128, 2048], BF16))
        Stab = en(nc.sbuf_tensor("Stab", [128, 2048], BF16))
        wdkv_s = en(nc.sbuf_tensor("wdkv_s", [128, 2048], BF16))
        wuk_s = en(nc.sbuf_tensor("wuk_s", [128, 2048], BF16))
        wuv_s = en(nc.sbuf_tensor("wuv_s", [128, 2048], BF16))
        wqs = en(nc.sbuf_tensor("wqs", [128, 2048], BF16))
        wos = en(nc.sbuf_tensor("wos", [128, 2048], BF16))
        wgs = en(nc.sbuf_tensor("wgs", [128, 2048], BF16))
        wus = en(nc.sbuf_tensor("wus", [128, 2048], BF16))
        wds = en(nc.sbuf_tensor("wds", [128, 3072], BF16))
        latn = en(nc.sbuf_tensor("latn", [128, 4096], BF16))
        qt = en(nc.sbuf_tensor("qt", [128, 4096], BF16))
        ktb = en(nc.sbuf_tensor("ktb", [128, 4096], BF16))
        vbuf = en(nc.sbuf_tensor("vbuf", [128, 2112], BF16))
        pbuf = en(nc.sbuf_tensor("pbuf", [128, 2048], BF16))
        tb1 = en(nc.sbuf_tensor("tb1", [128, 512], F32))
        tbh = en(nc.sbuf_tensor("tbh", [128, 512], F32))
        tbT = en(nc.sbuf_tensor("tbT", [128, 512], F32))
        tbK = en(nc.sbuf_tensor("tbK", [128, 512], I32))
        R = en(nc.sbuf_tensor("R", [128, 32768], BF16))
        xnA = en(nc.sbuf_tensor("xnA", [128, 16384], BF16))

        pb = [en(nc.psum_tensor("pb%d" % i, [128, 512], F32))
              for i in range(8)]
        PB = ["pb%d" % i for i in range(8)]

        blk = en(nc.Block())

        TBRES = blocks("R", 0, 4096)

        def mm(bank, out_ap, lhsT, rhs, reads, start, stop, tp=None,
               first=True, last=True):
            return E("tensor",
                     lambda e: e.matmul(out_ap, lhsT, rhs, start=start,
                                        stop=stop, tile_position=tp,
                                        skip_group_check=True),
                     reads=reads, writes=([bank] if last else []),
                     after=sc.bank_deps(bank) if first else ())

        def mm_group(bank, items):
            n = len(items)
            for i, (o, l, r, rd, st, sp, tp) in enumerate(items):
                mm(bank, o, l, r, rd, st, sp, tp, first=(i == 0),
                   last=(i == n - 1))

        E("sync", lambda e: e.dma_start(out=ident[:], in_=idf_d), dma="ident",
          writes=["ident"])
        E("gpsimd", lambda e: e.dma_start(out=p64b[:], in_=p64_d), dma="p64b",
          writes=["p64b"])
        for c in range(2):
            E("sync", lambda e, c=c: e.dma_start(out=gamb[:, c:c + 1],
              in_=gam_d[c].unsqueeze(1)), dma="gamb", writes=["gamb"])
            E("sync", lambda e, c=c: e.dma_start(out=betb[:, c:c + 1],
              in_=bet_d[c].unsqueeze(1)), dma="betb", writes=["betb"])
        for b in range(4):
            E("sync", lambda e, b=b: e.dma_start(
                out=sfh[b * 32:(b + 1) * 32, 0:1], in_=fhl_d[:, 0:1],
                allow_slow_non_contiguous=True), dma="sfh", writes=["sfh"])
            E("sync", lambda e, b=b: e.dma_start(
                out=sfl[b * 32:(b + 1) * 32, 0:1], in_=fhl_d[:, 1:2],
                allow_slow_non_contiguous=True), dma="sfl", writes=["sfl"])
        E("gpsimd", lambda e: e.dma_start(
            out=wdkv_s[:].rearrange("p (c o) -> p c o", c=8),
            in_=wdkv_d.rearrange("(c p) o -> p c o", p=128)), dma="wdkv",
          writes=["wdkv_s"])
        E("gpsimd", lambda e: e.dma_start(
            out=wuk_s[:].rearrange("p (c o) -> p c o", c=2),
            in_=wuk_d.rearrange("(c p) o -> p c o", p=128)), dma="wuk",
          writes=["wuk_s"])
        E("gpsimd", lambda e: e.dma_start(
            out=wuv_s[:].rearrange("p (c o) -> p c o", c=2),
            in_=wuv_d.rearrange("(c p) o -> p c o", p=128)), dma="wuv",
          writes=["wuv_s"])

        E("gpsimd", lambda e: e.dma_start(out=onesb[:], in_=cst_d[:, 0:1], allow_slow_non_contiguous=True),
          dma="onesb", writes=["onesb"])
        E("gpsimd", lambda e: e.dma_start(out=onescol[:],
          in_=cst_d[0:65, 0:128], allow_slow_non_contiguous=True), dma="onescol", writes=["onescol"])
        E("sync", lambda e: e.dma_start(out=s127[:],
          in_=cst_d[:, 2113:2114], allow_slow_non_contiguous=True), dma="s127", writes=["s127"])
        E("sync", lambda e: e.dma_start(out=seps[:],
          in_=cst_d[:, 2114:2115], allow_slow_non_contiguous=True), dma="seps", writes=["seps"])
        E("sync", lambda e: e.dma_start(out=sgn[:],
          in_=cst_d[:, 2112:2113], allow_slow_non_contiguous=True), dma="sgn", writes=["sgn"])
        E("vector", lambda e: e.reciprocal(r127[:], s127[:]),
          reads=["s127"], writes=["r127"])
        E("gpsimd", lambda e: e.dma_start(out=vbuf[:],
          in_=cst_d[:, 0:2112], allow_slow_non_contiguous=True), dma="vbufi",
          writes=blocks("vb", 0, 2112, 132))
        E("gpsimd", lambda e: e.dma_start(out=maskP[:], in_=mkp_d),
          dma="maskP", writes=["maskP"])


        PIH = float(np.float32(np.pi))
        PIL = float(np.float32(np.pi - np.float64(PIH)))

        def tv(fn):
            E("vector", fn, reads=TBRES, writes=TBRES)

        def table_chunk(dst, dstres, c0, add_pi2, scale_ap):
            E("sync", lambda e: e.dma_start(out=tb1[:],
              in_=pos_d[:, c0:c0 + 512]), dma="post", writes=TBRES)
            E("vector", lambda e: e.tensor_scalar(tbh[:], tb1[:], sfh[:],
              None, ALU.mult), reads=TBRES + ["sfh"], writes=TBRES)
            E("vector", lambda e: e.tensor_scalar(tbT[:], tb1[:], sfl[:],
              None, ALU.mult), reads=TBRES + ["sfl"], writes=TBRES)
            tv(lambda e: e.tensor_add(tbh[:], tbh[:], tbT[:]))
            if add_pi2:
                tv(lambda e: e.tensor_single_scalar(tbh[:], tbh[:], PI2H,
                   ALU.add))
                tv(lambda e: e.tensor_single_scalar(tbh[:], tbh[:], PI2L,
                   ALU.add))
            tv(lambda e: e.tensor_single_scalar(tbT[:], tbh[:], I2PH,
               ALU.mult))
            tv(lambda e: e.tensor_single_scalar(tb1[:], tbh[:], I2PL,
               ALU.mult))
            tv(lambda e: e.tensor_add(tbT[:], tbT[:], tb1[:]))
            tv(lambda e: e.tensor_single_scalar(tbT[:], tbT[:], 0.5,
               ALU.add))
            tv(lambda e: e.tensor_copy(tbK[:], tbT[:]))
            tv(lambda e: e.tensor_copy(tb1[:], tbK[:]))
            tv(lambda e: e.tensor_tensor(tbT[:], tb1[:], tbT[:], ALU.is_gt))
            tv(lambda e: e.tensor_sub(tb1[:], tb1[:], tbT[:]))
            tv(lambda e: e.tensor_single_scalar(tbT[:], tb1[:], TPH,
               ALU.mult))
            tv(lambda e: e.tensor_sub(tbh[:], tbh[:], tbT[:]))
            tv(lambda e: e.tensor_single_scalar(tbT[:], tb1[:], TPL,
               ALU.mult))
            tv(lambda e: e.tensor_sub(tbh[:], tbh[:], tbT[:]))
            tv(lambda e: e.tensor_scalar_max(tbh[:], tbh[:], -3.1415925))
            tv(lambda e: e.tensor_scalar_min(tbh[:], tbh[:], 3.1415925))
            E("scalar", lambda e: e.activation(dst[:, c0:c0 + 512], tbh[:],
              ACTF.Sin, scale=scale_ap), reads=TBRES + ["sgn"],
              writes=[dstres])

        for q4 in range(4):
            table_chunk(Stab, "Stab", q4 * 512, False, sgn[:])
            table_chunk(Ctab, "Ctab", q4 * 512, True, 1.0)

        def layer_norm(qci, nch, src_fn, dst_fn, nrm):
            items = []
            for dc in range(nch):
                a, rn = src_fn(dc)
                items.append((pb[6][0:1, :], onesb[:, 0:1], a,
                              rn + ["onesb"], dc == 0, dc == nch - 1, None))
            mm_group("pb6", items)
            for dc in range(nch):
                a, rn = src_fn(dc)
                xa = x2scr[:, (dc % 2) * 512:(dc % 2) * 512 + 512]
                x2n = "x2_%d" % (dc % 2)
                E("vector", lambda e, a=a, xa=xa: e.tensor_mul(xa, a, a),
                  reads=rn, writes=[x2n])
                mm("pb7", pb[7][0:1, :], onesb[:, 0:1], xa,
                   [x2n, "onesb"], dc == 0, dc == nch - 1, None,
                   first=(dc == 0), last=(dc == nch - 1))
            rpA = rowpool[0:1, 0:512]
            rpB = rowpool[0:1, 512:1024]
            rpT = rowpool[0:1, 1024:1536]
            sl = 0
            srs = statrow[0:1, 0:512]
            srm = statrow[0:1, 512:1024]
            rs_n, rm_n = "srs%d" % sl, "srm%d" % sl
            inv_n = 1.0 / nrm
            E("scalar", lambda e: e.activation(rpA, pb[6][0:1, :], ACTF.Copy,
              scale=inv_n), reads=["pb6"], writes=["rpA"])
            E("scalar", lambda e: e.activation(rpB, pb[7][0:1, :], ACTF.Copy,
              scale=inv_n), reads=["pb7"], writes=["rpB"])
            E("vector", lambda e: e.tensor_mul(rpT, rpA, rpA),
              reads=["rpA"], writes=["rpT"])
            E("vector", lambda e: e.tensor_sub(rpT, rpB, rpT),
              reads=["rpB", "rpT"], writes=["rpT"])
            E("scalar", lambda e: e.activation(rpB, rpT, ACTF.Ln,
              bias=seps[0:1, 0:1]), reads=["rpT", "seps"], writes=["rpB"])
            E("scalar", lambda e: e.activation(srs, rpB, ACTF.Exp,
              scale=-0.5), reads=["rpB"], writes=[rs_n])
            E("vector", lambda e: e.scalar_tensor_tensor(srm, rpA, -1.0, srs,
              ALU.mult, ALU.mult), reads=["rpA", rs_n], writes=[rm_n])
            mm("pb6", pb[6][:], onescol[0:1, 0:128], srs,
               [rs_n, "onescol"], True, True, (0, 0))
            mm("pb7", pb[7][:], onescol[0:1, 0:128], srm,
               [rm_n, "onescol"], True, True, (0, 0))
            for dc in range(nch):
                a, rn = src_fn(dc)
                d, wn, post = dst_fn(dc)
                E("vector", lambda e, a=a, d=d: e.scalar_tensor_tensor(
                    d, a, 1.0, pb[6][:], ALU.mult, ALU.mult),
                  reads=rn + ["pb6"], writes=wn)
                E("vector", lambda e, d=d: e.tensor_add(d, d, pb[7][:]),
                  reads=["pb7"] + wn, writes=wn)
                if post is not None:
                    oc = post
                    E("vector", lambda e, d=d, oc=oc: e.tensor_scalar(
                        d, d, gamb[:, oc:oc + 1], betb[:, oc:oc + 1],
                        ALU.mult, ALU.add),
                      reads=wn + ["gamb", "betb"], writes=wn)

        def dbg_store(s, ap3, rd=()):
            return E("gpsimd", lambda e, s=s, ap3=ap3: e.dma_start(
                out=out_d[s].rearrange("(c p) t -> p c t", p=128)[
                    :, 0:ap3.shape[1], :],
                in_=ap3), dma="dbg",
                reads=list(rd), after=sc.barrier(), force=True)

        sc.barrier = lambda: [(k, v) for k, v in sc.cnt.items() if v > 0]

        store_toks = []
        for s in range(2):
            if DBG_STAGE == "s0only" and s == 1:
                continue
            Rc = s * 16384
            Ro = (1 - s) * 16384

            E("gpsimd", lambda e, s=s, Rc=Rc: e.dma_start(
                out=R[:, Rc:Rc + 16384].rearrange("p (c t) -> p c t", c=8),
                in_=x_d[s].rearrange("(c p) t -> p c t", p=128)),
              dma="xl", writes=blocks("R", Rc, 16384))

            for qc in range(QC):
                tc = qc * 512

                def src(dc, Rc=Rc, tc=tc):
                    c = Rc + dc * 2048 + tc
                    return (R[:, c:c + 512], blocks("R", c, 512))

                def dst(dc, tc=tc):
                    c = dc * 2048 + tc
                    return (xnA[:, c:c + 512], blocks("xA", c, 512), None)
                layer_norm(qc, DC, src, dst, D)

            if DBG_STAGE == "xn":
                store_toks.append(dbg_store(
                    s, xnA[:].rearrange("p (c t) -> p c t", c=8),
                    rd=blocks("xA", 0, 16384)))
                continue

            for qc in range(QC):
                tc = qc * 512
                for oc in range(LC):
                    items = []
                    for dc in range(DC):
                        c = dc * 2048 + tc
                        items.append((pb[0][:],
                                      wdkv_s[:, dc * 256 + oc * 128:
                                             dc * 256 + oc * 128 + 128],
                                      xnA[:, c:c + 512],
                                      ["wdkv_s"] + blocks("xA", c, 512),
                                      dc == 0, dc == DC - 1, None))
                    mm_group("pb0", items)
                    lc_ = oc * 2048 + tc
                    E("scalar", lambda e, lc_=lc_: e.activation(
                        latn[:, lc_:lc_ + 512], pb[0][:], ACTF.Copy),
                      reads=["pb0"], writes=blocks("ln", lc_, 512))

                def srcl(ocx, tc=tc):
                    c = ocx * 2048 + tc
                    return (latn[:, c:c + 512], blocks("ln", c, 512))

                def dstl(ocx, tc=tc):
                    c = ocx * 2048 + tc
                    return (latn[:, c:c + 512], blocks("ln", c, 512), ocx)
                layer_norm(qc, LC, srcl, dstl, 256)

            if DBG_STAGE == "latn":
                store_toks.append(dbg_store(
                    s, latn[:].rearrange("p (c t) -> p c t", c=2),
                    rd=blocks("ln", 0, 4096)))
                continue

            for hp in range(8):
                hs = hp % 2
                qof = hs * 2048
                wqn = "wqs%d" % hs
                E("gpsimd", lambda e, hp=hp, hs=hs: e.dma_start(
                    out=wqs[:, hs * 1024:hs * 1024 + 1024].rearrange(
                        "p (c o) -> p c o", c=8),
                    in_=wq_d.rearrange("(c p) o -> p c o", p=128)[
                        :, :, hp * 128:hp * 128 + 128]),
                  dma="wq%d" % hs, writes=[wqn])

                def rope(dst_ap, dst_res, tc, qp):
                    qsl = q_sb[:, qp * 512:qp * 512 + 512]
                    t2 = t2sb[:, 0:512]
                    t1 = t1sb[:, qp * 512:qp * 512 + 512]
                    qn, t1n, t2n = "qsb%d" % qp, "t1_%d" % qp, "t2_0"
                    E("scalar", lambda e: e.activation(qsl, pb[0][:],
                      ACTF.Copy), reads=["pb0"], writes=[qn])
                    mm("pb1", pb[1][:], p64b[:], qsl, ["p64b", qn],
                       True, True, None)
                    E("vector", lambda e: e.tensor_mul(t2, pb[1][:],
                      Stab[:, tc:tc + 512]), reads=["pb1", "Stab"],
                      writes=[t2n])
                    E("vector", lambda e: e.tensor_mul(t1, qsl,
                      Ctab[:, tc:tc + 512]), reads=[qn, "Ctab"],
                      writes=[t1n])
                    E("vector", lambda e: e.tensor_add(dst_ap, t1, t2),
                      reads=[t1n, t2n], writes=dst_res)

                for qc in range(QC):
                    tc = qc * 512
                    items = []
                    for dc in range(DC):
                        c = dc * 2048 + tc
                        items.append((pb[0][:],
                                      wqs[:, hs * 1024 + dc * 128:
                                          hs * 1024 + dc * 128 + 128],
                                      xnA[:, c:c + 512],
                                      [wqn] + blocks("xA", c, 512),
                                      dc == 0, dc == DC - 1, None))
                    mm_group("pb0", items)
                    rope(qt[:, qof + tc:qof + tc + 512],
                         blocks("qt", qof + tc, 512), tc, qc % 2)
                for qc in range(QC):
                    tc = qc * 512
                    items = []
                    for lc in range(LC):
                        c = lc * 2048 + tc
                        items.append((pb[0][:],
                                      wuk_s[:, lc * 1024 + hp * 128:
                                            lc * 1024 + hp * 128 + 128],
                                      latn[:, c:c + 512],
                                      ["wuk_s"] + blocks("ln", c, 512),
                                      lc == 0, lc == LC - 1, None))
                    mm_group("pb0", items)
                    rope(ktb[:, qof + tc:qof + tc + 512],
                         blocks("kk", qof + tc, 512), tc, qc % 2)
                for kti in range(KT):
                    items = []
                    for lc in range(LC):
                        c = lc * 2048 + kti * 128
                        items.append((pb[2][:, 0:128],
                                      latn[:, c:c + 128],
                                      wuv_s[:, lc * 1024 + hp * 128:
                                            lc * 1024 + hp * 128 + 128],
                                      ["wuv_s"] + blocks("ln", c, 128),
                                      lc == 0, lc == LC - 1, None))
                    mm_group("pb2", items)
                    E("vector", lambda e, kti=kti: e.tensor_copy(
                        vbuf[:, kti * 132:kti * 132 + 132].rearrange(
                            "p (h x) -> p h x", h=2)[:, :, 0:64],
                        pb[2][:, 0:128].rearrange("p (h x) -> p h x", h=2)),
                      reads=["pb2"], writes=["vb%d" % kti])

                if DBG_STAGE == "qkv" and hp == 0:
                    store_toks.append(dbg_store(
                        s, ktb[:, 0:2048].rearrange("p (c t) -> p c t", c=1),
                        rd=blocks("kk", 0, 2048)))
                    store_toks.append(E("gpsimd", lambda e, s=s: e.dma_start(
                        out=out_d[s].rearrange("(c p) t -> p c t", p=128)[
                            :, 1:2, :],
                        in_=qt[:, 0:2048].rearrange(
                            "p (c t) -> p c t", c=1)), dma="dbg",
                        force=True,
                        reads=blocks("qt", 0, 2048), after=sc.barrier()))
                    store_toks.append(E("gpsimd", lambda e, s=s: e.dma_start(
                        out=out_d[s].rearrange("(c p) t -> p c t", p=128)[
                            :, 2:3, 0:2048],
                        in_=vbuf[:, 0:2048].rearrange(
                            "p (c t) -> p c t", c=1)), dma="dbg",
                        force=True,
                        reads=blocks("vb", 0, 2048, 132),
                        after=sc.barrier()))
                    break

                for p1rep in range(2):
                  for head in range(2):
                      r0 = head * 64
                      for qtile in range(KT):
                          qcol = qof + qtile * 128
                          qres = blocks("qt", qcol, 128)
                          nkc = (qtile * 128 + 128 + 511) // 512
                          for kc in range(nkc):
                              N = min(512, qtile * 128 + 128 - kc * 512)
                              b = head + 2 * (kc % 2)
                              mm(PB[b], pb[b][:, 0:N],
                                 qt[r0:r0 + 64, qcol:qcol + 128],
                                 ktb[r0:r0 + 64, qof + kc * 512:
                                     qof + kc * 512 + N],
                                 qres + blocks("kk", qof + kc * 512, N),
                                 True, True, (r0, 0))
                          for kc in range(nkc):
                              N = min(512, qtile * 128 + 128 - kc * 512)
                              b = head + 2 * (kc % 2)
                              if kc == 0:
                                  E("vector", lambda e, b=b, N=N, head=head,
                                    qtile=qtile: e.reduce_max(
                                      mbuf[:, head * 16 + qtile:
                                           head * 16 + qtile + 1],
                                      pb[b][:, 0:N], axis=AX),
                                    reads=[PB[b]], writes=["mbuf"])
                              else:
                                  E("vector", lambda e, b=b, N=N, head=head:
                                    e.reduce_max(mtmp[:, head:head + 1],
                                                 pb[b][:, 0:N], axis=AX),
                                    reads=[PB[b]], writes=["mtmp"])
                                  E("vector", lambda e, head=head, qtile=qtile:
                                    e.tensor_max(
                                      mbuf[:, head * 16 + qtile:
                                           head * 16 + qtile + 1],
                                      mbuf[:, head * 16 + qtile:
                                           head * 16 + qtile + 1],
                                      mtmp[:, head:head + 1]),
                                    reads=["mtmp", "mbuf"], writes=["mbuf"])
                                  if (DBG_STAGE == "mdump" and hp == 0
                                          and head == 0 and qtile == 4
                                          and s == 0):
                                      E("vector", lambda e: e.reduce_max(
                                          dbgc[:, 0:1], pb[0][:, 0:512],
                                          axis=AX), reads=["pb0"],
                                        writes=["dbgc"])
                                      E("vector", lambda e: e.reduce_max(
                                          dbgc[:, 1:2], pb[2][:, 0:128],
                                          axis=AX), reads=["pb2"],
                                        writes=["dbgc"])
                      E("vector", lambda e, head=head: e.tensor_scalar_mul(
                          mbuf[:, head * 16:head * 16 + 16],
                          mbuf[:, head * 16:head * 16 + 16], -1.0),
                        reads=["mbuf"], writes=["mbuf"])
                      E("vector", lambda e, head=head: e.tensor_scalar_max(
                          mbuf[:, head * 16:head * 16 + 16],
                          mbuf[:, head * 16:head * 16 + 16], -64.0),
                        reads=["mbuf"], writes=["mbuf"])
                      E("vector", lambda e, head=head: e.tensor_scalar_min(
                          mbuf[:, head * 16:head * 16 + 16],
                          mbuf[:, head * 16:head * 16 + 16], 64.0),
                        reads=["mbuf"], writes=["mbuf"])
                      E("tensor", lambda e, head=head: e.transpose(
                          pb[2][0:16, 0:128], mbuf[:, head * 16:head * 16 + 16],
                          ident[:]), reads=["mbuf", "ident"], writes=["pb2"],
                        after=sc.bank_deps("pb2"))
                      E("vector", lambda e, head=head: e.tensor_copy(
                          smtt[:, head * 128:head * 128 + 128],
                          pb[2][0:16, 0:128]), reads=["pb2"],
                        writes=["smtt%d" % head])
                      mr = head * 32
                      E("sync", lambda e, head=head, mr=mr, hs=hs: e.dma_start(
                          out=mrow[mr:mr + 1, 0:2048].rearrange(
                              "p (t q) -> p t q", t=16),
                          in_=smtt[:, head * 128:head * 128 + 128]),
                        dma="mr%d%d" % (head, hs),
                        reads=["smtt%d" % head],
                        writes=["mrow%d_%d" % (head, hs)])

                  if DBG_STAGE == "mdump" and hp == 0:
                      store_toks.append(E("gpsimd", lambda e, s=s: e.dma_start(
                          out=out_d[s].rearrange("(c p) t -> p c t", p=128)[
                              :, 0:1, 32:34],
                          in_=mtmp[:].rearrange("p (c t) -> p c t", c=1)),
                          dma="dbg", force=True, reads=["mtmp"],
                          after=sc.barrier()))
                      store_toks.append(E("gpsimd", lambda e, s=s: e.dma_start(
                          out=out_d[s].rearrange("(c p) t -> p c t", p=128)[
                              :, 0:1, 40:44],
                          in_=dbgc[:].rearrange("p (c t) -> p c t", c=1)),
                          dma="dbg", force=True, reads=["dbgc"],
                          after=sc.barrier()))
                      store_toks.append(E("gpsimd", lambda e, s=s: e.dma_start(
                          out=out_d[s].rearrange("(c p) t -> p c t", p=128)[
                              :, 2:3, 0:2048],
                          in_=qt[:, 0:2048].rearrange(
                              "p (c t) -> p c t", c=1)),
                          dma="dbg", force=True, reads=blocks("qt", 0, 2048),
                          after=sc.barrier()))
                      store_toks.append(E("gpsimd", lambda e, s=s: e.dma_start(
                          out=out_d[s].rearrange("(c p) t -> p c t", p=128)[
                              :, 3:4, 0:2048],
                          in_=ktb[:, 0:2048].rearrange(
                              "p (c t) -> p c t", c=1)),
                          dma="dbg", force=True, reads=blocks("kk", 0, 2048),
                          after=sc.barrier()))
                      store_toks.append(E("gpsimd", lambda e, s=s: e.dma_start(
                          out=out_d[s].rearrange("(c p) t -> p c t", p=128)[
                              :, 0:1, 0:32],
                          in_=mbuf[:].rearrange("p (c t) -> p c t", c=1)),
                          dma="dbg", force=True, reads=["mbuf"],
                          after=sc.barrier()))
                      store_toks.append(E("gpsimd", lambda e, s=s: e.dma_start(
                          out=out_d[s].rearrange("(c p) t -> p c t", p=128)[
                              0:33, 1:2, 0:2048],
                          in_=mrow[:, 0:2048].rearrange(
                              "p (c t) -> p c t", c=1)),
                          dma="dbg", force=True,
                          reads=["mrow0_0", "mrow1_0"], after=sc.barrier()))
                      break

                for qc in range(QC):
                    tc = qc * 512
                    lastkt = min(KT - 1, 4 * qc + 3)

                    def st_pair(kti, tc=tc, qof=qof, hs=hs):
                        par = kti % 2
                        off = max(0, kti * 128 - tc)
                        for head in range(2):
                            b = par * 2 + head
                            r0 = head * 64
                            kcol = qof + kti * 128
                            mm(PB[b], pb[b][:, off:512],
                               ktb[r0:r0 + 64, kcol:kcol + 128],
                               qt[r0:r0 + 64, qof + tc + off:qof + tc + 512],
                               blocks("kk", kcol, 128) +
                               blocks("qt", qof + tc + off, 512 - off),
                               True, False, (r0, 0), first=True, last=False)
                            mr = head * 32
                            mm(PB[b], pb[b][:, off:512],
                               onescol[mr:mr + 1, 0:128],
                               mrow[mr:mr + 1, tc + off:tc + 512],
                               ["onescol", "mrow%d_%d" % (head, hs)],
                               False, True, (mr, 0), first=False, last=True)

                    def exp_mask(kti, tc=tc):
                        par = kti % 2
                        off = max(0, kti * 128 - tc)
                        for head in range(2):
                            b = par * 2 + head
                            sl = head * 2 + par
                            ps = pbuf[:, sl * 512:sl * 512 + 512]
                            pres = "pbuf%d" % sl
                            E("scalar", lambda e, b=b, off=off, ps=ps:
                              e.activation(ps[:, off:512], pb[b][:, off:512],
                                           ACTF.Exp, scale=0.125),
                              reads=[PB[b]], writes=[pres])
                            if kti * 128 >= tc:
                                E("vector", lambda e, ps=ps, off=off:
                                  e.tensor_mul(ps[:, off:off + 128],
                                               ps[:, off:off + 128],
                                               maskP[:]),
                                  reads=[pres, "maskP"], writes=[pres])

                    def av(kti, tc=tc, lastkt=lastkt):
                        par = kti % 2
                        off = max(0, kti * 128 - tc)
                        for head in range(2):
                            bav = 4 + head
                            sl = head * 2 + par
                            ps = pbuf[:, sl * 512:sl * 512 + 512]
                            pres = "pbuf%d" % sl
                            mm(PB[bav], pb[bav][0:65, off:512],
                               vbuf[:, kti * 132 + head * 66:
                                    kti * 132 + head * 66 + 65],
                               ps[:, off:512], ["vb%d" % kti, pres],
                               kti == 0, kti == lastkt, None,
                               first=(kti == 0), last=(kti == lastkt))

                    st_pair(0)
                    for kti in range(lastkt + 1):
                        if kti + 1 <= lastkt:
                            st_pair(kti + 1)
                        exp_mask(kti)
                        av(kti)

                    for head in range(2):
                        bav = 4 + head
                        bb = head
                        rr = reciprow[64:65, head * 512:head * 512 + 512]
                        rres = "recip%d" % head
                        E("scalar", lambda e, bav=bav: e.activation(
                            pb[bav][64:65, :], pb[bav][64:65, :], ACTF.Ln),
                          reads=[PB[bav]], writes=[PB[bav]])
                        E("scalar", lambda e, bav=bav, rr=rr: e.activation(
                            rr, pb[bav][64:65, :], ACTF.Exp, scale=-1.0),
                          reads=[PB[bav]], writes=[rres])
                        mm(PB[bb], pb[bb][0:64, :], onescol[64:65, 0:64], rr,
                           [rres, "onescol"], True, True, (64, 0))
                        avs = av_sb[:, head * 512:head * 512 + 512]
                        avn = "avsb%d" % head
                        E("scalar", lambda e, bav=bav, avs=avs: e.activation(
                            avs, pb[bav][0:64, :], ACTF.Copy),
                          reads=[PB[bav]], writes=[avn])
                        ac = Ro + hp * 2048 + tc
                        if head == 0:
                            E("vector", lambda e, avs=avs, bb=bb, ac=ac:
                              e.scalar_tensor_tensor(
                                R[0:64, ac:ac + 512], avs, 1.0,
                                pb[bb][0:64, :], ALU.mult, ALU.mult),
                              reads=[avn, PB[bb]], writes=blocks("R", ac, 512))
                        else:
                            E("vector", lambda e, avs=avs, bb=bb, tc=tc:
                              e.scalar_tensor_tensor(
                                bstg[:, tc:tc + 512], avs, 1.0,
                                pb[bb][0:64, :], ALU.mult, ALU.mult),
                              reads=[avn, PB[bb]], writes=["bstg"])
                if True:
                    ac0 = Ro + hp * 2048
                    E("sync", lambda e, ac0=ac0: e.dma_start(
                        out=R[64:128, ac0:ac0 + 2048], in_=bstg[:]),
                      dma="bs", reads=["bstg"],
                      writes=blocks("R", ac0, 2048))

            if DBG_STAGE == "qkv":
                continue
            if DBG_STAGE == "att":
                store_toks.append(dbg_store(
                    s, R[:, Ro:Ro + 16384].rearrange("p (c t) -> p c t",
                                                     c=8),
                    rd=blocks("R", Ro, 16384)))
                continue

            for oc in range(8):
                osl = oc % 2
                won = "wos%d" % osl
                E("gpsimd", lambda e, oc=oc, osl=osl: e.dma_start(
                    out=wos[:, osl * 1024:osl * 1024 + 1024].rearrange(
                        "p (c o) -> p c o", c=8),
                    in_=wo_d.rearrange("(c p) o -> p c o", p=128)[
                        :, :, oc * 128:oc * 128 + 128]),
                  dma="wo%d" % osl, writes=[won])
                for qc in range(QC):
                    tc = qc * 512
                    b = qc % 2
                    items = []
                    for dc in range(DC):
                        ac = Ro + dc * 2048 + tc
                        items.append((pb[b][:],
                                      wos[:, osl * 1024 + dc * 128:
                                          osl * 1024 + dc * 128 + 128],
                                      R[:, ac:ac + 512],
                                      [won] + blocks("R", ac, 512),
                                      dc == 0, dc == DC - 1, None))
                    mm_group(PB[b], items)
                    rc = Rc + oc * 2048 + tc
                    E("vector", lambda e, b=b, rc=rc: e.tensor_add(
                        R[:, rc:rc + 512], R[:, rc:rc + 512], pb[b][:]),
                      reads=[PB[b]] + blocks("R", rc, 512),
                      writes=blocks("R", rc, 512))

            if DBG_STAGE == "wo":
                store_toks.append(dbg_store(
                    s, R[:, Rc:Rc + 16384].rearrange("p (c t) -> p c t",
                                                     c=8),
                    rd=blocks("R", Rc, 16384)))
                continue

            def hloc(hc, t0):
                if hc < 16:
                    c = Ro + hc * 1024 + t0
                    return (R[:, c:c + 512], blocks("R", c, 512))
                c = 8192 + (hc - 16) * 1024 + t0
                return (xnA[:, c:c + 512], blocks("xA", c, 512))

            for st_i in range(2):
                stc = st_i * 1024
                for qc2 in range(2):
                    loc = qc2 * 512

                    def src2(dc, Rc=Rc, stc=stc, loc=loc):
                        c = Rc + dc * 2048 + stc + loc
                        return (R[:, c:c + 512], blocks("R", c, 512))

                    def dst2(dc, loc=loc):
                        c = dc * 1024 + loc
                        return (xnA[:, c:c + 512], blocks("xA", c, 512),
                                None)
                    layer_norm(qc2, DC, src2, dst2, D)

                if DBG_STAGE == "ffnln" and st_i == 0:
                    store_toks.append(dbg_store(
                        s, xnA[:, 0:8192].rearrange("p (c t) -> p c t", c=4),
                        rd=blocks("xA", 0, 8192)))
                    break
                for hc in range(HC):
                    hsl = hc % 2
                    wgn, wun = "wgs%d" % hsl, "wus%d" % hsl
                    E("gpsimd", lambda e, hc=hc, hsl=hsl: e.dma_start(
                        out=wgs[:, hsl * 1024:hsl * 1024 + 1024].rearrange(
                            "p (c o) -> p c o", c=8),
                        in_=wgt_d.rearrange("(c p) h -> p c h", p=128)[
                            :, :, hc * 128:hc * 128 + 128]),
                      dma="wg%d" % hsl, writes=[wgn])
                    E("gpsimd", lambda e, hc=hc, hsl=hsl: e.dma_start(
                        out=wus[:, hsl * 1024:hsl * 1024 + 1024].rearrange(
                            "p (c o) -> p c o", c=8),
                        in_=wut_d.rearrange("(c p) h -> p c h", p=128)[
                            :, :, hc * 128:hc * 128 + 128]),
                      dma="wu%d" % hsl, writes=[wun])
                    for tt in range(2):
                        t0 = tt * 512
                        bg, bu = tt * 2, tt * 2 + 1
                        for wname, bx in ((wgn, bg), (wun, bu)):
                            wt = wgs if wname.startswith("wg") else wus
                            items = []
                            for dc in range(DC):
                                c = dc * 1024 + t0
                                items.append((pb[bx][:],
                                              wt[:, hsl * 1024 + dc * 128:
                                                 hsl * 1024 + dc * 128 + 128],
                                              xnA[:, c:c + 512],
                                              [wname] + blocks("xA", c, 512),
                                              dc == 0, dc == DC - 1, None))
                            mm_group(PB[bx], items)
                        gs = q_sb[:, tt * 512:tt * 512 + 512]
                        gn = "qsb%d" % tt
                        E("scalar", lambda e, bg=bg, gs=gs: e.activation(
                            gs, pb[bg][:], ACTF.Silu, scale=r127[:]),
                          reads=[PB[bg], "r127"], writes=[gn])
                        ha, hres = hloc(hc, t0)
                        E("vector", lambda e, bu=bu, gs=gs, ha=ha:
                          e.scalar_tensor_tensor(ha, pb[bu][:], r127[:], gs,
                                                 ALU.mult, ALU.mult),
                          reads=[PB[bu], "r127", gn], writes=hres)
                if DBG_STAGE == "ffnh" and st_i == 0:
                    store_toks.append(dbg_store(
                        s, R[:, Ro:Ro + 16384].rearrange(
                            "p (c t) -> p c t", c=8),
                        rd=blocks("R", Ro, 16384)))
                    break
                if DBG_STAGE == "ffnh2" and st_i == 0:
                    store_toks.append(dbg_store(
                        s, xnA[:].rearrange("p (c t) -> p c t", c=8),
                        rd=blocks("xA", 0, 16384)))
                    break
                for dc in range(DC):
                    dsl = 0
                    wdn = "wds0"
                    E("gpsimd", lambda e, dc=dc, dsl=dsl: e.dma_start(
                        out=wds[:, 0:3072].rearrange(
                            "p (c o) -> p c o", c=24),
                        in_=wdt_d.rearrange("(c p) o -> p c o", p=128)[
                            :, :, dc * 128:dc * 128 + 128]),
                      dma="wd%d" % dsl, writes=[wdn])
                    for tt in range(2):
                        t0 = tt * 512
                        b = 4 + tt
                        items = []
                        for hc in range(HC):
                            ha, hres = hloc(hc, t0)
                            items.append((pb[b][:],
                                          wds[:, dsl * 3072 + hc * 128:
                                              dsl * 3072 + hc * 128 + 128],
                                          ha, [wdn] + hres,
                                          hc == 0, hc == HC - 1, None))
                        mm_group(PB[b], items)
                        rc = Rc + dc * 2048 + stc + t0
                        E("vector", lambda e, b=b, rc=rc:
                          e.scalar_tensor_tensor(
                            R[:, rc:rc + 512], pb[b][:], r127[:],
                            R[:, rc:rc + 512], ALU.mult, ALU.add),
                          reads=[PB[b], "r127"] + blocks("R", rc, 512),
                          writes=blocks("R", rc, 512))

            if DBG_STAGE not in ("ffnln", "ffnh", "ffnh2", "mdump"):
                tok = E("gpsimd", lambda e, s=s, Rc=Rc: e.dma_start(
                    out=out_d[s].rearrange("(c p) t -> p c t", p=128),
                    in_=R[:, Rc:Rc + 16384].rearrange("p (c t) -> p c t",
                                                      c=8)),
                  dma="st", reads=blocks("R", Rc, 16384))
                store_toks.append(tok)

        E("sync", lambda e: e.nop(), after=store_toks, tick=False)

        blk.tensor(lambda e: [f(e) for f in sc.q["tensor"]])
        blk.vector(lambda e: [f(e) for f in sc.q["vector"]])
        blk.scalar(lambda e: [f(e) for f in sc.q["scalar"]])
        blk.gpsimd(lambda e: [f(e) for f in sc.q["gpsimd"]])
        blk.sync(lambda e: [f(e) for f in sc.q["sync"]])

    nc._sched = sc
    import sys
    print("kernel build: instrs",
          {k: len(v) for k, v in sc.q.items()}, "waits", sc.nwait,
          file=sys.stderr)
    return nc


_NC = None


def _get_nc():
    global _NC
    if _NC is None:
        _NC = build_nc()
    return _NC


def kernel(L0_attn_x, L0_attn_W_q, L0_attn_W_dkv, L0_attn_W_uk, L0_attn_W_uv,
           L0_attn_W_o, L0_attn_ln_gamma, L0_attn_ln_beta, L0_ffn_w_gate,
           L0_ffn_w_up, L0_ffn_w_down):
    x = np.asarray(L0_attn_x, dtype=np.float32)
    xT = np.ascontiguousarray(np.transpose(x, (0, 2, 1)))
    wq = np.ascontiguousarray(np.asarray(L0_attn_W_q, np.float32))
    wdkv = np.ascontiguousarray(np.asarray(L0_attn_W_dkv, np.float32))
    wuk = np.ascontiguousarray(np.asarray(L0_attn_W_uk, np.float32))
    wuv = np.ascontiguousarray(np.asarray(L0_attn_W_uv, np.float32))
    wo = np.ascontiguousarray(np.asarray(L0_attn_W_o, np.float32))
    gam = np.ascontiguousarray(
        np.asarray(L0_attn_ln_gamma, np.float32).reshape(2, 128))
    bet = np.ascontiguousarray(
        np.asarray(L0_attn_ln_beta, np.float32).reshape(2, 128))
    wgt = np.ascontiguousarray(np.asarray(L0_ffn_w_gate, np.int32).T)
    wut = np.ascontiguousarray(np.asarray(L0_ffn_w_up, np.int32).T)
    wdt = np.ascontiguousarray(np.asarray(L0_ffn_w_down, np.int32).T)
    inv = 1.0 / (10000.0 ** (np.arange(32, dtype=np.float64) / 32.0))
    fh = inv.astype(np.float32)
    fl = (inv - fh.astype(np.float64)).astype(np.float32)
    fhl = np.ascontiguousarray(np.stack([fh, fl], axis=1))
    idf = np.eye(128, dtype=np.float32)
    p64 = np.zeros((128, 128), dtype=np.float32)
    p64[np.arange(128), np.arange(128) ^ 32] = 1.0
    cst = np.ones((128, 2118), dtype=np.float32)
    sgnv = np.ones(128, np.float32)
    sgnv[0:32] = -1.0; sgnv[32:64] = 1.0; sgnv[64:96] = -1.0; sgnv[96:128] = 1.0
    cst[:, 2112] = sgnv
    cst[:, 2113] = 127.0
    cst[:, 2114] = 1e-5
    mkp = (np.arange(128)[None, :] >= np.arange(128)[:, None]).astype(np.float32)
    post = np.broadcast_to(np.arange(2048, dtype=np.float32)[None, :],
                           (128, 2048)).copy()

    nc = _get_nc()
    in_maps = []
    for c in range(8):
        in_maps.append(dict(
            x=np.ascontiguousarray(xT[2 * c:2 * c + 2]),
            wq=wq, wdkv=wdkv, wuk=wuk, wuv=wuv, wo=wo, gam=gam, bet=bet,
            wgt=wgt, wut=wut, wdt=wdt, fhl=fhl, idf=idf, p64=p64,
            cst=cst, mkp=mkp, post=post))
    res = run_bass_kernel_spmd(nc, in_maps, core_ids=list(range(8)))
    outs = []
    for c in range(8):
        o = np.asarray(res.results[c]["out"])
        outs.append(np.transpose(o, (0, 2, 1)))
    return np.ascontiguousarray(np.concatenate(outs, axis=0))


if __name__ == "__main__":
    build_nc()
    print("build ok")


# version 11
# speedup vs baseline: 1.8449x; 1.5495x over previous
import numpy as np
from contextlib import ExitStack

import concourse.bass as bass
import concourse.mybir as mybir
from concourse.bass_utils import run_bass_kernel_spmd

F32 = mybir.dt.float32
BF16 = mybir.dt.bfloat16
I32 = mybir.dt.int32
AX = mybir.AxisListType.X
ALU = mybir.AluOpType
ACTF = mybir.ActivationFunctionType


def _sp(v):
    hi = float(np.float32(v))
    lo = float(np.float32(v - np.float64(hi)))
    return hi, lo


PI2H, PI2L = _sp(np.pi / 2)
TPH, TPL = _sp(2 * np.pi)
I2PH, I2PL = _sp(1.0 / (2 * np.pi))

D = 1024
DBG_STAGE = None
SENTINEL = False
FORCE_WAITS = False
DC = 8
LC = 2
HC = 24
QC = 4
KT = 16

ENG_SEM = {"tensor": "T", "vector": "V", "scalar": "S", "gpsimd": "G",
           "sync": "Y"}


class Sched:
    def __init__(self):
        self.q = {e: [] for e in ENG_SEM}
        self.new_sem = None
        self.cnt = {s: 0 for s in ("T", "V", "S", "G", "Y")}
        self.eclk = {e: {} for e in ENG_SEM}
        self.tokclk = {}
        self.res = {}
        self.semh = None
        self.nwait = 0

    def r_(self, name):
        if name not in self.res:
            self.res[name] = {"w": [], "r": []}
        return self.res[name]

    def emit(self, eng, fn, reads=(), writes=(), after=(), dma=False,
             tick=True, force=False):
        waits = []
        for r in reads:
            waits += self.r_(r)["w"]
        for w in writes:
            waits += self.r_(w)["w"] + self.r_(w)["r"]
        waits += list(after)
        my = ENG_SEM[eng]
        if dma:
            assert isinstance(dma, str), "dma must be a stream name"
            sem = "d_" + dma
            if sem not in self.cnt:
                self.cnt[sem] = 0
                self.semh[sem] = self.new_sem(sem)
            inc = 16
        else:
            sem = my
            inc = 1
        clk = self.eclk[eng]
        clk[my] = self.cnt[my]
        final = []
        for s, v in sorted(set(waits)):
            if s == my and not dma:
                continue
            if clk.get(s, 0) >= v and not (force or FORCE_WAITS):
                continue
            final.append((s, v))
            tc = self.tokclk.get((s, v))
            if tc:
                for s2, v2 in tc.items():
                    if clk.get(s2, 0) < v2:
                        clk[s2] = v2
            else:
                clk[s] = max(clk.get(s, 0), v)
        if not FORCE_WAITS:
            final = [(s, v) for (s, v) in final
                     if not (s != my and self.eclk[eng].get(s, -1) > v)]
        else:
            best = {}
            for s, v in final:
                best[s] = max(best.get(s, 0), v)
            final = sorted(best.items())
        self.nwait += len(final)
        tok = None
        if tick:
            self.cnt[sem] += inc
            tok = (sem, self.cnt[sem])
            clk2 = dict(clk)
            clk2[sem] = self.cnt[sem]
            self.tokclk[tok] = clk2
        self.eclk[eng] = clk

        def run(e, _w=tuple(final), _fn=fn, _tok=tok, _sem=sem, _inc=inc):
            for s, v in _w:
                e.wait_ge(self.semh[s], v)
            ins = _fn(e)
            if _tok is not None:
                ins.then_inc(self.semh[_sem], _inc)
        run._waits = tuple(final)
        run._tok = tok
        run._writes = tuple(writes)
        run._reads = tuple(reads)
        self.q[eng].append(run)

        if tok is not None:
            for w in writes:
                self.res[w] = {"w": [tok], "r": []}
            for r in reads:
                self.r_(r)["r"].append(tok)
        return tok

    def bank_deps(self, bank):
        return tuple(self.r_(bank)["w"] + self.r_(bank)["r"])


def blocks(base, col, n, gran=512):
    return ["%s%d" % (base, i) for i in range(col // gran,
                                              (col + n + gran - 1) // gran)]


def build_nc():
    nc = bass.Bass("TRN2", target_bir_lowering=False,
                   detect_race_conditions=False)
    dt = nc.dram_tensor
    x_d = dt("x", [2, D, 2048], F32, kind="ExternalInput").ap()
    wq_d = dt("wq", [D, D], F32, kind="ExternalInput").ap()
    wdkv_d = dt("wdkv", [D, 256], F32, kind="ExternalInput").ap()
    wuk_d = dt("wuk", [256, D], F32, kind="ExternalInput").ap()
    wuv_d = dt("wuv", [256, D], F32, kind="ExternalInput").ap()
    wo_d = dt("wo", [D, D], F32, kind="ExternalInput").ap()
    gam_d = dt("gam", [2, 128], F32, kind="ExternalInput").ap()
    bet_d = dt("bet", [2, 128], F32, kind="ExternalInput").ap()
    wgt_d = dt("wgt", [D, 3072], I32, kind="ExternalInput").ap()
    wut_d = dt("wut", [D, 3072], I32, kind="ExternalInput").ap()
    wdt_d = dt("wdt", [3072, D], I32, kind="ExternalInput").ap()
    fhl_d = dt("fhl", [32, 2], F32, kind="ExternalInput").ap()
    idf_d = dt("idf", [128, 128], F32, kind="ExternalInput").ap()
    p64_d = dt("p64", [128, 128], F32, kind="ExternalInput").ap()
    cst_d = dt("cst", [128, 2118], F32, kind="ExternalInput").ap()
    mkp_d = dt("mkp", [128, 128], F32, kind="ExternalInput").ap()
    pos_d = dt("post", [128, 2048], F32, kind="ExternalInput").ap()
    out_d = dt("out", [2, D, 2048], F32, kind="ExternalOutput").ap()

    sc = Sched()
    E = sc.emit

    with ExitStack() as stk:
        en = stk.enter_context
        sems = {}
        for s in ("T", "V", "S", "G", "Y"):
            sems[s] = en(nc.semaphore("sem_" + s))
        sc.semh = sems
        sc.new_sem = lambda name: en(nc.semaphore("sem_" + name))

        ident = en(nc.sbuf_tensor("ident", [128, 128], F32))
        p64b = en(nc.sbuf_tensor("p64b", [128, 128], BF16))
        maskP = en(nc.sbuf_tensor("maskP", [128, 128], BF16))
        onesb = en(nc.sbuf_tensor("onesb", [128, 1], BF16))
        onescol = en(nc.sbuf_tensor("onescol", [65, 128], BF16))
        sgn = en(nc.sbuf_tensor("sgn", [128, 1], F32))
        sfh = en(nc.sbuf_tensor("sfh", [128, 1], F32))
        sfl = en(nc.sbuf_tensor("sfl", [128, 1], F32))
        s127 = en(nc.sbuf_tensor("s127", [128, 1], F32))
        seps = en(nc.sbuf_tensor("seps", [128, 1], F32))
        r127 = en(nc.sbuf_tensor("r127", [128, 1], F32))
        gamb = en(nc.sbuf_tensor("gamb", [128, 2], F32))
        betb = en(nc.sbuf_tensor("betb", [128, 2], F32))
        rowpool = en(nc.sbuf_tensor("rowpool", [1, 1536], BF16))
        statrow = en(nc.sbuf_tensor("statrow", [1, 1024], BF16))
        reciprow = en(nc.sbuf_tensor("reciprow", [65, 1024], BF16))
        bstg = en(nc.sbuf_tensor("bstg", [64, 2048], BF16))
        smtt = en(nc.sbuf_tensor("smtt", [16, 256], BF16))
        mbuf = en(nc.sbuf_tensor("mbuf", [128, 32], F32))
        dbgc = en(nc.sbuf_tensor("dbgc", [128, 4], F32))
        mtmp = en(nc.sbuf_tensor("mtmp", [128, 2], F32))
        mrow = en(nc.sbuf_tensor("mrow", [33, 2048], BF16))
        x2scr = en(nc.sbuf_tensor("x2scr", [128, 1024], BF16))
        q_sb = en(nc.sbuf_tensor("q_sb", [128, 1024], BF16))
        t2sb = en(nc.sbuf_tensor("t2sb", [128, 512], F32))
        t1sb = en(nc.sbuf_tensor("t1sb", [128, 1024], BF16))
        av_sb = en(nc.sbuf_tensor("av_sb", [64, 1024], BF16))
        Ctab = en(nc.sbuf_tensor("Ctab", [128, 2048], BF16))
        Stab = en(nc.sbuf_tensor("Stab", [128, 2048], BF16))
        wdkv_s = en(nc.sbuf_tensor("wdkv_s", [128, 2048], BF16))
        wuk_s = en(nc.sbuf_tensor("wuk_s", [128, 2048], BF16))
        wuv_s = en(nc.sbuf_tensor("wuv_s", [128, 2048], BF16))
        wqs = en(nc.sbuf_tensor("wqs", [128, 2048], BF16))
        wos = en(nc.sbuf_tensor("wos", [128, 2048], BF16))
        wgs = en(nc.sbuf_tensor("wgs", [128, 2048], BF16))
        wus = en(nc.sbuf_tensor("wus", [128, 2048], BF16))
        wds = en(nc.sbuf_tensor("wds", [128, 3072], BF16))
        latn = en(nc.sbuf_tensor("latn", [128, 4096], BF16))
        qt = en(nc.sbuf_tensor("qt", [128, 4096], BF16))
        ktb = en(nc.sbuf_tensor("ktb", [128, 4096], BF16))
        vbuf = en(nc.sbuf_tensor("vbuf", [128, 2112], BF16))
        pbuf = en(nc.sbuf_tensor("pbuf", [128, 2048], BF16))
        tb1 = en(nc.sbuf_tensor("tb1", [128, 512], F32))
        tbh = en(nc.sbuf_tensor("tbh", [128, 512], F32))
        tbT = en(nc.sbuf_tensor("tbT", [128, 512], F32))
        tbK = en(nc.sbuf_tensor("tbK", [128, 512], I32))
        R = en(nc.sbuf_tensor("R", [128, 32768], BF16))
        xnA = en(nc.sbuf_tensor("xnA", [128, 16384], BF16))

        pb = [en(nc.psum_tensor("pb%d" % i, [128, 512], F32))
              for i in range(8)]
        PB = ["pb%d" % i for i in range(8)]

        blk = en(nc.Block())

        TBRES = blocks("R", 0, 4096)

        def mm(bank, out_ap, lhsT, rhs, reads, start, stop, tp=None,
               first=True, last=True):
            return E("tensor",
                     lambda e: e.matmul(out_ap, lhsT, rhs, start=start,
                                        stop=stop, tile_position=tp,
                                        skip_group_check=True),
                     reads=reads, writes=([bank] if last else []),
                     after=sc.bank_deps(bank) if first else ())

        def mm_group(bank, items):
            n = len(items)
            for i, (o, l, r, rd, st, sp, tp) in enumerate(items):
                mm(bank, o, l, r, rd, st, sp, tp, first=(i == 0),
                   last=(i == n - 1))

        E("sync", lambda e: e.dma_start(out=ident[:], in_=idf_d), dma="ident",
          writes=["ident"])
        E("gpsimd", lambda e: e.dma_start(out=p64b[:], in_=p64_d), dma="p64b",
          writes=["p64b"])
        for c in range(2):
            E("sync", lambda e, c=c: e.dma_start(out=gamb[:, c:c + 1],
              in_=gam_d[c].unsqueeze(1)), dma="gamb", writes=["gamb"])
            E("sync", lambda e, c=c: e.dma_start(out=betb[:, c:c + 1],
              in_=bet_d[c].unsqueeze(1)), dma="betb", writes=["betb"])
        for b in range(4):
            E("sync", lambda e, b=b: e.dma_start(
                out=sfh[b * 32:(b + 1) * 32, 0:1], in_=fhl_d[:, 0:1],
                allow_slow_non_contiguous=True), dma="sfh", writes=["sfh"])
            E("sync", lambda e, b=b: e.dma_start(
                out=sfl[b * 32:(b + 1) * 32, 0:1], in_=fhl_d[:, 1:2],
                allow_slow_non_contiguous=True), dma="sfl", writes=["sfl"])
        E("gpsimd", lambda e: e.dma_start(
            out=wdkv_s[:].rearrange("p (c o) -> p c o", c=8),
            in_=wdkv_d.rearrange("(c p) o -> p c o", p=128)), dma="wdkv",
          writes=["wdkv_s"])
        E("gpsimd", lambda e: e.dma_start(
            out=wuk_s[:].rearrange("p (c o) -> p c o", c=2),
            in_=wuk_d.rearrange("(c p) o -> p c o", p=128)), dma="wuk",
          writes=["wuk_s"])
        E("gpsimd", lambda e: e.dma_start(
            out=wuv_s[:].rearrange("p (c o) -> p c o", c=2),
            in_=wuv_d.rearrange("(c p) o -> p c o", p=128)), dma="wuv",
          writes=["wuv_s"])

        E("gpsimd", lambda e: e.dma_start(out=onesb[:], in_=cst_d[:, 0:1], allow_slow_non_contiguous=True),
          dma="onesb", writes=["onesb"])
        E("gpsimd", lambda e: e.dma_start(out=onescol[:],
          in_=cst_d[0:65, 0:128], allow_slow_non_contiguous=True), dma="onescol", writes=["onescol"])
        E("sync", lambda e: e.dma_start(out=s127[:],
          in_=cst_d[:, 2113:2114], allow_slow_non_contiguous=True), dma="s127", writes=["s127"])
        E("sync", lambda e: e.dma_start(out=seps[:],
          in_=cst_d[:, 2114:2115], allow_slow_non_contiguous=True), dma="seps", writes=["seps"])
        E("sync", lambda e: e.dma_start(out=sgn[:],
          in_=cst_d[:, 2112:2113], allow_slow_non_contiguous=True), dma="sgn", writes=["sgn"])
        E("vector", lambda e: e.reciprocal(r127[:], s127[:]),
          reads=["s127"], writes=["r127"])
        E("gpsimd", lambda e: e.dma_start(out=vbuf[:],
          in_=cst_d[:, 0:2112], allow_slow_non_contiguous=True), dma="vbufi",
          writes=blocks("vb", 0, 2112, 132))
        E("gpsimd", lambda e: e.dma_start(out=maskP[:], in_=mkp_d),
          dma="maskP", writes=["maskP"])


        PIH = float(np.float32(np.pi))
        PIL = float(np.float32(np.pi - np.float64(PIH)))

        def tv(fn):
            E("vector", fn, reads=TBRES, writes=TBRES)

        def table_chunk(dst, dstres, c0, add_pi2, scale_ap):
            E("sync", lambda e: e.dma_start(out=tb1[:],
              in_=pos_d[:, c0:c0 + 512]), dma="post", writes=TBRES)
            E("vector", lambda e: e.tensor_scalar(tbh[:], tb1[:], sfh[:],
              None, ALU.mult), reads=TBRES + ["sfh"], writes=TBRES)
            E("vector", lambda e: e.tensor_scalar(tbT[:], tb1[:], sfl[:],
              None, ALU.mult), reads=TBRES + ["sfl"], writes=TBRES)
            tv(lambda e: e.tensor_add(tbh[:], tbh[:], tbT[:]))
            if add_pi2:
                tv(lambda e: e.tensor_single_scalar(tbh[:], tbh[:], PI2H,
                   ALU.add))
                tv(lambda e: e.tensor_single_scalar(tbh[:], tbh[:], PI2L,
                   ALU.add))
            tv(lambda e: e.tensor_single_scalar(tbT[:], tbh[:], I2PH,
               ALU.mult))
            tv(lambda e: e.tensor_single_scalar(tb1[:], tbh[:], I2PL,
               ALU.mult))
            tv(lambda e: e.tensor_add(tbT[:], tbT[:], tb1[:]))
            tv(lambda e: e.tensor_single_scalar(tbT[:], tbT[:], 0.5,
               ALU.add))
            tv(lambda e: e.tensor_copy(tbK[:], tbT[:]))
            tv(lambda e: e.tensor_copy(tb1[:], tbK[:]))
            tv(lambda e: e.tensor_tensor(tbT[:], tb1[:], tbT[:], ALU.is_gt))
            tv(lambda e: e.tensor_sub(tb1[:], tb1[:], tbT[:]))
            tv(lambda e: e.tensor_single_scalar(tbT[:], tb1[:], TPH,
               ALU.mult))
            tv(lambda e: e.tensor_sub(tbh[:], tbh[:], tbT[:]))
            tv(lambda e: e.tensor_single_scalar(tbT[:], tb1[:], TPL,
               ALU.mult))
            tv(lambda e: e.tensor_sub(tbh[:], tbh[:], tbT[:]))
            tv(lambda e: e.tensor_scalar_max(tbh[:], tbh[:], -3.1415925))
            tv(lambda e: e.tensor_scalar_min(tbh[:], tbh[:], 3.1415925))
            E("scalar", lambda e: e.activation(dst[:, c0:c0 + 512], tbh[:],
              ACTF.Sin, scale=scale_ap), reads=TBRES + ["sgn"],
              writes=[dstres])

        for q4 in range(4):
            table_chunk(Stab, "Stab", q4 * 512, False, sgn[:])
            table_chunk(Ctab, "Ctab", q4 * 512, True, 1.0)

        def layer_norm(qci, nch, src_fn, dst_fn, nrm):
            items = []
            for dc in range(nch):
                a, rn = src_fn(dc)
                items.append((pb[6][0:1, :], onesb[:, 0:1], a,
                              rn + ["onesb"], dc == 0, dc == nch - 1, None))
            mm_group("pb6", items)
            for dc in range(nch):
                a, rn = src_fn(dc)
                xa = x2scr[:, (dc % 2) * 512:(dc % 2) * 512 + 512]
                x2n = "x2_%d" % (dc % 2)
                E("vector", lambda e, a=a, xa=xa: e.tensor_mul(xa, a, a),
                  reads=rn, writes=[x2n])
                mm("pb7", pb[7][0:1, :], onesb[:, 0:1], xa,
                   [x2n, "onesb"], dc == 0, dc == nch - 1, None,
                   first=(dc == 0), last=(dc == nch - 1))
            rpA = rowpool[0:1, 0:512]
            rpB = rowpool[0:1, 512:1024]
            rpT = rowpool[0:1, 1024:1536]
            sl = 0
            srs = statrow[0:1, 0:512]
            srm = statrow[0:1, 512:1024]
            rs_n, rm_n = "srs%d" % sl, "srm%d" % sl
            inv_n = 1.0 / nrm
            E("scalar", lambda e: e.activation(rpA, pb[6][0:1, :], ACTF.Copy,
              scale=inv_n), reads=["pb6"], writes=["rpA"])
            E("scalar", lambda e: e.activation(rpB, pb[7][0:1, :], ACTF.Copy,
              scale=inv_n), reads=["pb7"], writes=["rpB"])
            E("vector", lambda e: e.tensor_mul(rpT, rpA, rpA),
              reads=["rpA"], writes=["rpT"])
            E("vector", lambda e: e.tensor_sub(rpT, rpB, rpT),
              reads=["rpB", "rpT"], writes=["rpT"])
            E("scalar", lambda e: e.activation(rpB, rpT, ACTF.Ln,
              bias=seps[0:1, 0:1]), reads=["rpT", "seps"], writes=["rpB"])
            E("scalar", lambda e: e.activation(srs, rpB, ACTF.Exp,
              scale=-0.5), reads=["rpB"], writes=[rs_n])
            E("vector", lambda e: e.scalar_tensor_tensor(srm, rpA, -1.0, srs,
              ALU.mult, ALU.mult), reads=["rpA", rs_n], writes=[rm_n])
            mm("pb6", pb[6][:], onescol[0:1, 0:128], srs,
               [rs_n, "onescol"], True, True, (0, 0))
            mm("pb7", pb[7][:], onescol[0:1, 0:128], srm,
               [rm_n, "onescol"], True, True, (0, 0))
            for dc in range(nch):
                a, rn = src_fn(dc)
                d, wn, post = dst_fn(dc)
                E("vector", lambda e, a=a, d=d: e.scalar_tensor_tensor(
                    d, a, 1.0, pb[6][:], ALU.mult, ALU.mult),
                  reads=rn + ["pb6"], writes=wn)
                E("vector", lambda e, d=d: e.tensor_add(d, d, pb[7][:]),
                  reads=["pb7"] + wn, writes=wn)
                if post is not None:
                    oc = post
                    E("vector", lambda e, d=d, oc=oc: e.tensor_scalar(
                        d, d, gamb[:, oc:oc + 1], betb[:, oc:oc + 1],
                        ALU.mult, ALU.add),
                      reads=wn + ["gamb", "betb"], writes=wn)

        def dbg_store(s, ap3, rd=()):
            return E("gpsimd", lambda e, s=s, ap3=ap3: e.dma_start(
                out=out_d[s].rearrange("(c p) t -> p c t", p=128)[
                    :, 0:ap3.shape[1], :],
                in_=ap3), dma="dbg",
                reads=list(rd), after=sc.barrier(), force=True)

        sc.barrier = lambda: [(k, v) for k, v in sc.cnt.items() if v > 0]

        store_toks = []
        for s in range(2):
            if DBG_STAGE == "s0only" and s == 1:
                continue
            Rc = s * 16384
            Ro = (1 - s) * 16384

            E("gpsimd", lambda e, s=s, Rc=Rc: e.dma_start(
                out=R[:, Rc:Rc + 16384].rearrange("p (c t) -> p c t", c=8),
                in_=x_d[s].rearrange("(c p) t -> p c t", p=128)),
              dma="xl", writes=blocks("R", Rc, 16384))

            for qc in range(QC):
                tc = qc * 512

                def src(dc, Rc=Rc, tc=tc):
                    c = Rc + dc * 2048 + tc
                    return (R[:, c:c + 512], blocks("R", c, 512))

                def dst(dc, tc=tc):
                    c = dc * 2048 + tc
                    return (xnA[:, c:c + 512], blocks("xA", c, 512), None)
                layer_norm(qc, DC, src, dst, D)

            if DBG_STAGE == "xn":
                store_toks.append(dbg_store(
                    s, xnA[:].rearrange("p (c t) -> p c t", c=8),
                    rd=blocks("xA", 0, 16384)))
                continue

            for qc in range(QC):
                tc = qc * 512
                for oc in range(LC):
                    items = []
                    for dc in range(DC):
                        c = dc * 2048 + tc
                        items.append((pb[0][:],
                                      wdkv_s[:, dc * 256 + oc * 128:
                                             dc * 256 + oc * 128 + 128],
                                      xnA[:, c:c + 512],
                                      ["wdkv_s"] + blocks("xA", c, 512),
                                      dc == 0, dc == DC - 1, None))
                    mm_group("pb0", items)
                    lc_ = oc * 2048 + tc
                    E("scalar", lambda e, lc_=lc_: e.activation(
                        latn[:, lc_:lc_ + 512], pb[0][:], ACTF.Copy),
                      reads=["pb0"], writes=blocks("ln", lc_, 512))

                def srcl(ocx, tc=tc):
                    c = ocx * 2048 + tc
                    return (latn[:, c:c + 512], blocks("ln", c, 512))

                def dstl(ocx, tc=tc):
                    c = ocx * 2048 + tc
                    return (latn[:, c:c + 512], blocks("ln", c, 512), ocx)
                layer_norm(qc, LC, srcl, dstl, 256)

            if DBG_STAGE == "latn":
                store_toks.append(dbg_store(
                    s, latn[:].rearrange("p (c t) -> p c t", c=2),
                    rd=blocks("ln", 0, 4096)))
                continue

            for hp in range(8):
                hs = hp % 2
                qof = hs * 2048
                wqn = "wqs%d" % hs
                E("gpsimd", lambda e, hp=hp, hs=hs: e.dma_start(
                    out=wqs[:, hs * 1024:hs * 1024 + 1024].rearrange(
                        "p (c o) -> p c o", c=8),
                    in_=wq_d.rearrange("(c p) o -> p c o", p=128)[
                        :, :, hp * 128:hp * 128 + 128]),
                  dma="wq%d" % hs, writes=[wqn])

                def rope(dst_ap, dst_res, tc, qp):
                    qsl = q_sb[:, qp * 512:qp * 512 + 512]
                    t2 = t2sb[:, 0:512]
                    t1 = t1sb[:, qp * 512:qp * 512 + 512]
                    qn, t1n, t2n = "qsb%d" % qp, "t1_%d" % qp, "t2_0"
                    E("scalar", lambda e: e.activation(qsl, pb[0][:],
                      ACTF.Copy), reads=["pb0"], writes=[qn])
                    mm("pb1", pb[1][:], p64b[:], qsl, ["p64b", qn],
                       True, True, None)
                    E("vector", lambda e: e.tensor_mul(t2, pb[1][:],
                      Stab[:, tc:tc + 512]), reads=["pb1", "Stab"],
                      writes=[t2n])
                    E("vector", lambda e: e.tensor_mul(t1, qsl,
                      Ctab[:, tc:tc + 512]), reads=[qn, "Ctab"],
                      writes=[t1n])
                    E("vector", lambda e: e.tensor_add(dst_ap, t1, t2),
                      reads=[t1n, t2n], writes=dst_res)

                for qc in range(QC):
                    tc = qc * 512
                    items = []
                    for dc in range(DC):
                        c = dc * 2048 + tc
                        items.append((pb[0][:],
                                      wqs[:, hs * 1024 + dc * 128:
                                          hs * 1024 + dc * 128 + 128],
                                      xnA[:, c:c + 512],
                                      [wqn] + blocks("xA", c, 512),
                                      dc == 0, dc == DC - 1, None))
                    mm_group("pb0", items)
                    rope(qt[:, qof + tc:qof + tc + 512],
                         blocks("qt", qof + tc, 512), tc, qc % 2)
                for qc in range(QC):
                    tc = qc * 512
                    items = []
                    for lc in range(LC):
                        c = lc * 2048 + tc
                        items.append((pb[0][:],
                                      wuk_s[:, lc * 1024 + hp * 128:
                                            lc * 1024 + hp * 128 + 128],
                                      latn[:, c:c + 512],
                                      ["wuk_s"] + blocks("ln", c, 512),
                                      lc == 0, lc == LC - 1, None))
                    mm_group("pb0", items)
                    rope(ktb[:, qof + tc:qof + tc + 512],
                         blocks("kk", qof + tc, 512), tc, qc % 2)
                for kti in range(KT):
                    items = []
                    for lc in range(LC):
                        c = lc * 2048 + kti * 128
                        items.append((pb[2][:, 0:128],
                                      latn[:, c:c + 128],
                                      wuv_s[:, lc * 1024 + hp * 128:
                                            lc * 1024 + hp * 128 + 128],
                                      ["wuv_s"] + blocks("ln", c, 128),
                                      lc == 0, lc == LC - 1, None))
                    mm_group("pb2", items)
                    E("vector", lambda e, kti=kti: e.tensor_copy(
                        vbuf[:, kti * 132:kti * 132 + 132].rearrange(
                            "p (h x) -> p h x", h=2)[:, :, 0:64],
                        pb[2][:, 0:128].rearrange("p (h x) -> p h x", h=2)),
                      reads=["pb2"], writes=["vb%d" % kti])

                if DBG_STAGE == "qkv" and hp == 0:
                    store_toks.append(dbg_store(
                        s, ktb[:, 0:2048].rearrange("p (c t) -> p c t", c=1),
                        rd=blocks("kk", 0, 2048)))
                    store_toks.append(E("gpsimd", lambda e, s=s: e.dma_start(
                        out=out_d[s].rearrange("(c p) t -> p c t", p=128)[
                            :, 1:2, :],
                        in_=qt[:, 0:2048].rearrange(
                            "p (c t) -> p c t", c=1)), dma="dbg",
                        force=True,
                        reads=blocks("qt", 0, 2048), after=sc.barrier()))
                    store_toks.append(E("gpsimd", lambda e, s=s: e.dma_start(
                        out=out_d[s].rearrange("(c p) t -> p c t", p=128)[
                            :, 2:3, 0:2048],
                        in_=vbuf[:, 0:2048].rearrange(
                            "p (c t) -> p c t", c=1)), dma="dbg",
                        force=True,
                        reads=blocks("vb", 0, 2048, 132),
                        after=sc.barrier()))
                    break

                if True:
                  for head in range(2):
                      r0 = head * 64
                      for qtile in range(KT):
                          qcol = qof + qtile * 128
                          qres = blocks("qt", qcol, 128)
                          b = head + 2 * (qtile % 2)
                          mm(PB[b], pb[b][:, 0:128],
                             qt[r0:r0 + 64, qcol:qcol + 128],
                             ktb[r0:r0 + 64, qcol:qcol + 128],
                             qres + blocks("kk", qcol, 128),
                             True, True, (r0, 0))
                          E("vector", lambda e, b=b, head=head,
                            qtile=qtile: e.reduce_max(
                              mbuf[:, head * 16 + qtile:
                                   head * 16 + qtile + 1],
                              pb[b][:, 0:128], axis=AX),
                            reads=[PB[b]], writes=["mbuf"])
                      E("vector", lambda e, head=head: e.tensor_scalar_mul(
                          mbuf[:, head * 16:head * 16 + 16],
                          mbuf[:, head * 16:head * 16 + 16], -1.0),
                        reads=["mbuf"], writes=["mbuf"])
                      E("vector", lambda e, head=head: e.tensor_scalar_max(
                          mbuf[:, head * 16:head * 16 + 16],
                          mbuf[:, head * 16:head * 16 + 16], -64.0),
                        reads=["mbuf"], writes=["mbuf"])
                      E("vector", lambda e, head=head: e.tensor_scalar_min(
                          mbuf[:, head * 16:head * 16 + 16],
                          mbuf[:, head * 16:head * 16 + 16], 64.0),
                        reads=["mbuf"], writes=["mbuf"])
                      E("tensor", lambda e, head=head: e.transpose(
                          pb[2][0:16, 0:128], mbuf[:, head * 16:head * 16 + 16],
                          ident[:]), reads=["mbuf", "ident"], writes=["pb2"],
                        after=sc.bank_deps("pb2"))
                      E("vector", lambda e, head=head: e.tensor_copy(
                          smtt[:, head * 128:head * 128 + 128],
                          pb[2][0:16, 0:128]), reads=["pb2"],
                        writes=["smtt%d" % head])
                      mr = head * 32
                      E("sync", lambda e, head=head, mr=mr, hs=hs: e.dma_start(
                          out=mrow[mr:mr + 1, 0:2048].rearrange(
                              "p (t q) -> p t q", t=16),
                          in_=smtt[:, head * 128:head * 128 + 128]),
                        dma="mr%d%d" % (head, hs),
                        reads=["smtt%d" % head],
                        writes=["mrow%d_%d" % (head, hs)])

                  if DBG_STAGE == "mdump" and hp == 0:
                      store_toks.append(E("gpsimd", lambda e, s=s: e.dma_start(
                          out=out_d[s].rearrange("(c p) t -> p c t", p=128)[
                              :, 0:1, 32:34],
                          in_=mtmp[:].rearrange("p (c t) -> p c t", c=1)),
                          dma="dbg", force=True, reads=["mtmp"],
                          after=sc.barrier()))
                      store_toks.append(E("gpsimd", lambda e, s=s: e.dma_start(
                          out=out_d[s].rearrange("(c p) t -> p c t", p=128)[
                              :, 0:1, 40:44],
                          in_=dbgc[:].rearrange("p (c t) -> p c t", c=1)),
                          dma="dbg", force=True, reads=["dbgc"],
                          after=sc.barrier()))
                      store_toks.append(E("gpsimd", lambda e, s=s: e.dma_start(
                          out=out_d[s].rearrange("(c p) t -> p c t", p=128)[
                              :, 2:3, 0:2048],
                          in_=qt[:, 0:2048].rearrange(
                              "p (c t) -> p c t", c=1)),
                          dma="dbg", force=True, reads=blocks("qt", 0, 2048),
                          after=sc.barrier()))
                      store_toks.append(E("gpsimd", lambda e, s=s: e.dma_start(
                          out=out_d[s].rearrange("(c p) t -> p c t", p=128)[
                              :, 3:4, 0:2048],
                          in_=ktb[:, 0:2048].rearrange(
                              "p (c t) -> p c t", c=1)),
                          dma="dbg", force=True, reads=blocks("kk", 0, 2048),
                          after=sc.barrier()))
                      store_toks.append(E("gpsimd", lambda e, s=s: e.dma_start(
                          out=out_d[s].rearrange("(c p) t -> p c t", p=128)[
                              :, 0:1, 0:32],
                          in_=mbuf[:].rearrange("p (c t) -> p c t", c=1)),
                          dma="dbg", force=True, reads=["mbuf"],
                          after=sc.barrier()))
                      store_toks.append(E("gpsimd", lambda e, s=s: e.dma_start(
                          out=out_d[s].rearrange("(c p) t -> p c t", p=128)[
                              0:33, 1:2, 0:2048],
                          in_=mrow[:, 0:2048].rearrange(
                              "p (c t) -> p c t", c=1)),
                          dma="dbg", force=True,
                          reads=["mrow0_0", "mrow1_0"], after=sc.barrier()))
                      break

                for qc in range(QC):
                    tc = qc * 512
                    lastkt = min(KT - 1, 4 * qc + 3)

                    def st_pair(kti, tc=tc, qof=qof, hs=hs):
                        par = kti % 2
                        off = max(0, kti * 128 - tc)
                        for head in range(2):
                            b = par * 2 + head
                            r0 = head * 64
                            kcol = qof + kti * 128
                            mm(PB[b], pb[b][:, off:512],
                               ktb[r0:r0 + 64, kcol:kcol + 128],
                               qt[r0:r0 + 64, qof + tc + off:qof + tc + 512],
                               blocks("kk", kcol, 128) +
                               blocks("qt", qof + tc + off, 512 - off),
                               True, False, (r0, 0), first=True, last=False)
                            mr = head * 32
                            mm(PB[b], pb[b][:, off:512],
                               onescol[mr:mr + 1, 0:128],
                               mrow[mr:mr + 1, tc + off:tc + 512],
                               ["onescol", "mrow%d_%d" % (head, hs)],
                               False, True, (mr, 0), first=False, last=True)

                    def exp_mask(kti, tc=tc):
                        par = kti % 2
                        off = max(0, kti * 128 - tc)
                        for head in range(2):
                            b = par * 2 + head
                            sl = head * 2 + par
                            ps = pbuf[:, sl * 512:sl * 512 + 512]
                            pres = "pbuf%d" % sl
                            E("scalar", lambda e, b=b, off=off, ps=ps:
                              e.activation(ps[:, off:512], pb[b][:, off:512],
                                           ACTF.Exp, scale=0.125),
                              reads=[PB[b]], writes=[pres])
                            if kti * 128 >= tc:
                                E("vector", lambda e, ps=ps, off=off:
                                  e.tensor_mul(ps[:, off:off + 128],
                                               ps[:, off:off + 128],
                                               maskP[:]),
                                  reads=[pres, "maskP"], writes=[pres])

                    def av(kti, tc=tc, lastkt=lastkt):
                        par = kti % 2
                        off = max(0, kti * 128 - tc)
                        for head in range(2):
                            bav = 4 + head
                            sl = head * 2 + par
                            ps = pbuf[:, sl * 512:sl * 512 + 512]
                            pres = "pbuf%d" % sl
                            mm(PB[bav], pb[bav][0:65, off:512],
                               vbuf[:, kti * 132 + head * 66:
                                    kti * 132 + head * 66 + 65],
                               ps[:, off:512], ["vb%d" % kti, pres],
                               kti == 0, kti == lastkt, None,
                               first=(kti == 0), last=(kti == lastkt))

                    st_pair(0)
                    for kti in range(lastkt + 1):
                        if kti + 1 <= lastkt:
                            st_pair(kti + 1)
                        exp_mask(kti)
                        av(kti)

                    for head in range(2):
                        bav = 4 + head
                        bb = head
                        rr = reciprow[64:65, head * 512:head * 512 + 512]
                        rres = "recip%d" % head
                        E("scalar", lambda e, bav=bav: e.activation(
                            pb[bav][64:65, :], pb[bav][64:65, :], ACTF.Ln),
                          reads=[PB[bav]], writes=[PB[bav]])
                        E("scalar", lambda e, bav=bav, rr=rr: e.activation(
                            rr, pb[bav][64:65, :], ACTF.Exp, scale=-1.0),
                          reads=[PB[bav]], writes=[rres])
                        mm(PB[bb], pb[bb][0:64, :], onescol[64:65, 0:64], rr,
                           [rres, "onescol"], True, True, (64, 0))
                        avs = av_sb[:, head * 512:head * 512 + 512]
                        avn = "avsb%d" % head
                        E("scalar", lambda e, bav=bav, avs=avs: e.activation(
                            avs, pb[bav][0:64, :], ACTF.Copy),
                          reads=[PB[bav]], writes=[avn])
                        ac = Ro + hp * 2048 + tc
                        if head == 0:
                            E("vector", lambda e, avs=avs, bb=bb, ac=ac:
                              e.scalar_tensor_tensor(
                                R[0:64, ac:ac + 512], avs, 1.0,
                                pb[bb][0:64, :], ALU.mult, ALU.mult),
                              reads=[avn, PB[bb]], writes=blocks("R", ac, 512))
                        else:
                            E("vector", lambda e, avs=avs, bb=bb, tc=tc:
                              e.scalar_tensor_tensor(
                                bstg[:, tc:tc + 512], avs, 1.0,
                                pb[bb][0:64, :], ALU.mult, ALU.mult),
                              reads=[avn, PB[bb]], writes=["bstg"])
                if True:
                    ac0 = Ro + hp * 2048
                    E("sync", lambda e, ac0=ac0: e.dma_start(
                        out=R[64:128, ac0:ac0 + 2048], in_=bstg[:]),
                      dma="bs", reads=["bstg"],
                      writes=blocks("R", ac0, 2048))

            if DBG_STAGE == "qkv":
                continue
            if DBG_STAGE == "att":
                store_toks.append(dbg_store(
                    s, R[:, Ro:Ro + 16384].rearrange("p (c t) -> p c t",
                                                     c=8),
                    rd=blocks("R", Ro, 16384)))
                continue

            for oc in range(8):
                osl = oc % 2
                won = "wos%d" % osl
                E("gpsimd", lambda e, oc=oc, osl=osl: e.dma_start(
                    out=wos[:, osl * 1024:osl * 1024 + 1024].rearrange(
                        "p (c o) -> p c o", c=8),
                    in_=wo_d.rearrange("(c p) o -> p c o", p=128)[
                        :, :, oc * 128:oc * 128 + 128]),
                  dma="wo%d" % osl, writes=[won])
                for qc in range(QC):
                    tc = qc * 512
                    b = qc % 2
                    items = []
                    for dc in range(DC):
                        ac = Ro + dc * 2048 + tc
                        items.append((pb[b][:],
                                      wos[:, osl * 1024 + dc * 128:
                                          osl * 1024 + dc * 128 + 128],
                                      R[:, ac:ac + 512],
                                      [won] + blocks("R", ac, 512),
                                      dc == 0, dc == DC - 1, None))
                    mm_group(PB[b], items)
                    rc = Rc + oc * 2048 + tc
                    E("vector", lambda e, b=b, rc=rc: e.tensor_add(
                        R[:, rc:rc + 512], R[:, rc:rc + 512], pb[b][:]),
                      reads=[PB[b]] + blocks("R", rc, 512),
                      writes=blocks("R", rc, 512))

            if DBG_STAGE == "wo":
                store_toks.append(dbg_store(
                    s, R[:, Rc:Rc + 16384].rearrange("p (c t) -> p c t",
                                                     c=8),
                    rd=blocks("R", Rc, 16384)))
                continue

            def hloc(hc, t0):
                if hc < 16:
                    c = Ro + hc * 1024 + t0
                    return (R[:, c:c + 512], blocks("R", c, 512))
                c = 8192 + (hc - 16) * 1024 + t0
                return (xnA[:, c:c + 512], blocks("xA", c, 512))

            for st_i in range(2):
                stc = st_i * 1024
                for qc2 in range(2):
                    loc = qc2 * 512

                    def src2(dc, Rc=Rc, stc=stc, loc=loc):
                        c = Rc + dc * 2048 + stc + loc
                        return (R[:, c:c + 512], blocks("R", c, 512))

                    def dst2(dc, loc=loc):
                        c = dc * 1024 + loc
                        return (xnA[:, c:c + 512], blocks("xA", c, 512),
                                None)
                    layer_norm(qc2, DC, src2, dst2, D)

                if DBG_STAGE == "ffnln" and st_i == 0:
                    store_toks.append(dbg_store(
                        s, xnA[:, 0:8192].rearrange("p (c t) -> p c t", c=4),
                        rd=blocks("xA", 0, 8192)))
                    break
                for hc in range(HC):
                    hsl = hc % 2
                    wgn, wun = "wgs%d" % hsl, "wus%d" % hsl
                    E("gpsimd", lambda e, hc=hc, hsl=hsl: e.dma_start(
                        out=wgs[:, hsl * 1024:hsl * 1024 + 1024].rearrange(
                            "p (c o) -> p c o", c=8),
                        in_=wgt_d.rearrange("(c p) h -> p c h", p=128)[
                            :, :, hc * 128:hc * 128 + 128]),
                      dma="wg%d" % hsl, writes=[wgn])
                    E("gpsimd", lambda e, hc=hc, hsl=hsl: e.dma_start(
                        out=wus[:, hsl * 1024:hsl * 1024 + 1024].rearrange(
                            "p (c o) -> p c o", c=8),
                        in_=wut_d.rearrange("(c p) h -> p c h", p=128)[
                            :, :, hc * 128:hc * 128 + 128]),
                      dma="wu%d" % hsl, writes=[wun])
                    for tt in range(2):
                        t0 = tt * 512
                        bg, bu = tt * 2, tt * 2 + 1
                        for wname, bx in ((wgn, bg), (wun, bu)):
                            wt = wgs if wname.startswith("wg") else wus
                            items = []
                            for dc in range(DC):
                                c = dc * 1024 + t0
                                items.append((pb[bx][:],
                                              wt[:, hsl * 1024 + dc * 128:
                                                 hsl * 1024 + dc * 128 + 128],
                                              xnA[:, c:c + 512],
                                              [wname] + blocks("xA", c, 512),
                                              dc == 0, dc == DC - 1, None))
                            mm_group(PB[bx], items)
                        gs = q_sb[:, tt * 512:tt * 512 + 512]
                        gn = "qsb%d" % tt
                        E("scalar", lambda e, bg=bg, gs=gs: e.activation(
                            gs, pb[bg][:], ACTF.Silu, scale=r127[:]),
                          reads=[PB[bg], "r127"], writes=[gn])
                        ha, hres = hloc(hc, t0)
                        E("vector", lambda e, bu=bu, gs=gs, ha=ha:
                          e.scalar_tensor_tensor(ha, pb[bu][:], r127[:], gs,
                                                 ALU.mult, ALU.mult),
                          reads=[PB[bu], "r127", gn], writes=hres)
                if DBG_STAGE == "ffnh" and st_i == 0:
                    store_toks.append(dbg_store(
                        s, R[:, Ro:Ro + 16384].rearrange(
                            "p (c t) -> p c t", c=8),
                        rd=blocks("R", Ro, 16384)))
                    break
                if DBG_STAGE == "ffnh2" and st_i == 0:
                    store_toks.append(dbg_store(
                        s, xnA[:].rearrange("p (c t) -> p c t", c=8),
                        rd=blocks("xA", 0, 16384)))
                    break
                for dc in range(DC):
                    dsl = 0
                    wdn = "wds0"
                    E("gpsimd", lambda e, dc=dc, dsl=dsl: e.dma_start(
                        out=wds[:, 0:3072].rearrange(
                            "p (c o) -> p c o", c=24),
                        in_=wdt_d.rearrange("(c p) o -> p c o", p=128)[
                            :, :, dc * 128:dc * 128 + 128]),
                      dma="wd%d" % dsl, writes=[wdn])
                    for tt in range(2):
                        t0 = tt * 512
                        b = 4 + tt
                        items = []
                        for hc in range(HC):
                            ha, hres = hloc(hc, t0)
                            items.append((pb[b][:],
                                          wds[:, dsl * 3072 + hc * 128:
                                              dsl * 3072 + hc * 128 + 128],
                                          ha, [wdn] + hres,
                                          hc == 0, hc == HC - 1, None))
                        mm_group(PB[b], items)
                        rc = Rc + dc * 2048 + stc + t0
                        E("vector", lambda e, b=b, rc=rc:
                          e.scalar_tensor_tensor(
                            R[:, rc:rc + 512], pb[b][:], r127[:],
                            R[:, rc:rc + 512], ALU.mult, ALU.add),
                          reads=[PB[b], "r127"] + blocks("R", rc, 512),
                          writes=blocks("R", rc, 512))

            if DBG_STAGE not in ("ffnln", "ffnh", "ffnh2", "mdump"):
                tok = E("gpsimd", lambda e, s=s, Rc=Rc: e.dma_start(
                    out=out_d[s].rearrange("(c p) t -> p c t", p=128),
                    in_=R[:, Rc:Rc + 16384].rearrange("p (c t) -> p c t",
                                                      c=8)),
                  dma="st", reads=blocks("R", Rc, 16384))
                store_toks.append(tok)

        E("sync", lambda e: e.nop(), after=store_toks, tick=False)

        blk.tensor(lambda e: [f(e) for f in sc.q["tensor"]])
        blk.vector(lambda e: [f(e) for f in sc.q["vector"]])
        blk.scalar(lambda e: [f(e) for f in sc.q["scalar"]])
        blk.gpsimd(lambda e: [f(e) for f in sc.q["gpsimd"]])
        blk.sync(lambda e: [f(e) for f in sc.q["sync"]])

    nc._sched = sc
    import sys
    print("kernel build: instrs",
          {k: len(v) for k, v in sc.q.items()}, "waits", sc.nwait,
          file=sys.stderr)
    return nc


_NC = None


def _get_nc():
    global _NC
    if _NC is None:
        _NC = build_nc()
    return _NC


def kernel(L0_attn_x, L0_attn_W_q, L0_attn_W_dkv, L0_attn_W_uk, L0_attn_W_uv,
           L0_attn_W_o, L0_attn_ln_gamma, L0_attn_ln_beta, L0_ffn_w_gate,
           L0_ffn_w_up, L0_ffn_w_down):
    x = np.asarray(L0_attn_x, dtype=np.float32)
    xT = np.ascontiguousarray(np.transpose(x, (0, 2, 1)))
    wq = np.ascontiguousarray(np.asarray(L0_attn_W_q, np.float32))
    wdkv = np.ascontiguousarray(np.asarray(L0_attn_W_dkv, np.float32))
    wuk = np.ascontiguousarray(np.asarray(L0_attn_W_uk, np.float32))
    wuv = np.ascontiguousarray(np.asarray(L0_attn_W_uv, np.float32))
    wo = np.ascontiguousarray(np.asarray(L0_attn_W_o, np.float32))
    gam = np.ascontiguousarray(
        np.asarray(L0_attn_ln_gamma, np.float32).reshape(2, 128))
    bet = np.ascontiguousarray(
        np.asarray(L0_attn_ln_beta, np.float32).reshape(2, 128))
    wgt = np.ascontiguousarray(np.asarray(L0_ffn_w_gate, np.int32).T)
    wut = np.ascontiguousarray(np.asarray(L0_ffn_w_up, np.int32).T)
    wdt = np.ascontiguousarray(np.asarray(L0_ffn_w_down, np.int32).T)
    inv = 1.0 / (10000.0 ** (np.arange(32, dtype=np.float64) / 32.0))
    fh = inv.astype(np.float32)
    fl = (inv - fh.astype(np.float64)).astype(np.float32)
    fhl = np.ascontiguousarray(np.stack([fh, fl], axis=1))
    idf = np.eye(128, dtype=np.float32)
    p64 = np.zeros((128, 128), dtype=np.float32)
    p64[np.arange(128), np.arange(128) ^ 32] = 1.0
    cst = np.ones((128, 2118), dtype=np.float32)
    sgnv = np.ones(128, np.float32)
    sgnv[0:32] = -1.0; sgnv[32:64] = 1.0; sgnv[64:96] = -1.0; sgnv[96:128] = 1.0
    cst[:, 2112] = sgnv
    cst[:, 2113] = 127.0
    cst[:, 2114] = 1e-5
    mkp = (np.arange(128)[None, :] >= np.arange(128)[:, None]).astype(np.float32)
    post = np.broadcast_to(np.arange(2048, dtype=np.float32)[None, :],
                           (128, 2048)).copy()

    nc = _get_nc()
    in_maps = []
    for c in range(8):
        in_maps.append(dict(
            x=np.ascontiguousarray(xT[2 * c:2 * c + 2]),
            wq=wq, wdkv=wdkv, wuk=wuk, wuv=wuv, wo=wo, gam=gam, bet=bet,
            wgt=wgt, wut=wut, wdt=wdt, fhl=fhl, idf=idf, p64=p64,
            cst=cst, mkp=mkp, post=post))
    res = run_bass_kernel_spmd(nc, in_maps, core_ids=list(range(8)))
    outs = []
    for c in range(8):
        o = np.asarray(res.results[c]["out"])
        outs.append(np.transpose(o, (0, 2, 1)))
    return np.ascontiguousarray(np.concatenate(outs, axis=0))


if __name__ == "__main__":
    build_nc()
    print("build ok")


# version 12
# speedup vs baseline: 2.4257x; 1.3148x over previous
import numpy as np
from contextlib import ExitStack

import concourse.bass as bass
import concourse.mybir as mybir
from concourse.bass_utils import run_bass_kernel_spmd

F32 = mybir.dt.float32
BF16 = mybir.dt.bfloat16
I32 = mybir.dt.int32
AX = mybir.AxisListType.X
ALU = mybir.AluOpType
ACTF = mybir.ActivationFunctionType


def _sp(v):
    hi = float(np.float32(v))
    lo = float(np.float32(v - np.float64(hi)))
    return hi, lo


PI2H, PI2L = _sp(np.pi / 2)
TPH, TPL = _sp(2 * np.pi)
I2PH, I2PL = _sp(1.0 / (2 * np.pi))

D = 1024
DBG_STAGE = None
SENTINEL = False
FORCE_WAITS = False
DC = 8
LC = 2
HC = 24
QC = 4
KT = 16

ENG_SEM = {"tensor": "T", "vector": "V", "scalar": "S", "gpsimd": "G",
           "sync": "Y"}


class Sched:
    def __init__(self):
        self.q = {e: [] for e in ENG_SEM}
        self.new_sem = None
        self.cnt = {s: 0 for s in ("T", "V", "S", "G", "Y")}
        self.eclk = {e: {} for e in ENG_SEM}
        self.tokclk = {}
        self.res = {}
        self.semh = None
        self.nwait = 0

    def r_(self, name):
        if name not in self.res:
            self.res[name] = {"w": [], "r": []}
        return self.res[name]

    def emit(self, eng, fn, reads=(), writes=(), after=(), dma=False,
             tick=True, force=False):
        waits = []
        for r in reads:
            waits += self.r_(r)["w"]
        for w in writes:
            waits += self.r_(w)["w"] + self.r_(w)["r"]
        waits += list(after)
        my = ENG_SEM[eng]
        if dma:
            assert isinstance(dma, str), "dma must be a stream name"
            sem = "d_" + dma
            if sem not in self.cnt:
                self.cnt[sem] = 0
                self.semh[sem] = self.new_sem(sem)
            inc = 16
        else:
            sem = my
            inc = 1
        clk = self.eclk[eng]
        clk[my] = self.cnt[my]
        final = []
        for s, v in sorted(set(waits)):
            if s == my and not dma:
                continue
            if clk.get(s, 0) >= v and not (force or FORCE_WAITS):
                continue
            final.append((s, v))
            tc = self.tokclk.get((s, v))
            if tc:
                for s2, v2 in tc.items():
                    if clk.get(s2, 0) < v2:
                        clk[s2] = v2
            else:
                clk[s] = max(clk.get(s, 0), v)
        if not FORCE_WAITS:
            final = [(s, v) for (s, v) in final
                     if not (s != my and self.eclk[eng].get(s, -1) > v)]
        else:
            best = {}
            for s, v in final:
                best[s] = max(best.get(s, 0), v)
            final = sorted(best.items())
        self.nwait += len(final)
        tok = None
        if tick:
            self.cnt[sem] += inc
            tok = (sem, self.cnt[sem])
            clk2 = dict(clk)
            clk2[sem] = self.cnt[sem]
            self.tokclk[tok] = clk2
        self.eclk[eng] = clk

        def run(e, _w=tuple(final), _fn=fn, _tok=tok, _sem=sem, _inc=inc):
            for s, v in _w:
                e.wait_ge(self.semh[s], v)
            ins = _fn(e)
            if _tok is not None:
                ins.then_inc(self.semh[_sem], _inc)
        run._waits = tuple(final)
        run._tok = tok
        run._writes = tuple(writes)
        run._reads = tuple(reads)
        self.q[eng].append(run)

        if tok is not None:
            for w in writes:
                self.res[w] = {"w": [tok], "r": []}
            for r in reads:
                self.r_(r)["r"].append(tok)
        return tok

    def bank_deps(self, bank):
        return tuple(self.r_(bank)["w"] + self.r_(bank)["r"])


def blocks(base, col, n, gran=512):
    return ["%s%d" % (base, i) for i in range(col // gran,
                                              (col + n + gran - 1) // gran)]


def build_nc():
    nc = bass.Bass("TRN2", target_bir_lowering=False,
                   detect_race_conditions=False)
    dt = nc.dram_tensor
    x_d = dt("x", [2, D, 2048], F32, kind="ExternalInput").ap()
    wq_d = dt("wq", [D, D], F32, kind="ExternalInput").ap()
    wdkv_d = dt("wdkv", [D, 256], F32, kind="ExternalInput").ap()
    wuk_d = dt("wuk", [256, D], F32, kind="ExternalInput").ap()
    wuv_d = dt("wuv", [256, D], F32, kind="ExternalInput").ap()
    wo_d = dt("wo", [D, D], F32, kind="ExternalInput").ap()
    gam_d = dt("gam", [2, 128], F32, kind="ExternalInput").ap()
    bet_d = dt("bet", [2, 128], F32, kind="ExternalInput").ap()
    wgt_d = dt("wgt", [D, 3072], I32, kind="ExternalInput").ap()
    wut_d = dt("wut", [D, 3072], I32, kind="ExternalInput").ap()
    wdt_d = dt("wdt", [3072, D], I32, kind="ExternalInput").ap()
    fhl_d = dt("fhl", [32, 2], F32, kind="ExternalInput").ap()
    idf_d = dt("idf", [128, 128], F32, kind="ExternalInput").ap()
    p64_d = dt("p64", [128, 128], F32, kind="ExternalInput").ap()
    cst_d = dt("cst", [128, 2118], F32, kind="ExternalInput").ap()
    mkp_d = dt("mkp", [128, 128], F32, kind="ExternalInput").ap()
    pos_d = dt("post", [128, 2048], F32, kind="ExternalInput").ap()
    out_d = dt("out", [2, D, 2048], F32, kind="ExternalOutput").ap()

    sc = Sched()
    E = sc.emit

    with ExitStack() as stk:
        en = stk.enter_context
        sems = {}
        for s in ("T", "V", "S", "G", "Y"):
            sems[s] = en(nc.semaphore("sem_" + s))
        sc.semh = sems
        sc.new_sem = lambda name: en(nc.semaphore("sem_" + name))

        ident = en(nc.sbuf_tensor("ident", [128, 128], F32))
        p64b = en(nc.sbuf_tensor("p64b", [128, 128], BF16))
        maskP = en(nc.sbuf_tensor("maskP", [128, 128], BF16))
        onesb = en(nc.sbuf_tensor("onesb", [128, 1], BF16))
        onescol = en(nc.sbuf_tensor("onescol", [65, 128], BF16))
        sgn = en(nc.sbuf_tensor("sgn", [128, 1], F32))
        sfh = en(nc.sbuf_tensor("sfh", [128, 1], F32))
        sfl = en(nc.sbuf_tensor("sfl", [128, 1], F32))
        s127 = en(nc.sbuf_tensor("s127", [128, 1], F32))
        seps = en(nc.sbuf_tensor("seps", [128, 1], F32))
        r127 = en(nc.sbuf_tensor("r127", [128, 1], F32))
        gamb = en(nc.sbuf_tensor("gamb", [128, 2], F32))
        betb = en(nc.sbuf_tensor("betb", [128, 2], F32))
        rowpool = en(nc.sbuf_tensor("rowpool", [1, 1536], BF16))
        statrow = en(nc.sbuf_tensor("statrow", [1, 1024], BF16))
        reciprow = en(nc.sbuf_tensor("reciprow", [65, 1024], BF16))
        bstg = en(nc.sbuf_tensor("bstg", [64, 2048], BF16))
        smtt = en(nc.sbuf_tensor("smtt", [16, 256], BF16))
        mbuf = en(nc.sbuf_tensor("mbuf", [128, 32], F32))
        dbgc = en(nc.sbuf_tensor("dbgc", [128, 4], F32))
        mtmp = en(nc.sbuf_tensor("mtmp", [128, 2], F32))
        mrow = en(nc.sbuf_tensor("mrow", [33, 2048], BF16))
        x2scr = en(nc.sbuf_tensor("x2scr", [128, 1024], BF16))
        q_sb = en(nc.sbuf_tensor("q_sb", [128, 1024], BF16))
        t2sb = en(nc.sbuf_tensor("t2sb", [128, 512], F32))
        t1sb = en(nc.sbuf_tensor("t1sb", [128, 1024], BF16))
        av_sb = en(nc.sbuf_tensor("av_sb", [64, 1024], BF16))
        Ctab = en(nc.sbuf_tensor("Ctab", [128, 2048], BF16))
        Stab = en(nc.sbuf_tensor("Stab", [128, 2048], BF16))
        wdkv_s = en(nc.sbuf_tensor("wdkv_s", [128, 2048], BF16))
        wuk_s = en(nc.sbuf_tensor("wuk_s", [128, 2048], BF16))
        wuv_s = en(nc.sbuf_tensor("wuv_s", [128, 2048], BF16))
        wqs = en(nc.sbuf_tensor("wqs", [128, 2048], BF16))
        wos = en(nc.sbuf_tensor("wos", [128, 2048], BF16))
        wgs = en(nc.sbuf_tensor("wgs", [128, 2048], BF16))
        wus = en(nc.sbuf_tensor("wus", [128, 2048], BF16))
        wds = en(nc.sbuf_tensor("wds", [128, 3072], BF16))
        latn = en(nc.sbuf_tensor("latn", [128, 4096], BF16))
        qt = en(nc.sbuf_tensor("qt", [128, 4096], BF16))
        ktb = en(nc.sbuf_tensor("ktb", [128, 4096], BF16))
        vbuf = en(nc.sbuf_tensor("vbuf", [128, 2112], BF16))
        pbuf = en(nc.sbuf_tensor("pbuf", [128, 2048], BF16))
        tb1 = en(nc.sbuf_tensor("tb1", [128, 512], F32))
        tbh = en(nc.sbuf_tensor("tbh", [128, 512], F32))
        tbT = en(nc.sbuf_tensor("tbT", [128, 512], F32))
        tbK = en(nc.sbuf_tensor("tbK", [128, 512], I32))
        R = en(nc.sbuf_tensor("R", [128, 32768], BF16))
        xnA = en(nc.sbuf_tensor("xnA", [128, 16384], BF16))

        pb = [en(nc.psum_tensor("pb%d" % i, [128, 512], F32))
              for i in range(8)]
        PB = ["pb%d" % i for i in range(8)]

        blk = en(nc.Block())

        TBRES = blocks("R", 0, 4096)

        def mm(bank, out_ap, lhsT, rhs, reads, start, stop, tp=None,
               first=True, last=True):
            return E("tensor",
                     lambda e: e.matmul(out_ap, lhsT, rhs, start=start,
                                        stop=stop, tile_position=tp,
                                        skip_group_check=True),
                     reads=reads, writes=([bank] if last else []),
                     after=sc.bank_deps(bank) if first else ())

        def mm_group(bank, items):
            n = len(items)
            for i, (o, l, r, rd, st, sp, tp) in enumerate(items):
                mm(bank, o, l, r, rd, st, sp, tp, first=(i == 0),
                   last=(i == n - 1))

        E("sync", lambda e: e.dma_start(out=ident[:], in_=idf_d), dma="ident",
          writes=["ident"])
        E("gpsimd", lambda e: e.dma_start(out=p64b[:], in_=p64_d), dma="p64b",
          writes=["p64b"])
        for c in range(2):
            E("sync", lambda e, c=c: e.dma_start(out=gamb[:, c:c + 1],
              in_=gam_d[c].unsqueeze(1)), dma="gamb", writes=["gamb"])
            E("sync", lambda e, c=c: e.dma_start(out=betb[:, c:c + 1],
              in_=bet_d[c].unsqueeze(1)), dma="betb", writes=["betb"])
        for b in range(4):
            E("sync", lambda e, b=b: e.dma_start(
                out=sfh[b * 32:(b + 1) * 32, 0:1], in_=fhl_d[:, 0:1],
                allow_slow_non_contiguous=True), dma="sfh", writes=["sfh"])
            E("sync", lambda e, b=b: e.dma_start(
                out=sfl[b * 32:(b + 1) * 32, 0:1], in_=fhl_d[:, 1:2],
                allow_slow_non_contiguous=True), dma="sfl", writes=["sfl"])
        E("gpsimd", lambda e: e.dma_start(
            out=wdkv_s[:].rearrange("p (c o) -> p c o", c=8),
            in_=wdkv_d.rearrange("(c p) o -> p c o", p=128)), dma="wdkv",
          writes=["wdkv_s"])
        E("gpsimd", lambda e: e.dma_start(
            out=wuk_s[:].rearrange("p (c o) -> p c o", c=2),
            in_=wuk_d.rearrange("(c p) o -> p c o", p=128)), dma="wuk",
          writes=["wuk_s"])
        E("gpsimd", lambda e: e.dma_start(
            out=wuv_s[:].rearrange("p (c o) -> p c o", c=2),
            in_=wuv_d.rearrange("(c p) o -> p c o", p=128)), dma="wuv",
          writes=["wuv_s"])

        E("gpsimd", lambda e: e.dma_start(out=onesb[:], in_=cst_d[:, 0:1], allow_slow_non_contiguous=True),
          dma="onesb", writes=["onesb"])
        E("gpsimd", lambda e: e.dma_start(out=onescol[:],
          in_=cst_d[0:65, 0:128], allow_slow_non_contiguous=True), dma="onescol", writes=["onescol"])
        E("sync", lambda e: e.dma_start(out=s127[:],
          in_=cst_d[:, 2113:2114], allow_slow_non_contiguous=True), dma="s127", writes=["s127"])
        E("sync", lambda e: e.dma_start(out=seps[:],
          in_=cst_d[:, 2114:2115], allow_slow_non_contiguous=True), dma="seps", writes=["seps"])
        E("sync", lambda e: e.dma_start(out=sgn[:],
          in_=cst_d[:, 2112:2113], allow_slow_non_contiguous=True), dma="sgn", writes=["sgn"])
        E("vector", lambda e: e.reciprocal(r127[:], s127[:]),
          reads=["s127"], writes=["r127"])
        E("gpsimd", lambda e: e.dma_start(out=vbuf[:],
          in_=cst_d[:, 0:2112], allow_slow_non_contiguous=True), dma="vbufi",
          writes=blocks("vb", 0, 2112, 132))
        E("gpsimd", lambda e: e.dma_start(out=maskP[:], in_=mkp_d),
          dma="maskP", writes=["maskP"])


        PIH = float(np.float32(np.pi))
        PIL = float(np.float32(np.pi - np.float64(PIH)))

        def tv(fn):
            E("vector", fn, reads=TBRES, writes=TBRES)

        def table_chunk(dst, dstres, c0, add_pi2, scale_ap):
            E("sync", lambda e: e.dma_start(out=tb1[:],
              in_=pos_d[:, c0:c0 + 512]), dma="post", writes=TBRES)
            E("vector", lambda e: e.tensor_scalar(tbh[:], tb1[:], sfh[:],
              None, ALU.mult), reads=TBRES + ["sfh"], writes=TBRES)
            E("vector", lambda e: e.tensor_scalar(tbT[:], tb1[:], sfl[:],
              None, ALU.mult), reads=TBRES + ["sfl"], writes=TBRES)
            tv(lambda e: e.tensor_add(tbh[:], tbh[:], tbT[:]))
            if add_pi2:
                tv(lambda e: e.tensor_single_scalar(tbh[:], tbh[:], PI2H,
                   ALU.add))
                tv(lambda e: e.tensor_single_scalar(tbh[:], tbh[:], PI2L,
                   ALU.add))
            tv(lambda e: e.tensor_single_scalar(tbT[:], tbh[:], I2PH,
               ALU.mult))
            tv(lambda e: e.tensor_single_scalar(tb1[:], tbh[:], I2PL,
               ALU.mult))
            tv(lambda e: e.tensor_add(tbT[:], tbT[:], tb1[:]))
            tv(lambda e: e.tensor_single_scalar(tbT[:], tbT[:], 0.5,
               ALU.add))
            tv(lambda e: e.tensor_copy(tbK[:], tbT[:]))
            tv(lambda e: e.tensor_copy(tb1[:], tbK[:]))
            tv(lambda e: e.tensor_tensor(tbT[:], tb1[:], tbT[:], ALU.is_gt))
            tv(lambda e: e.tensor_sub(tb1[:], tb1[:], tbT[:]))
            tv(lambda e: e.tensor_single_scalar(tbT[:], tb1[:], TPH,
               ALU.mult))
            tv(lambda e: e.tensor_sub(tbh[:], tbh[:], tbT[:]))
            tv(lambda e: e.tensor_single_scalar(tbT[:], tb1[:], TPL,
               ALU.mult))
            tv(lambda e: e.tensor_sub(tbh[:], tbh[:], tbT[:]))
            tv(lambda e: e.tensor_scalar_max(tbh[:], tbh[:], -3.1415925))
            tv(lambda e: e.tensor_scalar_min(tbh[:], tbh[:], 3.1415925))
            E("scalar", lambda e: e.activation(dst[:, c0:c0 + 512], tbh[:],
              ACTF.Sin, scale=scale_ap), reads=TBRES + ["sgn"],
              writes=[dstres])

        for q4 in range(4):
            table_chunk(Stab, "Stab", q4 * 512, False, sgn[:])
            table_chunk(Ctab, "Ctab", q4 * 512, True, 1.0)

        def layer_norm(qci, nch, src_fn, dst_fn, nrm):
            items = []
            for dc in range(nch):
                a, rn = src_fn(dc)
                items.append((pb[6][0:1, :], onesb[:, 0:1], a,
                              rn + ["onesb"], dc == 0, dc == nch - 1, None))
            mm_group("pb6", items)
            for dc in range(nch):
                a, rn = src_fn(dc)
                xa = x2scr[:, (dc % 2) * 512:(dc % 2) * 512 + 512]
                x2n = "x2_%d" % (dc % 2)
                E("vector", lambda e, a=a, xa=xa: e.tensor_mul(xa, a, a),
                  reads=rn, writes=[x2n])
                mm("pb7", pb[7][0:1, :], onesb[:, 0:1], xa,
                   [x2n, "onesb"], dc == 0, dc == nch - 1, None,
                   first=(dc == 0), last=(dc == nch - 1))
            rpA = rowpool[0:1, 0:512]
            rpB = rowpool[0:1, 512:1024]
            rpT = rowpool[0:1, 1024:1536]
            sl = 0
            srs = statrow[0:1, 0:512]
            srm = statrow[0:1, 512:1024]
            rs_n, rm_n = "srs%d" % sl, "srm%d" % sl
            inv_n = 1.0 / nrm
            E("scalar", lambda e: e.activation(rpA, pb[6][0:1, :], ACTF.Copy,
              scale=inv_n), reads=["pb6"], writes=["rpA"])
            E("scalar", lambda e: e.activation(rpB, pb[7][0:1, :], ACTF.Copy,
              scale=inv_n), reads=["pb7"], writes=["rpB"])
            E("vector", lambda e: e.tensor_mul(rpT, rpA, rpA),
              reads=["rpA"], writes=["rpT"])
            E("vector", lambda e: e.tensor_sub(rpT, rpB, rpT),
              reads=["rpB", "rpT"], writes=["rpT"])
            E("scalar", lambda e: e.activation(rpB, rpT, ACTF.Ln,
              bias=seps[0:1, 0:1]), reads=["rpT", "seps"], writes=["rpB"])
            E("scalar", lambda e: e.activation(srs, rpB, ACTF.Exp,
              scale=-0.5), reads=["rpB"], writes=[rs_n])
            E("vector", lambda e: e.scalar_tensor_tensor(srm, rpA, -1.0, srs,
              ALU.mult, ALU.mult), reads=["rpA", rs_n], writes=[rm_n])
            mm("pb6", pb[6][:], onescol[0:1, 0:128], srs,
               [rs_n, "onescol"], True, True, (0, 0))
            mm("pb7", pb[7][:], onescol[0:1, 0:128], srm,
               [rm_n, "onescol"], True, True, (0, 0))
            for dc in range(nch):
                a, rn = src_fn(dc)
                d, wn, post = dst_fn(dc)
                E("vector", lambda e, a=a, d=d: e.scalar_tensor_tensor(
                    d, a, 1.0, pb[6][:], ALU.mult, ALU.mult),
                  reads=rn + ["pb6"], writes=wn)
                E("vector", lambda e, d=d: e.tensor_add(d, d, pb[7][:]),
                  reads=["pb7"] + wn, writes=wn)
                if post is not None:
                    oc = post
                    E("vector", lambda e, d=d, oc=oc: e.tensor_scalar(
                        d, d, gamb[:, oc:oc + 1], betb[:, oc:oc + 1],
                        ALU.mult, ALU.add),
                      reads=wn + ["gamb", "betb"], writes=wn)

        def dbg_store(s, ap3, rd=()):
            return E("gpsimd", lambda e, s=s, ap3=ap3: e.dma_start(
                out=out_d[s].rearrange("(c p) t -> p c t", p=128)[
                    :, 0:ap3.shape[1], :],
                in_=ap3), dma="dbg",
                reads=list(rd), after=sc.barrier(), force=True)

        sc.barrier = lambda: [(k, v) for k, v in sc.cnt.items() if v > 0]

        store_toks = []
        for s in range(2):
            if DBG_STAGE == "s0only" and s == 1:
                continue
            Rc = s * 16384
            Ro = (1 - s) * 16384

            E("gpsimd", lambda e, s=s, Rc=Rc: e.dma_start(
                out=R[:, Rc:Rc + 16384].rearrange("p (c t) -> p c t", c=8),
                in_=x_d[s].rearrange("(c p) t -> p c t", p=128)),
              dma="xl", writes=blocks("R", Rc, 16384))

            for qc in range(QC):
                tc = qc * 512

                def src(dc, Rc=Rc, tc=tc):
                    c = Rc + dc * 2048 + tc
                    return (R[:, c:c + 512], blocks("R", c, 512))

                def dst(dc, tc=tc):
                    c = dc * 2048 + tc
                    return (xnA[:, c:c + 512], blocks("xA", c, 512), None)
                layer_norm(qc, DC, src, dst, D)

            if DBG_STAGE == "xn":
                store_toks.append(dbg_store(
                    s, xnA[:].rearrange("p (c t) -> p c t", c=8),
                    rd=blocks("xA", 0, 16384)))
                continue

            for qc in range(QC):
                tc = qc * 512
                for oc in range(LC):
                    items = []
                    for dc in range(DC):
                        c = dc * 2048 + tc
                        items.append((pb[0][:],
                                      wdkv_s[:, dc * 256 + oc * 128:
                                             dc * 256 + oc * 128 + 128],
                                      xnA[:, c:c + 512],
                                      ["wdkv_s"] + blocks("xA", c, 512),
                                      dc == 0, dc == DC - 1, None))
                    mm_group("pb0", items)
                    lc_ = oc * 2048 + tc
                    E("scalar", lambda e, lc_=lc_: e.activation(
                        latn[:, lc_:lc_ + 512], pb[0][:], ACTF.Copy),
                      reads=["pb0"], writes=blocks("ln", lc_, 512))

                def srcl(ocx, tc=tc):
                    c = ocx * 2048 + tc
                    return (latn[:, c:c + 512], blocks("ln", c, 512))

                def dstl(ocx, tc=tc):
                    c = ocx * 2048 + tc
                    return (latn[:, c:c + 512], blocks("ln", c, 512), ocx)
                layer_norm(qc, LC, srcl, dstl, 256)

            if DBG_STAGE == "latn":
                store_toks.append(dbg_store(
                    s, latn[:].rearrange("p (c t) -> p c t", c=2),
                    rd=blocks("ln", 0, 4096)))
                continue

            for hp in range(8):
                hs = hp % 2
                qof = hs * 2048
                wqn = "wqs%d" % hs
                E("gpsimd", lambda e, hp=hp, hs=hs: e.dma_start(
                    out=wqs[:, hs * 1024:hs * 1024 + 1024].rearrange(
                        "p (c o) -> p c o", c=8),
                    in_=wq_d.rearrange("(c p) o -> p c o", p=128)[
                        :, :, hp * 128:hp * 128 + 128]),
                  dma="wq%d" % hs, writes=[wqn])

                def rope(dst_ap, dst_res, tc, qp):
                    qsl = q_sb[:, qp * 512:qp * 512 + 512]
                    t2 = t2sb[:, 0:512]
                    t1 = t1sb[:, qp * 512:qp * 512 + 512]
                    qn, t1n, t2n = "qsb%d" % qp, "t1_%d" % qp, "t2_0"
                    E("scalar", lambda e: e.activation(qsl, pb[0][:],
                      ACTF.Copy), reads=["pb0"], writes=[qn])
                    mm("pb1", pb[1][:], p64b[:], qsl, ["p64b", qn],
                       True, True, None)
                    E("vector", lambda e: e.tensor_mul(t2, pb[1][:],
                      Stab[:, tc:tc + 512]), reads=["pb1", "Stab"],
                      writes=[t2n])
                    E("vector", lambda e: e.tensor_mul(t1, qsl,
                      Ctab[:, tc:tc + 512]), reads=[qn, "Ctab"],
                      writes=[t1n])
                    E("vector", lambda e: e.tensor_add(dst_ap, t1, t2),
                      reads=[t1n, t2n], writes=dst_res)

                for qc in range(QC):
                    tc = qc * 512
                    items = []
                    for dc in range(DC):
                        c = dc * 2048 + tc
                        items.append((pb[0][:],
                                      wqs[:, hs * 1024 + dc * 128:
                                          hs * 1024 + dc * 128 + 128],
                                      xnA[:, c:c + 512],
                                      [wqn] + blocks("xA", c, 512),
                                      dc == 0, dc == DC - 1, None))
                    mm_group("pb0", items)
                    rope(qt[:, qof + tc:qof + tc + 512],
                         blocks("qt", qof + tc, 512), tc, qc % 2)
                for qc in range(QC):
                    tc = qc * 512
                    items = []
                    for lc in range(LC):
                        c = lc * 2048 + tc
                        items.append((pb[0][:],
                                      wuk_s[:, lc * 1024 + hp * 128:
                                            lc * 1024 + hp * 128 + 128],
                                      latn[:, c:c + 512],
                                      ["wuk_s"] + blocks("ln", c, 512),
                                      lc == 0, lc == LC - 1, None))
                    mm_group("pb0", items)
                    rope(ktb[:, qof + tc:qof + tc + 512],
                         blocks("kk", qof + tc, 512), tc, qc % 2)
                for kti in range(KT):
                    items = []
                    for lc in range(LC):
                        c = lc * 2048 + kti * 128
                        items.append((pb[2][:, 0:128],
                                      latn[:, c:c + 128],
                                      wuv_s[:, lc * 1024 + hp * 128:
                                            lc * 1024 + hp * 128 + 128],
                                      ["wuv_s"] + blocks("ln", c, 128),
                                      lc == 0, lc == LC - 1, None))
                    mm_group("pb2", items)
                    E("vector", lambda e, kti=kti: e.tensor_copy(
                        vbuf[:, kti * 132:kti * 132 + 132].rearrange(
                            "p (h x) -> p h x", h=2)[:, :, 0:64],
                        pb[2][:, 0:128].rearrange("p (h x) -> p h x", h=2)),
                      reads=["pb2"], writes=["vb%d" % kti])

                if DBG_STAGE == "qkv" and hp == 0:
                    store_toks.append(dbg_store(
                        s, ktb[:, 0:2048].rearrange("p (c t) -> p c t", c=1),
                        rd=blocks("kk", 0, 2048)))
                    store_toks.append(E("gpsimd", lambda e, s=s: e.dma_start(
                        out=out_d[s].rearrange("(c p) t -> p c t", p=128)[
                            :, 1:2, :],
                        in_=qt[:, 0:2048].rearrange(
                            "p (c t) -> p c t", c=1)), dma="dbg",
                        force=True,
                        reads=blocks("qt", 0, 2048), after=sc.barrier()))
                    store_toks.append(E("gpsimd", lambda e, s=s: e.dma_start(
                        out=out_d[s].rearrange("(c p) t -> p c t", p=128)[
                            :, 2:3, 0:2048],
                        in_=vbuf[:, 0:2048].rearrange(
                            "p (c t) -> p c t", c=1)), dma="dbg",
                        force=True,
                        reads=blocks("vb", 0, 2048, 132),
                        after=sc.barrier()))
                    break


                for qc in range(QC):
                    tc = qc * 512
                    lastkt = min(KT - 1, 4 * qc + 3)

                    def st_pair(kti, tc=tc, qof=qof, hs=hs):
                        par = kti % 2
                        off = max(0, kti * 128 - tc)
                        for head in range(2):
                            b = par * 2 + head
                            r0 = head * 64
                            kcol = qof + kti * 128
                            mm(PB[b], pb[b][:, off:512],
                               ktb[r0:r0 + 64, kcol:kcol + 128],
                               qt[r0:r0 + 64, qof + tc + off:qof + tc + 512],
                               blocks("kk", kcol, 128) +
                               blocks("qt", qof + tc + off, 512 - off),
                               True, True, (r0, 0))

                    def exp_mask(kti, tc=tc):
                        par = kti % 2
                        off = max(0, kti * 128 - tc)
                        for head in range(2):
                            b = par * 2 + head
                            sl = head * 2 + par
                            ps = pbuf[:, sl * 512:sl * 512 + 512]
                            pres = "pbuf%d" % sl
                            E("scalar", lambda e, b=b, off=off, ps=ps:
                              e.activation(ps[:, off:512], pb[b][:, off:512],
                                           ACTF.Exp, scale=0.125),
                              reads=[PB[b]], writes=[pres])
                            if kti * 128 >= tc:
                                E("vector", lambda e, ps=ps, off=off:
                                  e.tensor_mul(ps[:, off:off + 128],
                                               ps[:, off:off + 128],
                                               maskP[:]),
                                  reads=[pres, "maskP"], writes=[pres])

                    def av(kti, tc=tc, lastkt=lastkt):
                        par = kti % 2
                        off = max(0, kti * 128 - tc)
                        for head in range(2):
                            bav = 4 + head
                            sl = head * 2 + par
                            ps = pbuf[:, sl * 512:sl * 512 + 512]
                            pres = "pbuf%d" % sl
                            mm(PB[bav], pb[bav][0:65, off:512],
                               vbuf[:, kti * 132 + head * 66:
                                    kti * 132 + head * 66 + 65],
                               ps[:, off:512], ["vb%d" % kti, pres],
                               kti == 0, kti == lastkt, None,
                               first=(kti == 0), last=(kti == lastkt))

                    st_pair(0)
                    for kti in range(lastkt + 1):
                        if kti + 1 <= lastkt:
                            st_pair(kti + 1)
                        exp_mask(kti)
                        av(kti)

                    for head in range(2):
                        bav = 4 + head
                        bb = head
                        rr = reciprow[64:65, head * 512:head * 512 + 512]
                        rres = "recip%d" % head
                        E("scalar", lambda e, bav=bav: e.activation(
                            pb[bav][64:65, :], pb[bav][64:65, :], ACTF.Ln),
                          reads=[PB[bav]], writes=[PB[bav]])
                        E("scalar", lambda e, bav=bav, rr=rr: e.activation(
                            rr, pb[bav][64:65, :], ACTF.Exp, scale=-1.0),
                          reads=[PB[bav]], writes=[rres])
                        mm(PB[bb], pb[bb][0:64, :], onescol[64:65, 0:64], rr,
                           [rres, "onescol"], True, True, (64, 0))
                        avs = av_sb[:, head * 512:head * 512 + 512]
                        avn = "avsb%d" % head
                        E("scalar", lambda e, bav=bav, avs=avs: e.activation(
                            avs, pb[bav][0:64, :], ACTF.Copy),
                          reads=[PB[bav]], writes=[avn])
                        ac = Ro + hp * 2048 + tc
                        if head == 0:
                            E("vector", lambda e, avs=avs, bb=bb, ac=ac:
                              e.scalar_tensor_tensor(
                                R[0:64, ac:ac + 512], avs, 1.0,
                                pb[bb][0:64, :], ALU.mult, ALU.mult),
                              reads=[avn, PB[bb]], writes=blocks("R", ac, 512))
                        else:
                            E("vector", lambda e, avs=avs, bb=bb, tc=tc:
                              e.scalar_tensor_tensor(
                                bstg[:, tc:tc + 512], avs, 1.0,
                                pb[bb][0:64, :], ALU.mult, ALU.mult),
                              reads=[avn, PB[bb]], writes=["bstg"])
                if True:
                    ac0 = Ro + hp * 2048
                    E("sync", lambda e, ac0=ac0: e.dma_start(
                        out=R[64:128, ac0:ac0 + 2048], in_=bstg[:]),
                      dma="bs", reads=["bstg"],
                      writes=blocks("R", ac0, 2048))

            if DBG_STAGE == "qkv":
                continue
            if DBG_STAGE == "att":
                store_toks.append(dbg_store(
                    s, R[:, Ro:Ro + 16384].rearrange("p (c t) -> p c t",
                                                     c=8),
                    rd=blocks("R", Ro, 16384)))
                continue

            for oc in range(8):
                osl = oc % 2
                won = "wos%d" % osl
                E("gpsimd", lambda e, oc=oc, osl=osl: e.dma_start(
                    out=wos[:, osl * 1024:osl * 1024 + 1024].rearrange(
                        "p (c o) -> p c o", c=8),
                    in_=wo_d.rearrange("(c p) o -> p c o", p=128)[
                        :, :, oc * 128:oc * 128 + 128]),
                  dma="wo%d" % osl, writes=[won])
                for qc in range(QC):
                    tc = qc * 512
                    b = qc % 2
                    items = []
                    for dc in range(DC):
                        ac = Ro + dc * 2048 + tc
                        items.append((pb[b][:],
                                      wos[:, osl * 1024 + dc * 128:
                                          osl * 1024 + dc * 128 + 128],
                                      R[:, ac:ac + 512],
                                      [won] + blocks("R", ac, 512),
                                      dc == 0, dc == DC - 1, None))
                    mm_group(PB[b], items)
                    rc = Rc + oc * 2048 + tc
                    E("vector", lambda e, b=b, rc=rc: e.tensor_add(
                        R[:, rc:rc + 512], R[:, rc:rc + 512], pb[b][:]),
                      reads=[PB[b]] + blocks("R", rc, 512),
                      writes=blocks("R", rc, 512))

            if DBG_STAGE == "wo":
                store_toks.append(dbg_store(
                    s, R[:, Rc:Rc + 16384].rearrange("p (c t) -> p c t",
                                                     c=8),
                    rd=blocks("R", Rc, 16384)))
                continue

            def hloc(hc, t0):
                if hc < 16:
                    c = Ro + hc * 1024 + t0
                    return (R[:, c:c + 512], blocks("R", c, 512))
                c = 8192 + (hc - 16) * 1024 + t0
                return (xnA[:, c:c + 512], blocks("xA", c, 512))

            for st_i in range(2):
                stc = st_i * 1024
                for qc2 in range(2):
                    loc = qc2 * 512

                    def src2(dc, Rc=Rc, stc=stc, loc=loc):
                        c = Rc + dc * 2048 + stc + loc
                        return (R[:, c:c + 512], blocks("R", c, 512))

                    def dst2(dc, loc=loc):
                        c = dc * 1024 + loc
                        return (xnA[:, c:c + 512], blocks("xA", c, 512),
                                None)
                    layer_norm(qc2, DC, src2, dst2, D)

                if DBG_STAGE == "ffnln" and st_i == 0:
                    store_toks.append(dbg_store(
                        s, xnA[:, 0:8192].rearrange("p (c t) -> p c t", c=4),
                        rd=blocks("xA", 0, 8192)))
                    break
                for hc in range(HC):
                    hsl = hc % 2
                    wgn, wun = "wgs%d" % hsl, "wus%d" % hsl
                    E("gpsimd", lambda e, hc=hc, hsl=hsl: e.dma_start(
                        out=wgs[:, hsl * 1024:hsl * 1024 + 1024].rearrange(
                            "p (c o) -> p c o", c=8),
                        in_=wgt_d.rearrange("(c p) h -> p c h", p=128)[
                            :, :, hc * 128:hc * 128 + 128]),
                      dma="wg%d" % hsl, writes=[wgn])
                    E("gpsimd", lambda e, hc=hc, hsl=hsl: e.dma_start(
                        out=wus[:, hsl * 1024:hsl * 1024 + 1024].rearrange(
                            "p (c o) -> p c o", c=8),
                        in_=wut_d.rearrange("(c p) h -> p c h", p=128)[
                            :, :, hc * 128:hc * 128 + 128]),
                      dma="wu%d" % hsl, writes=[wun])
                    for tt in range(2):
                        t0 = tt * 512
                        bg, bu = tt * 2, tt * 2 + 1
                        for wname, bx in ((wgn, bg), (wun, bu)):
                            wt = wgs if wname.startswith("wg") else wus
                            items = []
                            for dc in range(DC):
                                c = dc * 1024 + t0
                                items.append((pb[bx][:],
                                              wt[:, hsl * 1024 + dc * 128:
                                                 hsl * 1024 + dc * 128 + 128],
                                              xnA[:, c:c + 512],
                                              [wname] + blocks("xA", c, 512),
                                              dc == 0, dc == DC - 1, None))
                            mm_group(PB[bx], items)
                        gs = q_sb[:, tt * 512:tt * 512 + 512]
                        gn = "qsb%d" % tt
                        E("scalar", lambda e, bg=bg, gs=gs: e.activation(
                            gs, pb[bg][:], ACTF.Silu, scale=r127[:]),
                          reads=[PB[bg], "r127"], writes=[gn])
                        ha, hres = hloc(hc, t0)
                        E("vector", lambda e, bu=bu, gs=gs, ha=ha:
                          e.scalar_tensor_tensor(ha, pb[bu][:], r127[:], gs,
                                                 ALU.mult, ALU.mult),
                          reads=[PB[bu], "r127", gn], writes=hres)
                if DBG_STAGE == "ffnh" and st_i == 0:
                    store_toks.append(dbg_store(
                        s, R[:, Ro:Ro + 16384].rearrange(
                            "p (c t) -> p c t", c=8),
                        rd=blocks("R", Ro, 16384)))
                    break
                if DBG_STAGE == "ffnh2" and st_i == 0:
                    store_toks.append(dbg_store(
                        s, xnA[:].rearrange("p (c t) -> p c t", c=8),
                        rd=blocks("xA", 0, 16384)))
                    break
                for dc in range(DC):
                    dsl = 0
                    wdn = "wds0"
                    E("gpsimd", lambda e, dc=dc, dsl=dsl: e.dma_start(
                        out=wds[:, 0:3072].rearrange(
                            "p (c o) -> p c o", c=24),
                        in_=wdt_d.rearrange("(c p) o -> p c o", p=128)[
                            :, :, dc * 128:dc * 128 + 128]),
                      dma="wd%d" % dsl, writes=[wdn])
                    for tt in range(2):
                        t0 = tt * 512
                        b = 4 + tt
                        items = []
                        for hc in range(HC):
                            ha, hres = hloc(hc, t0)
                            items.append((pb[b][:],
                                          wds[:, dsl * 3072 + hc * 128:
                                              dsl * 3072 + hc * 128 + 128],
                                          ha, [wdn] + hres,
                                          hc == 0, hc == HC - 1, None))
                        mm_group(PB[b], items)
                        rc = Rc + dc * 2048 + stc + t0
                        E("vector", lambda e, b=b, rc=rc:
                          e.scalar_tensor_tensor(
                            R[:, rc:rc + 512], pb[b][:], r127[:],
                            R[:, rc:rc + 512], ALU.mult, ALU.add),
                          reads=[PB[b], "r127"] + blocks("R", rc, 512),
                          writes=blocks("R", rc, 512))

            if DBG_STAGE not in ("ffnln", "ffnh", "ffnh2", "mdump"):
                tok = E("gpsimd", lambda e, s=s, Rc=Rc: e.dma_start(
                    out=out_d[s].rearrange("(c p) t -> p c t", p=128),
                    in_=R[:, Rc:Rc + 16384].rearrange("p (c t) -> p c t",
                                                      c=8)),
                  dma="st", reads=blocks("R", Rc, 16384))
                store_toks.append(tok)

        E("sync", lambda e: e.nop(), after=store_toks, tick=False)

        blk.tensor(lambda e: [f(e) for f in sc.q["tensor"]])
        blk.vector(lambda e: [f(e) for f in sc.q["vector"]])
        blk.scalar(lambda e: [f(e) for f in sc.q["scalar"]])
        blk.gpsimd(lambda e: [f(e) for f in sc.q["gpsimd"]])
        blk.sync(lambda e: [f(e) for f in sc.q["sync"]])

    nc._sched = sc
    import sys
    print("kernel build: instrs",
          {k: len(v) for k, v in sc.q.items()}, "waits", sc.nwait,
          file=sys.stderr)
    return nc


_NC = None


def _get_nc():
    global _NC
    if _NC is None:
        _NC = build_nc()
    return _NC


def kernel(L0_attn_x, L0_attn_W_q, L0_attn_W_dkv, L0_attn_W_uk, L0_attn_W_uv,
           L0_attn_W_o, L0_attn_ln_gamma, L0_attn_ln_beta, L0_ffn_w_gate,
           L0_ffn_w_up, L0_ffn_w_down):
    x = np.asarray(L0_attn_x, dtype=np.float32)
    xT = np.ascontiguousarray(np.transpose(x, (0, 2, 1)))
    wq = np.ascontiguousarray(np.asarray(L0_attn_W_q, np.float32))
    wdkv = np.ascontiguousarray(np.asarray(L0_attn_W_dkv, np.float32))
    wuk = np.ascontiguousarray(np.asarray(L0_attn_W_uk, np.float32))
    wuv = np.ascontiguousarray(np.asarray(L0_attn_W_uv, np.float32))
    wo = np.ascontiguousarray(np.asarray(L0_attn_W_o, np.float32))
    gam = np.ascontiguousarray(
        np.asarray(L0_attn_ln_gamma, np.float32).reshape(2, 128))
    bet = np.ascontiguousarray(
        np.asarray(L0_attn_ln_beta, np.float32).reshape(2, 128))
    wgt = np.ascontiguousarray(np.asarray(L0_ffn_w_gate, np.int32).T)
    wut = np.ascontiguousarray(np.asarray(L0_ffn_w_up, np.int32).T)
    wdt = np.ascontiguousarray(np.asarray(L0_ffn_w_down, np.int32).T)
    inv = 1.0 / (10000.0 ** (np.arange(32, dtype=np.float64) / 32.0))
    fh = inv.astype(np.float32)
    fl = (inv - fh.astype(np.float64)).astype(np.float32)
    fhl = np.ascontiguousarray(np.stack([fh, fl], axis=1))
    idf = np.eye(128, dtype=np.float32)
    p64 = np.zeros((128, 128), dtype=np.float32)
    p64[np.arange(128), np.arange(128) ^ 32] = 1.0
    cst = np.ones((128, 2118), dtype=np.float32)
    sgnv = np.ones(128, np.float32)
    sgnv[0:32] = -1.0; sgnv[32:64] = 1.0; sgnv[64:96] = -1.0; sgnv[96:128] = 1.0
    cst[:, 2112] = sgnv
    cst[:, 2113] = 127.0
    cst[:, 2114] = 1e-5
    mkp = (np.arange(128)[None, :] >= np.arange(128)[:, None]).astype(np.float32)
    post = np.broadcast_to(np.arange(2048, dtype=np.float32)[None, :],
                           (128, 2048)).copy()

    nc = _get_nc()
    in_maps = []
    for c in range(8):
        in_maps.append(dict(
            x=np.ascontiguousarray(xT[2 * c:2 * c + 2]),
            wq=wq, wdkv=wdkv, wuk=wuk, wuv=wuv, wo=wo, gam=gam, bet=bet,
            wgt=wgt, wut=wut, wdt=wdt, fhl=fhl, idf=idf, p64=p64,
            cst=cst, mkp=mkp, post=post))
    res = run_bass_kernel_spmd(nc, in_maps, core_ids=list(range(8)))
    outs = []
    for c in range(8):
        o = np.asarray(res.results[c]["out"])
        outs.append(np.transpose(o, (0, 2, 1)))
    return np.ascontiguousarray(np.concatenate(outs, axis=0))


if __name__ == "__main__":
    build_nc()
    print("build ok")


# version 13 (final)
# speedup vs baseline: 2.4541x; 1.0117x over previous
import numpy as np
from contextlib import ExitStack

import concourse.bass as bass
import concourse.mybir as mybir
from concourse.bass_utils import run_bass_kernel_spmd

F32 = mybir.dt.float32
BF16 = mybir.dt.bfloat16
I32 = mybir.dt.int32
AX = mybir.AxisListType.X
ALU = mybir.AluOpType
ACTF = mybir.ActivationFunctionType


def _sp(v):
    hi = float(np.float32(v))
    lo = float(np.float32(v - np.float64(hi)))
    return hi, lo


PI2H, PI2L = _sp(np.pi / 2)
TPH, TPL = _sp(2 * np.pi)
I2PH, I2PL = _sp(1.0 / (2 * np.pi))

D = 1024
DBG_STAGE = None
SENTINEL = False
FORCE_WAITS = False
DC = 8
LC = 2
HC = 24
QC = 4
KT = 16

ENG_SEM = {"tensor": "T", "vector": "V", "scalar": "S", "gpsimd": "G",
           "sync": "Y"}


class Sched:
    def __init__(self):
        self.q = {e: [] for e in ENG_SEM}
        self.new_sem = None
        self.cnt = {s: 0 for s in ("T", "V", "S", "G", "Y")}
        self.eclk = {e: {} for e in ENG_SEM}
        self.tokclk = {}
        self.res = {}
        self.semh = None
        self.nwait = 0

    def r_(self, name):
        if name not in self.res:
            self.res[name] = {"w": [], "r": []}
        return self.res[name]

    def emit(self, eng, fn, reads=(), writes=(), after=(), dma=False,
             tick=True, force=False):
        waits = []
        for r in reads:
            waits += self.r_(r)["w"]
        for w in writes:
            waits += self.r_(w)["w"] + self.r_(w)["r"]
        waits += list(after)
        my = ENG_SEM[eng]
        if dma:
            assert isinstance(dma, str), "dma must be a stream name"
            sem = "d_" + dma
            if sem not in self.cnt:
                self.cnt[sem] = 0
                self.semh[sem] = self.new_sem(sem)
            inc = 16
        else:
            sem = my
            inc = 1
        clk = self.eclk[eng]
        clk[my] = self.cnt[my]
        final = []
        for s, v in sorted(set(waits)):
            if s == my and not dma:
                continue
            if clk.get(s, 0) >= v and not (force or FORCE_WAITS):
                continue
            final.append((s, v))
            tc = self.tokclk.get((s, v))
            if tc:
                for s2, v2 in tc.items():
                    if clk.get(s2, 0) < v2:
                        clk[s2] = v2
            else:
                clk[s] = max(clk.get(s, 0), v)
        if not FORCE_WAITS:
            final = [(s, v) for (s, v) in final
                     if not (s != my and self.eclk[eng].get(s, -1) > v)]
        else:
            best = {}
            for s, v in final:
                best[s] = max(best.get(s, 0), v)
            final = sorted(best.items())
        self.nwait += len(final)
        tok = None
        if tick:
            self.cnt[sem] += inc
            tok = (sem, self.cnt[sem])
            clk2 = dict(clk)
            clk2[sem] = self.cnt[sem]
            self.tokclk[tok] = clk2
        self.eclk[eng] = clk

        def run(e, _w=tuple(final), _fn=fn, _tok=tok, _sem=sem, _inc=inc):
            for s, v in _w:
                e.wait_ge(self.semh[s], v)
            ins = _fn(e)
            if _tok is not None:
                ins.then_inc(self.semh[_sem], _inc)
        run._waits = tuple(final)
        run._tok = tok
        run._writes = tuple(writes)
        run._reads = tuple(reads)
        self.q[eng].append(run)

        if tok is not None:
            for w in writes:
                self.res[w] = {"w": [tok], "r": []}
            for r in reads:
                self.r_(r)["r"].append(tok)
        return tok

    def bank_deps(self, bank):
        return tuple(self.r_(bank)["w"] + self.r_(bank)["r"])


def blocks(base, col, n, gran=512):
    return ["%s%d" % (base, i) for i in range(col // gran,
                                              (col + n + gran - 1) // gran)]


def build_nc():
    nc = bass.Bass("TRN2", target_bir_lowering=False,
                   detect_race_conditions=False)
    dt = nc.dram_tensor
    x_d = dt("x", [2, D, 2048], F32, kind="ExternalInput").ap()
    wq_d = dt("wq", [D, D], F32, kind="ExternalInput").ap()
    wdkv_d = dt("wdkv", [D, 256], F32, kind="ExternalInput").ap()
    wuk_d = dt("wuk", [256, D], F32, kind="ExternalInput").ap()
    wuv_d = dt("wuv", [256, D], F32, kind="ExternalInput").ap()
    wo_d = dt("wo", [D, D], F32, kind="ExternalInput").ap()
    gam_d = dt("gam", [2, 128], F32, kind="ExternalInput").ap()
    bet_d = dt("bet", [2, 128], F32, kind="ExternalInput").ap()
    wgt_d = dt("wgt", [D, 3072], I32, kind="ExternalInput").ap()
    wut_d = dt("wut", [D, 3072], I32, kind="ExternalInput").ap()
    wdt_d = dt("wdt", [3072, D], I32, kind="ExternalInput").ap()
    fhl_d = dt("fhl", [32, 2], F32, kind="ExternalInput").ap()
    idf_d = dt("idf", [128, 128], F32, kind="ExternalInput").ap()
    p64_d = dt("p64", [128, 128], F32, kind="ExternalInput").ap()
    cst_d = dt("cst", [128, 2118], F32, kind="ExternalInput").ap()
    mkp_d = dt("mkp", [128, 128], F32, kind="ExternalInput").ap()
    pos_d = dt("post", [128, 2048], F32, kind="ExternalInput").ap()
    out_d = dt("out", [2, D, 2048], F32, kind="ExternalOutput").ap()

    sc = Sched()
    E = sc.emit

    with ExitStack() as stk:
        en = stk.enter_context
        sems = {}
        for s in ("T", "V", "S", "G", "Y"):
            sems[s] = en(nc.semaphore("sem_" + s))
        sc.semh = sems
        sc.new_sem = lambda name: en(nc.semaphore("sem_" + name))

        p64b = en(nc.sbuf_tensor("p64b", [128, 128], BF16))
        maskP = en(nc.sbuf_tensor("maskP", [128, 128], BF16))
        onesb = en(nc.sbuf_tensor("onesb", [128, 1], BF16))
        onescol = en(nc.sbuf_tensor("onescol", [65, 128], BF16))
        sgn = en(nc.sbuf_tensor("sgn", [128, 1], F32))
        sfh = en(nc.sbuf_tensor("sfh", [128, 1], F32))
        sfl = en(nc.sbuf_tensor("sfl", [128, 1], F32))
        s127 = en(nc.sbuf_tensor("s127", [128, 1], F32))
        seps = en(nc.sbuf_tensor("seps", [128, 1], F32))
        r127 = en(nc.sbuf_tensor("r127", [128, 1], F32))
        gamb = en(nc.sbuf_tensor("gamb", [128, 2], F32))
        betb = en(nc.sbuf_tensor("betb", [128, 2], F32))
        rowpool = en(nc.sbuf_tensor("rowpool", [1, 1536], BF16))
        statrow = en(nc.sbuf_tensor("statrow", [1, 1024], BF16))
        reciprow = en(nc.sbuf_tensor("reciprow", [65, 1024], BF16))
        bstg = en(nc.sbuf_tensor("bstg", [64, 2048], BF16))
        wstg = en(nc.sbuf_tensor("wstg", [128, 2048], I32))
        x2scr = en(nc.sbuf_tensor("x2scr", [128, 1024], BF16))
        q_sb = en(nc.sbuf_tensor("q_sb", [128, 1024], BF16))
        t2sb = en(nc.sbuf_tensor("t2sb", [128, 512], F32))
        t1sb = en(nc.sbuf_tensor("t1sb", [128, 1024], BF16))
        av_sb = en(nc.sbuf_tensor("av_sb", [64, 1024], BF16))
        Ctab = en(nc.sbuf_tensor("Ctab", [128, 2048], BF16))
        Stab = en(nc.sbuf_tensor("Stab", [128, 2048], BF16))
        wdkv_s = en(nc.sbuf_tensor("wdkv_s", [128, 2048], BF16))
        wuk_s = en(nc.sbuf_tensor("wuk_s", [128, 2048], BF16))
        wuv_s = en(nc.sbuf_tensor("wuv_s", [128, 2048], BF16))
        wqs = en(nc.sbuf_tensor("wqs", [128, 2048], BF16))
        wos = en(nc.sbuf_tensor("wos", [128, 2048], BF16))
        wgs = en(nc.sbuf_tensor("wgs", [128, 2048], BF16))
        wus = en(nc.sbuf_tensor("wus", [128, 2048], BF16))
        wds = en(nc.sbuf_tensor("wds", [128, 3072], BF16))
        latn = en(nc.sbuf_tensor("latn", [128, 4096], BF16))
        qt = en(nc.sbuf_tensor("qt", [128, 4096], BF16))
        ktb = en(nc.sbuf_tensor("ktb", [128, 4096], BF16))
        vbuf = en(nc.sbuf_tensor("vbuf", [128, 2112], BF16))
        pbuf = en(nc.sbuf_tensor("pbuf", [128, 2048], BF16))
        tb1 = en(nc.sbuf_tensor("tb1", [128, 512], F32))
        tbh = en(nc.sbuf_tensor("tbh", [128, 512], F32))
        tbT = en(nc.sbuf_tensor("tbT", [128, 512], F32))
        tbK = None
        R = en(nc.sbuf_tensor("R", [128, 32768], BF16))
        xnA = en(nc.sbuf_tensor("xnA", [128, 16384], BF16))

        pb = [en(nc.psum_tensor("pb%d" % i, [128, 512], F32))
              for i in range(8)]
        PB = ["pb%d" % i for i in range(8)]

        blk = en(nc.Block())

        TBRES = blocks("R", 0, 4096)

        def mm(bank, out_ap, lhsT, rhs, reads, start, stop, tp=None,
               first=True, last=True):
            return E("tensor",
                     lambda e: e.matmul(out_ap, lhsT, rhs, start=start,
                                        stop=stop, tile_position=tp,
                                        skip_group_check=True),
                     reads=reads, writes=([bank] if last else []),
                     after=sc.bank_deps(bank) if first else ())

        def mm_group(bank, items):
            n = len(items)
            for i, (o, l, r, rd, st, sp, tp) in enumerate(items):
                mm(bank, o, l, r, rd, st, sp, tp, first=(i == 0),
                   last=(i == n - 1))

        E("gpsimd", lambda e: e.dma_start(out=p64b[:], in_=p64_d), dma="p64b",
          writes=["p64b"])
        for c in range(2):
            E("sync", lambda e, c=c: e.dma_start(out=gamb[:, c:c + 1],
              in_=gam_d[c].unsqueeze(1)), dma="gamb", writes=["gamb"])
            E("sync", lambda e, c=c: e.dma_start(out=betb[:, c:c + 1],
              in_=bet_d[c].unsqueeze(1)), dma="betb", writes=["betb"])
        for b in range(4):
            E("sync", lambda e, b=b: e.dma_start(
                out=sfh[b * 32:(b + 1) * 32, 0:1], in_=fhl_d[:, 0:1],
                allow_slow_non_contiguous=True), dma="sfh", writes=["sfh"])
            E("sync", lambda e, b=b: e.dma_start(
                out=sfl[b * 32:(b + 1) * 32, 0:1], in_=fhl_d[:, 1:2],
                allow_slow_non_contiguous=True), dma="sfl", writes=["sfl"])
        E("gpsimd", lambda e: e.dma_start(
            out=wdkv_s[:].rearrange("p (c o) -> p c o", c=8),
            in_=wdkv_d.rearrange("(c p) o -> p c o", p=128)), dma="wdkv",
          writes=["wdkv_s"])
        E("gpsimd", lambda e: e.dma_start(
            out=wuk_s[:].rearrange("p (c o) -> p c o", c=2),
            in_=wuk_d.rearrange("(c p) o -> p c o", p=128)), dma="wuk",
          writes=["wuk_s"])
        E("gpsimd", lambda e: e.dma_start(
            out=wuv_s[:].rearrange("p (c o) -> p c o", c=2),
            in_=wuv_d.rearrange("(c p) o -> p c o", p=128)), dma="wuv",
          writes=["wuv_s"])

        E("gpsimd", lambda e: e.dma_start(out=onesb[:], in_=cst_d[:, 0:1], allow_slow_non_contiguous=True),
          dma="onesb", writes=["onesb"])
        E("gpsimd", lambda e: e.dma_start(out=onescol[:],
          in_=cst_d[0:65, 0:128], allow_slow_non_contiguous=True), dma="onescol", writes=["onescol"])
        E("sync", lambda e: e.dma_start(out=s127[:],
          in_=cst_d[:, 2113:2114], allow_slow_non_contiguous=True), dma="s127", writes=["s127"])
        E("sync", lambda e: e.dma_start(out=seps[:],
          in_=cst_d[:, 2114:2115], allow_slow_non_contiguous=True), dma="seps", writes=["seps"])
        E("sync", lambda e: e.dma_start(out=sgn[:],
          in_=cst_d[:, 2112:2113], allow_slow_non_contiguous=True), dma="sgn", writes=["sgn"])
        E("vector", lambda e: e.reciprocal(r127[:], s127[:]),
          reads=["s127"], writes=["r127"])
        E("gpsimd", lambda e: e.dma_start(out=vbuf[:],
          in_=cst_d[:, 0:2112], allow_slow_non_contiguous=True), dma="vbufi",
          writes=blocks("vb", 0, 2112, 132))
        E("gpsimd", lambda e: e.dma_start(out=maskP[:], in_=mkp_d),
          dma="maskP", writes=["maskP"])


        PIH = float(np.float32(np.pi))
        PIL = float(np.float32(np.pi - np.float64(PIH)))

        def tv(fn):
            E("vector", fn, reads=TBRES, writes=TBRES)

        def table_chunk(dst, dstres, c0, add_pi2, scale_ap):
            E("sync", lambda e: e.dma_start(out=tb1[:],
              in_=pos_d[:, c0:c0 + 512]), dma="post", writes=TBRES)
            E("vector", lambda e: e.tensor_scalar(tbh[:], tb1[:], sfh[:],
              None, ALU.mult), reads=TBRES + ["sfh"], writes=TBRES)
            E("vector", lambda e: e.tensor_scalar(tbT[:], tb1[:], sfl[:],
              None, ALU.mult), reads=TBRES + ["sfl"], writes=TBRES)
            tv(lambda e: e.tensor_add(tbh[:], tbh[:], tbT[:]))
            if add_pi2:
                tv(lambda e: e.tensor_single_scalar(tbh[:], tbh[:], PI2H,
                   ALU.add))
                tv(lambda e: e.tensor_single_scalar(tbh[:], tbh[:], PI2L,
                   ALU.add))
            tv(lambda e: e.tensor_single_scalar(tbT[:], tbh[:], I2PH,
               ALU.mult))
            tv(lambda e: e.tensor_single_scalar(tb1[:], tbh[:], I2PL,
               ALU.mult))
            tv(lambda e: e.tensor_add(tbT[:], tbT[:], tb1[:]))
            tv(lambda e: e.tensor_single_scalar(tbT[:], tbT[:], 0.5,
               ALU.add))
            tv(lambda e: e.tensor_copy(wstg[:, 0:512], tbT[:]))
            tv(lambda e: e.tensor_copy(tb1[:], wstg[:, 0:512]))
            tv(lambda e: e.tensor_tensor(tbT[:], tb1[:], tbT[:], ALU.is_gt))
            tv(lambda e: e.tensor_sub(tb1[:], tb1[:], tbT[:]))
            tv(lambda e: e.tensor_single_scalar(tbT[:], tb1[:], TPH,
               ALU.mult))
            tv(lambda e: e.tensor_sub(tbh[:], tbh[:], tbT[:]))
            tv(lambda e: e.tensor_single_scalar(tbT[:], tb1[:], TPL,
               ALU.mult))
            tv(lambda e: e.tensor_sub(tbh[:], tbh[:], tbT[:]))
            tv(lambda e: e.tensor_scalar_max(tbh[:], tbh[:], -3.1415925))
            tv(lambda e: e.tensor_scalar_min(tbh[:], tbh[:], 3.1415925))
            E("scalar", lambda e: e.activation(dst[:, c0:c0 + 512], tbh[:],
              ACTF.Sin, scale=scale_ap), reads=TBRES + ["sgn"],
              writes=[dstres])

        for q4 in range(4):
            table_chunk(Stab, "Stab", q4 * 512, False, sgn[:])
            table_chunk(Ctab, "Ctab", q4 * 512, True, 1.0)

        def layer_norm(qci, nch, src_fn, dst_fn, nrm):
            items = []
            for dc in range(nch):
                a, rn = src_fn(dc)
                items.append((pb[6][0:1, :], onesb[:, 0:1], a,
                              rn + ["onesb"], dc == 0, dc == nch - 1, None))
            mm_group("pb6", items)
            for dc in range(nch):
                a, rn = src_fn(dc)
                xa = x2scr[:, (dc % 2) * 512:(dc % 2) * 512 + 512]
                x2n = "x2_%d" % (dc % 2)
                E("vector", lambda e, a=a, xa=xa: e.tensor_mul(xa, a, a),
                  reads=rn, writes=[x2n])
                mm("pb7", pb[7][0:1, :], onesb[:, 0:1], xa,
                   [x2n, "onesb"], dc == 0, dc == nch - 1, None,
                   first=(dc == 0), last=(dc == nch - 1))
            rpA = rowpool[0:1, 0:512]
            rpB = rowpool[0:1, 512:1024]
            rpT = rowpool[0:1, 1024:1536]
            sl = 0
            srs = statrow[0:1, 0:512]
            srm = statrow[0:1, 512:1024]
            rs_n, rm_n = "srs%d" % sl, "srm%d" % sl
            inv_n = 1.0 / nrm
            E("scalar", lambda e: e.activation(rpA, pb[6][0:1, :], ACTF.Copy,
              scale=inv_n), reads=["pb6"], writes=["rpA"])
            E("scalar", lambda e: e.activation(rpB, pb[7][0:1, :], ACTF.Copy,
              scale=inv_n), reads=["pb7"], writes=["rpB"])
            E("vector", lambda e: e.tensor_mul(rpT, rpA, rpA),
              reads=["rpA"], writes=["rpT"])
            E("vector", lambda e: e.tensor_sub(rpT, rpB, rpT),
              reads=["rpB", "rpT"], writes=["rpT"])
            E("scalar", lambda e: e.activation(rpB, rpT, ACTF.Ln,
              bias=seps[0:1, 0:1]), reads=["rpT", "seps"], writes=["rpB"])
            E("scalar", lambda e: e.activation(srs, rpB, ACTF.Exp,
              scale=-0.5), reads=["rpB"], writes=[rs_n])
            E("vector", lambda e: e.scalar_tensor_tensor(srm, rpA, -1.0, srs,
              ALU.mult, ALU.mult), reads=["rpA", rs_n], writes=[rm_n])
            mm("pb6", pb[6][:], onescol[0:1, 0:128], srs,
               [rs_n, "onescol"], True, True, (0, 0))
            mm("pb7", pb[7][:], onescol[0:1, 0:128], srm,
               [rm_n, "onescol"], True, True, (0, 0))
            for dc in range(nch):
                a, rn = src_fn(dc)
                d, wn, post = dst_fn(dc)
                E("vector", lambda e, a=a, d=d: e.scalar_tensor_tensor(
                    d, a, 1.0, pb[6][:], ALU.mult, ALU.mult),
                  reads=rn + ["pb6"], writes=wn)
                E("vector", lambda e, d=d: e.tensor_add(d, d, pb[7][:]),
                  reads=["pb7"] + wn, writes=wn)
                if post is not None:
                    oc = post
                    E("vector", lambda e, d=d, oc=oc: e.tensor_scalar(
                        d, d, gamb[:, oc:oc + 1], betb[:, oc:oc + 1],
                        ALU.mult, ALU.add),
                      reads=wn + ["gamb", "betb"], writes=wn)

        def dbg_store(s, ap3, rd=()):
            return E("gpsimd", lambda e, s=s, ap3=ap3: e.dma_start(
                out=out_d[s].rearrange("(c p) t -> p c t", p=128)[
                    :, 0:ap3.shape[1], :],
                in_=ap3), dma="dbg",
                reads=list(rd), after=sc.barrier(), force=True)

        sc.barrier = lambda: [(k, v) for k, v in sc.cnt.items() if v > 0]

        store_toks = []
        for s in range(2):
            if DBG_STAGE == "s0only" and s == 1:
                continue
            Rc = s * 16384
            Ro = (1 - s) * 16384

            E("gpsimd", lambda e, s=s, Rc=Rc: e.dma_start(
                out=R[:, Rc:Rc + 16384].rearrange("p (c t) -> p c t", c=8),
                in_=x_d[s].rearrange("(c p) t -> p c t", p=128)),
              dma="xl", writes=blocks("R", Rc, 16384))

            for qc in range(QC):
                tc = qc * 512

                def src(dc, Rc=Rc, tc=tc):
                    c = Rc + dc * 2048 + tc
                    return (R[:, c:c + 512], blocks("R", c, 512))

                def dst(dc, tc=tc):
                    c = dc * 2048 + tc
                    return (xnA[:, c:c + 512], blocks("xA", c, 512), None)
                layer_norm(qc, DC, src, dst, D)

            if DBG_STAGE == "xn":
                store_toks.append(dbg_store(
                    s, xnA[:].rearrange("p (c t) -> p c t", c=8),
                    rd=blocks("xA", 0, 16384)))
                continue

            for qc in range(QC):
                tc = qc * 512
                for oc in range(LC):
                    items = []
                    for dc in range(DC):
                        c = dc * 2048 + tc
                        items.append((pb[0][:],
                                      wdkv_s[:, dc * 256 + oc * 128:
                                             dc * 256 + oc * 128 + 128],
                                      xnA[:, c:c + 512],
                                      ["wdkv_s"] + blocks("xA", c, 512),
                                      dc == 0, dc == DC - 1, None))
                    mm_group("pb0", items)
                    lc_ = oc * 2048 + tc
                    E("scalar", lambda e, lc_=lc_: e.activation(
                        latn[:, lc_:lc_ + 512], pb[0][:], ACTF.Copy),
                      reads=["pb0"], writes=blocks("ln", lc_, 512))

                def srcl(ocx, tc=tc):
                    c = ocx * 2048 + tc
                    return (latn[:, c:c + 512], blocks("ln", c, 512))

                def dstl(ocx, tc=tc):
                    c = ocx * 2048 + tc
                    return (latn[:, c:c + 512], blocks("ln", c, 512), ocx)
                layer_norm(qc, LC, srcl, dstl, 256)

            if DBG_STAGE == "latn":
                store_toks.append(dbg_store(
                    s, latn[:].rearrange("p (c t) -> p c t", c=2),
                    rd=blocks("ln", 0, 4096)))
                continue

            for hp in range(8):
                hs = hp % 2
                qof = hs * 2048
                wqn = "wqs%d" % hs
                E("gpsimd", lambda e, hp=hp, hs=hs: e.dma_start(
                    out=wqs[:, hs * 1024:hs * 1024 + 1024].rearrange(
                        "p (c o) -> p c o", c=8),
                    in_=wq_d.rearrange("(c p) o -> p c o", p=128)[
                        :, :, hp * 128:hp * 128 + 128]),
                  dma="wq%d" % hs, writes=[wqn])

                def rope(dst_ap, dst_res, tc, qp):
                    qsl = q_sb[:, qp * 512:qp * 512 + 512]
                    t2 = t2sb[:, 0:512]
                    t1 = t1sb[:, qp * 512:qp * 512 + 512]
                    qn, t1n, t2n = "qsb%d" % qp, "t1_%d" % qp, "t2_0"
                    E("scalar", lambda e: e.activation(qsl, pb[0][:],
                      ACTF.Copy), reads=["pb0"], writes=[qn])
                    mm("pb1", pb[1][:], p64b[:], qsl, ["p64b", qn],
                       True, True, None)
                    E("vector", lambda e: e.tensor_mul(t2, pb[1][:],
                      Stab[:, tc:tc + 512]), reads=["pb1", "Stab"],
                      writes=[t2n])
                    E("vector", lambda e: e.tensor_mul(t1, qsl,
                      Ctab[:, tc:tc + 512]), reads=[qn, "Ctab"],
                      writes=[t1n])
                    E("vector", lambda e: e.tensor_add(dst_ap, t1, t2),
                      reads=[t1n, t2n], writes=dst_res)

                for qc in range(QC):
                    tc = qc * 512
                    items = []
                    for dc in range(DC):
                        c = dc * 2048 + tc
                        items.append((pb[0][:],
                                      wqs[:, hs * 1024 + dc * 128:
                                          hs * 1024 + dc * 128 + 128],
                                      xnA[:, c:c + 512],
                                      [wqn] + blocks("xA", c, 512),
                                      dc == 0, dc == DC - 1, None))
                    mm_group("pb0", items)
                    rope(qt[:, qof + tc:qof + tc + 512],
                         blocks("qt", qof + tc, 512), tc, qc % 2)
                for qc in range(QC):
                    tc = qc * 512
                    items = []
                    for lc in range(LC):
                        c = lc * 2048 + tc
                        items.append((pb[0][:],
                                      wuk_s[:, lc * 1024 + hp * 128:
                                            lc * 1024 + hp * 128 + 128],
                                      latn[:, c:c + 512],
                                      ["wuk_s"] + blocks("ln", c, 512),
                                      lc == 0, lc == LC - 1, None))
                    mm_group("pb0", items)
                    rope(ktb[:, qof + tc:qof + tc + 512],
                         blocks("kk", qof + tc, 512), tc, qc % 2)
                for kti in range(KT):
                    items = []
                    for lc in range(LC):
                        c = lc * 2048 + kti * 128
                        items.append((pb[2][:, 0:128],
                                      latn[:, c:c + 128],
                                      wuv_s[:, lc * 1024 + hp * 128:
                                            lc * 1024 + hp * 128 + 128],
                                      ["wuv_s"] + blocks("ln", c, 128),
                                      lc == 0, lc == LC - 1, None))
                    mm_group("pb2", items)
                    E("vector", lambda e, kti=kti: e.tensor_copy(
                        vbuf[:, kti * 132:kti * 132 + 132].rearrange(
                            "p (h x) -> p h x", h=2)[:, :, 0:64],
                        pb[2][:, 0:128].rearrange("p (h x) -> p h x", h=2)),
                      reads=["pb2"], writes=["vb%d" % kti])

                if DBG_STAGE == "qkv" and hp == 0:
                    store_toks.append(dbg_store(
                        s, ktb[:, 0:2048].rearrange("p (c t) -> p c t", c=1),
                        rd=blocks("kk", 0, 2048)))
                    store_toks.append(E("gpsimd", lambda e, s=s: e.dma_start(
                        out=out_d[s].rearrange("(c p) t -> p c t", p=128)[
                            :, 1:2, :],
                        in_=qt[:, 0:2048].rearrange(
                            "p (c t) -> p c t", c=1)), dma="dbg",
                        force=True,
                        reads=blocks("qt", 0, 2048), after=sc.barrier()))
                    store_toks.append(E("gpsimd", lambda e, s=s: e.dma_start(
                        out=out_d[s].rearrange("(c p) t -> p c t", p=128)[
                            :, 2:3, 0:2048],
                        in_=vbuf[:, 0:2048].rearrange(
                            "p (c t) -> p c t", c=1)), dma="dbg",
                        force=True,
                        reads=blocks("vb", 0, 2048, 132),
                        after=sc.barrier()))
                    break


                for qc in range(QC):
                    tc = qc * 512
                    lastkt = min(KT - 1, 4 * qc + 3)

                    def st_pair(kti, tc=tc, qof=qof, hs=hs):
                        par = kti % 2
                        off = max(0, kti * 128 - tc)
                        for head in range(2):
                            b = par * 2 + head
                            r0 = head * 64
                            kcol = qof + kti * 128
                            mm(PB[b], pb[b][:, off:512],
                               ktb[r0:r0 + 64, kcol:kcol + 128],
                               qt[r0:r0 + 64, qof + tc + off:qof + tc + 512],
                               blocks("kk", kcol, 128) +
                               blocks("qt", qof + tc + off, 512 - off),
                               True, True, (r0, 0))

                    def exp_mask(kti, tc=tc):
                        par = kti % 2
                        off = max(0, kti * 128 - tc)
                        for head in range(2):
                            b = par * 2 + head
                            sl = head * 2 + par
                            ps = pbuf[:, sl * 512:sl * 512 + 512]
                            pres = "pbuf%d" % sl
                            E("scalar", lambda e, b=b, off=off, ps=ps:
                              e.activation(ps[:, off:512], pb[b][:, off:512],
                                           ACTF.Exp, scale=0.125),
                              reads=[PB[b]], writes=[pres])
                            if kti * 128 >= tc:
                                E("vector", lambda e, ps=ps, off=off:
                                  e.tensor_mul(ps[:, off:off + 128],
                                               ps[:, off:off + 128],
                                               maskP[:]),
                                  reads=[pres, "maskP"], writes=[pres])

                    def av(kti, tc=tc, lastkt=lastkt):
                        par = kti % 2
                        off = max(0, kti * 128 - tc)
                        for head in range(2):
                            bav = 4 + head
                            sl = head * 2 + par
                            ps = pbuf[:, sl * 512:sl * 512 + 512]
                            pres = "pbuf%d" % sl
                            mm(PB[bav], pb[bav][0:65, off:512],
                               vbuf[:, kti * 132 + head * 66:
                                    kti * 132 + head * 66 + 65],
                               ps[:, off:512], ["vb%d" % kti, pres],
                               kti == 0, kti == lastkt, None,
                               first=(kti == 0), last=(kti == lastkt))

                    st_pair(0)
                    for kti in range(lastkt + 1):
                        if kti + 1 <= lastkt:
                            st_pair(kti + 1)
                        exp_mask(kti)
                        av(kti)

                    for head in range(2):
                        bav = 4 + head
                        bb = head
                        rr = reciprow[64:65, head * 512:head * 512 + 512]
                        rres = "recip%d" % head
                        E("scalar", lambda e, bav=bav: e.activation(
                            pb[bav][64:65, :], pb[bav][64:65, :], ACTF.Ln),
                          reads=[PB[bav]], writes=[PB[bav]])
                        E("scalar", lambda e, bav=bav, rr=rr: e.activation(
                            rr, pb[bav][64:65, :], ACTF.Exp, scale=-1.0),
                          reads=[PB[bav]], writes=[rres])
                        mm(PB[bb], pb[bb][0:64, :], onescol[64:65, 0:64], rr,
                           [rres, "onescol"], True, True, (64, 0))
                        avs = av_sb[:, head * 512:head * 512 + 512]
                        avn = "avsb%d" % head
                        E("scalar", lambda e, bav=bav, avs=avs: e.activation(
                            avs, pb[bav][0:64, :], ACTF.Copy),
                          reads=[PB[bav]], writes=[avn])
                        ac = Ro + hp * 2048 + tc
                        if head == 0:
                            E("vector", lambda e, avs=avs, bb=bb, ac=ac:
                              e.scalar_tensor_tensor(
                                R[0:64, ac:ac + 512], avs, 1.0,
                                pb[bb][0:64, :], ALU.mult, ALU.mult),
                              reads=[avn, PB[bb]], writes=blocks("R", ac, 512))
                        else:
                            E("vector", lambda e, avs=avs, bb=bb, tc=tc:
                              e.scalar_tensor_tensor(
                                bstg[:, tc:tc + 512], avs, 1.0,
                                pb[bb][0:64, :], ALU.mult, ALU.mult),
                              reads=[avn, PB[bb]], writes=["bstg"])
                if True:
                    ac0 = Ro + hp * 2048
                    E("sync", lambda e, ac0=ac0: e.dma_start(
                        out=R[64:128, ac0:ac0 + 2048], in_=bstg[:]),
                      dma="bs", reads=["bstg"],
                      writes=blocks("R", ac0, 2048))

            if DBG_STAGE == "qkv":
                continue
            if DBG_STAGE == "att":
                store_toks.append(dbg_store(
                    s, R[:, Ro:Ro + 16384].rearrange("p (c t) -> p c t",
                                                     c=8),
                    rd=blocks("R", Ro, 16384)))
                continue

            for oc in range(8):
                osl = oc % 2
                won = "wos%d" % osl
                E("gpsimd", lambda e, oc=oc, osl=osl: e.dma_start(
                    out=wos[:, osl * 1024:osl * 1024 + 1024].rearrange(
                        "p (c o) -> p c o", c=8),
                    in_=wo_d.rearrange("(c p) o -> p c o", p=128)[
                        :, :, oc * 128:oc * 128 + 128]),
                  dma="wo%d" % osl, writes=[won])
                for qc in range(QC):
                    tc = qc * 512
                    b = qc % 2
                    items = []
                    for dc in range(DC):
                        ac = Ro + dc * 2048 + tc
                        items.append((pb[b][:],
                                      wos[:, osl * 1024 + dc * 128:
                                          osl * 1024 + dc * 128 + 128],
                                      R[:, ac:ac + 512],
                                      [won] + blocks("R", ac, 512),
                                      dc == 0, dc == DC - 1, None))
                    mm_group(PB[b], items)
                    rc = Rc + oc * 2048 + tc
                    E("vector", lambda e, b=b, rc=rc: e.tensor_add(
                        R[:, rc:rc + 512], R[:, rc:rc + 512], pb[b][:]),
                      reads=[PB[b]] + blocks("R", rc, 512),
                      writes=blocks("R", rc, 512))

            if DBG_STAGE == "wo":
                store_toks.append(dbg_store(
                    s, R[:, Rc:Rc + 16384].rearrange("p (c t) -> p c t",
                                                     c=8),
                    rd=blocks("R", Rc, 16384)))
                continue

            def hloc(hc, t0):
                if hc < 16:
                    c = Ro + hc * 1024 + t0
                    return (R[:, c:c + 512], blocks("R", c, 512))
                c = 8192 + (hc - 16) * 1024 + t0
                return (xnA[:, c:c + 512], blocks("xA", c, 512))

            for st_i in range(2):
                stc = st_i * 1024
                for qc2 in range(2):
                    loc = qc2 * 512

                    def src2(dc, Rc=Rc, stc=stc, loc=loc):
                        c = Rc + dc * 2048 + stc + loc
                        return (R[:, c:c + 512], blocks("R", c, 512))

                    def dst2(dc, loc=loc):
                        c = dc * 1024 + loc
                        return (xnA[:, c:c + 512], blocks("xA", c, 512),
                                None)
                    layer_norm(qc2, DC, src2, dst2, D)

                if DBG_STAGE == "ffnln" and st_i == 0:
                    store_toks.append(dbg_store(
                        s, xnA[:, 0:8192].rearrange("p (c t) -> p c t", c=4),
                        rd=blocks("xA", 0, 8192)))
                    break
                for hc in range(HC):
                    hsl = hc % 2
                    wgn, wun = "wgs%d" % hsl, "wus%d" % hsl
                    E("sync", lambda e, hc=hc: e.dma_start(
                        out=wstg[:, 0:1024].rearrange(
                            "p (c o) -> p c o", c=8),
                        in_=wgt_d.rearrange("(c p) h -> p c h", p=128)[
                            :, :, hc * 128:hc * 128 + 128]),
                      dma="wg", writes=["wsg"],
                      after=tuple(sc.r_("Stab")["w"] + sc.r_("Ctab")["w"]))
                    E("vector", lambda e, hsl=hsl: e.tensor_copy(
                        wgs[:, hsl * 1024:hsl * 1024 + 1024],
                        wstg[:, 0:1024]),
                      reads=["wsg"], writes=[wgn])
                    E("sync", lambda e, hc=hc: e.dma_start(
                        out=wstg[:, 1024:2048].rearrange(
                            "p (c o) -> p c o", c=8),
                        in_=wut_d.rearrange("(c p) h -> p c h", p=128)[
                            :, :, hc * 128:hc * 128 + 128]),
                      dma="wu", writes=["wsu"],
                      after=tuple(sc.r_("Stab")["w"] + sc.r_("Ctab")["w"]))
                    E("vector", lambda e, hsl=hsl: e.tensor_copy(
                        wus[:, hsl * 1024:hsl * 1024 + 1024],
                        wstg[:, 1024:2048]),
                      reads=["wsu"], writes=[wun])
                    for tt in range(2):
                        t0 = tt * 512
                        bg, bu = tt * 2, tt * 2 + 1
                        for wname, bx in ((wgn, bg), (wun, bu)):
                            wt = wgs if wname.startswith("wg") else wus
                            items = []
                            for dc in range(DC):
                                c = dc * 1024 + t0
                                items.append((pb[bx][:],
                                              wt[:, hsl * 1024 + dc * 128:
                                                 hsl * 1024 + dc * 128 + 128],
                                              xnA[:, c:c + 512],
                                              [wname] + blocks("xA", c, 512),
                                              dc == 0, dc == DC - 1, None))
                            mm_group(PB[bx], items)
                        gs = q_sb[:, tt * 512:tt * 512 + 512]
                        gn = "qsb%d" % tt
                        E("scalar", lambda e, bg=bg, gs=gs: e.activation(
                            gs, pb[bg][:], ACTF.Silu, scale=r127[:]),
                          reads=[PB[bg], "r127"], writes=[gn])
                        ha, hres = hloc(hc, t0)
                        E("vector", lambda e, bu=bu, gs=gs, ha=ha:
                          e.scalar_tensor_tensor(ha, pb[bu][:], r127[:], gs,
                                                 ALU.mult, ALU.mult),
                          reads=[PB[bu], "r127", gn], writes=hres)
                if DBG_STAGE == "ffnh" and st_i == 0:
                    store_toks.append(dbg_store(
                        s, R[:, Ro:Ro + 16384].rearrange(
                            "p (c t) -> p c t", c=8),
                        rd=blocks("R", Ro, 16384)))
                    break
                if DBG_STAGE == "ffnh2" and st_i == 0:
                    store_toks.append(dbg_store(
                        s, xnA[:].rearrange("p (c t) -> p c t", c=8),
                        rd=blocks("xA", 0, 16384)))
                    break
                for dc in range(DC):
                    for pc in range(3):
                        E("gpsimd", lambda e, dc=dc, pc=pc: e.dma_start(
                            out=wds[:, pc * 1024:pc * 1024 + 1024].rearrange(
                                "p (c o) -> p c o", c=8),
                            in_=wdt_d.rearrange("(c p) o -> p c o", p=128)[
                                :, pc * 8:pc * 8 + 8,
                                dc * 128:dc * 128 + 128]),
                          dma="wd%d" % pc, writes=["wds%d" % pc])
                    for tt in range(2):
                        t0 = tt * 512
                        b = 4 + tt
                        items = []
                        for hc in range(HC):
                            ha, hres = hloc(hc, t0)
                            items.append((pb[b][:],
                                          wds[:, hc * 128:hc * 128 + 128],
                                          ha, ["wds%d" % (hc // 8)] + hres,
                                          hc == 0, hc == HC - 1, None))
                        mm_group(PB[b], items)
                        rc = Rc + dc * 2048 + stc + t0
                        E("vector", lambda e, b=b, rc=rc:
                          e.scalar_tensor_tensor(
                            R[:, rc:rc + 512], pb[b][:], r127[:],
                            R[:, rc:rc + 512], ALU.mult, ALU.add),
                          reads=[PB[b], "r127"] + blocks("R", rc, 512),
                          writes=blocks("R", rc, 512))

            if DBG_STAGE not in ("ffnln", "ffnh", "ffnh2", "mdump"):
                tok = E("gpsimd", lambda e, s=s, Rc=Rc: e.dma_start(
                    out=out_d[s].rearrange("(c p) t -> p c t", p=128),
                    in_=R[:, Rc:Rc + 16384].rearrange("p (c t) -> p c t",
                                                      c=8)),
                  dma="st", reads=blocks("R", Rc, 16384))
                store_toks.append(tok)

        E("sync", lambda e: e.nop(), after=store_toks, tick=False)

        blk.tensor(lambda e: [f(e) for f in sc.q["tensor"]])
        blk.vector(lambda e: [f(e) for f in sc.q["vector"]])
        blk.scalar(lambda e: [f(e) for f in sc.q["scalar"]])
        blk.gpsimd(lambda e: [f(e) for f in sc.q["gpsimd"]])
        blk.sync(lambda e: [f(e) for f in sc.q["sync"]])

    nc._sched = sc
    import sys
    print("kernel build: instrs",
          {k: len(v) for k, v in sc.q.items()}, "waits", sc.nwait,
          file=sys.stderr)
    return nc


_NC = None


def _get_nc():
    global _NC
    if _NC is None:
        _NC = build_nc()
    return _NC


def kernel(L0_attn_x, L0_attn_W_q, L0_attn_W_dkv, L0_attn_W_uk, L0_attn_W_uv,
           L0_attn_W_o, L0_attn_ln_gamma, L0_attn_ln_beta, L0_ffn_w_gate,
           L0_ffn_w_up, L0_ffn_w_down):
    x = np.asarray(L0_attn_x, dtype=np.float32)
    xT = np.ascontiguousarray(np.transpose(x, (0, 2, 1)))
    wq = np.ascontiguousarray(np.asarray(L0_attn_W_q, np.float32))
    wdkv = np.ascontiguousarray(np.asarray(L0_attn_W_dkv, np.float32))
    wuk = np.ascontiguousarray(np.asarray(L0_attn_W_uk, np.float32))
    wuv = np.ascontiguousarray(np.asarray(L0_attn_W_uv, np.float32))
    wo = np.ascontiguousarray(np.asarray(L0_attn_W_o, np.float32))
    gam = np.ascontiguousarray(
        np.asarray(L0_attn_ln_gamma, np.float32).reshape(2, 128))
    bet = np.ascontiguousarray(
        np.asarray(L0_attn_ln_beta, np.float32).reshape(2, 128))
    wgt = np.ascontiguousarray(np.asarray(L0_ffn_w_gate, np.int32).T)
    wut = np.ascontiguousarray(np.asarray(L0_ffn_w_up, np.int32).T)
    wdt = np.ascontiguousarray(np.asarray(L0_ffn_w_down, np.int32).T)
    inv = 1.0 / (10000.0 ** (np.arange(32, dtype=np.float64) / 32.0))
    fh = inv.astype(np.float32)
    fl = (inv - fh.astype(np.float64)).astype(np.float32)
    fhl = np.ascontiguousarray(np.stack([fh, fl], axis=1))
    idf = np.eye(128, dtype=np.float32)
    p64 = np.zeros((128, 128), dtype=np.float32)
    p64[np.arange(128), np.arange(128) ^ 32] = 1.0
    cst = np.ones((128, 2118), dtype=np.float32)
    sgnv = np.ones(128, np.float32)
    sgnv[0:32] = -1.0; sgnv[32:64] = 1.0; sgnv[64:96] = -1.0; sgnv[96:128] = 1.0
    cst[:, 2112] = sgnv
    cst[:, 2113] = 127.0
    cst[:, 2114] = 1e-5
    mkp = (np.arange(128)[None, :] >= np.arange(128)[:, None]).astype(np.float32)
    post = np.broadcast_to(np.arange(2048, dtype=np.float32)[None, :],
                           (128, 2048)).copy()

    nc = _get_nc()
    in_maps = []
    for c in range(8):
        in_maps.append(dict(
            x=np.ascontiguousarray(xT[2 * c:2 * c + 2]),
            wq=wq, wdkv=wdkv, wuk=wuk, wuv=wuv, wo=wo, gam=gam, bet=bet,
            wgt=wgt, wut=wut, wdt=wdt, fhl=fhl, idf=idf, p64=p64,
            cst=cst, mkp=mkp, post=post))
    res = run_bass_kernel_spmd(nc, in_maps, core_ids=list(range(8)))
    outs = []
    for c in range(8):
        o = np.asarray(res.results[c]["out"])
        outs.append(np.transpose(o, (0, 2, 1)))
    return np.ascontiguousarray(np.concatenate(outs, axis=0))


if __name__ == "__main__":
    build_nc()
    print("build ok")
